# version 18
# baseline (speedup 1.0000x reference)
import numpy as np
import concourse.bass as bass
import concourse.mybir as mybir
from concourse.bass_utils import run_bass_kernel_spmd
from concourse.bass_core_rust import IndirectOffsetOnAxis
from contextlib import ExitStack

F32 = mybir.dt.float32
F16 = mybir.dt.float16
BF16 = mybir.dt.bfloat16
I32 = mybir.dt.int32
U32 = mybir.dt.uint32
AL = mybir.AluOpType
AF = mybir.ActivationFunctionType
AX = mybir.AxisListType

NCORES = 8
T = 2048
D = 1024
HID = 4096
NH = 16
HD = 64
NTC = T // 128
NDC = D // 128
NHC = HID // 128
TTN = 512
NTT = T // TTN
EPS = 1e-5
RB = 1536.0
NEG = -3.0e38

UPTO = None


class Ctr:
    def __init__(self, sem, step=1):
        self.sem, self.n, self.step = sem, 0, step

    def inc(self, instr, k=1):
        instr.then_inc(self.sem, k * self.step)
        self.n += k * self.step
        return self.n


def _build():
    nc = bass.Bass("TRN2", target_bir_lowering=False)

    d_x = nc.dram_tensor("L0_attn_x", [T, D], F32, kind="ExternalInput").ap()
    d_wq = nc.dram_tensor("L0_attn_Wq", [D, D], F32, kind="ExternalInput").ap()
    d_wk = nc.dram_tensor("L0_attn_Wk", [D, D], F32, kind="ExternalInput").ap()
    d_wv = nc.dram_tensor("L0_attn_Wv", [D, D], F32, kind="ExternalInput").ap()
    d_wo = nc.dram_tensor("L0_attn_Wo", [D, D], F32, kind="ExternalInput").ap()
    d_w1 = nc.dram_tensor("L0_ffn_w1", [HID, D], F32, kind="ExternalInput").ap()
    d_b1 = nc.dram_tensor("L0_ffn_b1", [HID], F32, kind="ExternalInput").ap()
    d_w2 = nc.dram_tensor("L0_ffn_w2", [D, HID], F32, kind="ExternalInput").ap()
    d_b2 = nc.dram_tensor("L0_ffn_b2", [D], F32, kind="ExternalInput").ap()
    c_ident = nc.dram_tensor("c_ident", [128, 128], F32, kind="ExternalInput").ap()
    c_ones1 = nc.dram_tensor("c_ones1", [1, 128], F32, kind="ExternalInput").ap()
    c_ones2b = nc.dram_tensor("c_ones2b", [2, 128], BF16, kind="ExternalInput").ap()
    c_iota4 = nc.dram_tensor("c_iota4", [128, 4], F32, kind="ExternalInput").ap()
    c_selz = nc.dram_tensor("c_selz", [128, 128], F32, kind="ExternalInput").ap()
    c_selrz = nc.dram_tensor("c_selrz", [16, 1024], F32, kind="ExternalInput").ap()
    c_hsel = nc.dram_tensor("c_hsel", [16, 2048], F32, kind="ExternalInput").ap()
    c_wsb = nc.dram_tensor("c_wsb", [128, 2], F32, kind="ExternalInput").ap()

    d_y = nc.dram_tensor("y", [T, D], F32, kind="ExternalOutput").ap()

    s_xq = nc.dram_tensor("s_xq", [T, D], BF16).ap()
    s_g = nc.dram_tensor("s_g", [HID, T], F32).ap()
    s_xln2 = nc.dram_tensor("s_xln2", [T, D], F32).ap()
    s_xpark = nc.dram_tensor("s_xpark", [T, D], F32).ap()
    s_w1q = nc.dram_tensor("s_w1q", [HID, D], BF16).ap()
    s_w2q = nc.dram_tensor("s_w2q", [D, HID], BF16).ap()
    s_wqT = nc.dram_tensor("s_wqT", [D, D], F32).ap()
    s_woT = nc.dram_tensor("s_woT", [D, D], F32).ap()
    s_wkvT = nc.dram_tensor("s_wkvT", [D, 2 * D], F32).ap()
    cc_in = nc.dram_tensor("cc_in", [1, 8], F32).ap()
    cc_out = nc.dram_tensor("cc_out", [1, 8], F32).ap()
    ag_in = nc.dram_tensor("ag_in", [1032, 128], F32).ap()
    ag_out = nc.dram_tensor("ag_out", [8 * 1032, 128], F32).ap()
    RG = [list(range(NCORES))]

    es = ExitStack()
    sbuf = lambda n, s, d=F32: es.enter_context(nc.sbuf_tensor(n, s, d))
    sem = lambda n: es.enter_context(nc.semaphore(n))

    ident = sbuf("ident", [128, 128])
    ones1 = sbuf("ones1", [1, 128])
    ones2b = sbuf("ones2b", [2, 128], BF16)
    iota4 = sbuf("iota4", [128, 4])
    selz = sbuf("selz", [128, 128])
    wsb = sbuf("wsb", [128, 2])
    Ut = sbuf("Ut", [128, 128])
    b1t = sbuf("b1t", [128, NHC])
    b1sc = sbuf("b1sc", [128, NHC])
    b2s = sbuf("b2s", [2, D], BF16)
    scal = sbuf("scal", [128, 16])
    one1 = sbuf("one1", [1, 16])
    cct = sbuf("cct", [1, 8])
    ccr = sbuf("ccr", [1, 8])
    stat = sbuf("stat", [128, 16 * 8])
    junk = es.enter_context(nc.sbuf_tensor("junkt", [128, 1024], F32, side="right"))
    zcol = sbuf("zcol", [128, 1])
    epst = sbuf("epst", [128, 1])
    rbt = sbuf("rbt", [128, 1])
    attsc = sbuf("attsc", [128, 1])
    kselT = sbuf("kselTt", [128, 8 * 64])
    vselt = sbuf("vselt", [128, 8 * 64])
    gmax = sbuf("gmaxt", [128, 1])
    gtmp = sbuf("gtmpt", [128, 1])
    lred = sbuf("lredt", [1, 128])

    PH = sem("PH")
    t_d, v_d, s_d, g_d = sem("t_d"), sem("v_d"), sem("s_d"), sem("g_d")
    q_sy, q_gp = sem("q_sy"), sem("q_gp")
    p_mm, p_ev, p_rd = sem("p_mm"), sem("p_ev"), sem("p_rd")
    p_ev2 = sem("p_ev2")
    p_q, p_q2 = sem("p_q"), sem("p_q2")
    st4 = [sem(f"st4_{i}") for i in range(4)]
    ld2 = [sem(f"ld2_{i}") for i in range(2)]
    ig2 = [sem(f"ig2_{i}") for i in range(2)]
    ldt2 = [sem(f"ldt2_{i}") for i in range(2)]
    ldu2 = [sem(f"ldu2_{i}") for i in range(2)]
    sts2 = [sem(f"sts2_{i}") for i in range(2)]

    ps_g = es.enter_context(nc.psum_tensor("ps_g", [128, 128], F32))

    block = es.enter_context(nc.Block())

    tD, vD, sD, gD = Ctr(t_d), Ctr(v_d), Ctr(s_d), Ctr(g_d)
    qSY, qGP = Ctr(q_sy, 16), Ctr(q_gp, 16)
    pMM, pEV, pRD = Ctr(p_mm), Ctr(p_ev), Ctr(p_rd)
    pEV2 = Ctr(p_ev2)
    ST4 = [Ctr(x, 16) for x in st4]
    LD2 = [Ctr(x, 16) for x in ld2]
    IG2 = [Ctr(x, 16) for x in ig2]
    LDT2 = [Ctr(x, 16) for x in ldt2]
    LDU2 = [Ctr(x, 16) for x in ldu2]
    STS2 = [Ctr(x, 16) for x in sts2]
    pQ, pQ2 = Ctr(p_q), Ctr(p_q2)

    phase = [0]

    def barrier():
        phase[0] += 1
        tgt = 5 * phase[0]

        @block.tensor
        def _(t):
            t.wait_ge(t_d, tD.n)
            t.wait_ge(p_mm, pMM.n)
            t.sem_inc(PH, 1); t.wait_ge(PH, tgt)

        @block.vector
        def _(v):
            v.wait_ge(v_d, vD.n)
            v.wait_ge(p_q2, pQ2.n)
            v.wait_ge(p_rd, pRD.n)
            v.wait_ge(p_ev2, pEV2.n)
            v.sem_inc(PH, 1); v.wait_ge(PH, tgt)

        @block.scalar
        def _(s):
            s.wait_ge(s_d, sD.n)
            s.wait_ge(p_q, pQ.n)
            s.wait_ge(p_ev, pEV.n)
            s.sem_inc(PH, 1); s.wait_ge(PH, tgt)

        @block.sync
        def _(sy):
            sy.wait_ge(q_sy, qSY.n)
            sy.sem_inc(PH, 1); sy.wait_ge(PH, tgt)

        @block.gpsimd
        def _(g):
            g.wait_ge(q_gp, qGP.n)
            g.wait_ge(g_d, gD.n)
            for k in range(4):
                g.wait_ge(st4[k], ST4[k].n)
            for k in range(2):
                g.wait_ge(ld2[k], LD2[k].n)
                g.wait_ge(ig2[k], IG2[k].n)
                g.wait_ge(ldt2[k], LDT2[k].n)
                g.wait_ge(ldu2[k], LDU2[k].n)
                g.wait_ge(sts2[k], STS2[k].n)
            g.sem_inc(PH, 1); g.wait_ge(PH, tgt)

    def bcast_cols(src_c0, n, dst_c0):
        @block.tensor
        def _(t):
            t.wait_ge(v_d, vD.n)
            tD.inc(t.matmul(ps_g[:, 0:n], ones1[:], one1[0:1, src_c0:src_c0 + n],
                            start=True, stop=True))

        @block.vector
        def _(v):
            v.wait_ge(t_d, tD.n)
            vD.inc(v.tensor_copy(scal[:, dst_c0:dst_c0 + n], ps_g[:, 0:n]))

    def allreduce_max(src_col, dst_col):
        @block.vector
        def _(v):
            vD.inc(v.memset(cct[:], 0.0))
            v.wait_ge(v_d, vD.n)
            vD.inc(v.tensor_copy(cct[0:1, 0:1], one1[0:1, src_col:src_col + 1]))

        @block.gpsimd
        def _(g):
            g.wait_ge(v_d, vD.n)
            qGP.inc(g.dma_start(out=cc_in, in_=cct[:]))
            g.wait_ge(q_gp, qGP.n)
            gD.inc(g.collective_compute("AllReduce", AL.max, replica_groups=RG,
                                        ins=[cc_in.opt()], outs=[cc_out.opt()]))
            g.wait_ge(g_d, gD.n)
            qGP.inc(g.dma_start(out=ccr[:], in_=cc_out))

        @block.vector
        def _(v):
            v.wait_ge(q_gp, qGP.n)
            vD.inc(v.tensor_copy(one1[0:1, dst_col:dst_col + 1], ccr[0:1, 0:1]))

    def cross_part_max(dst_one1_col):
        @block.tensor
        def _(t):
            t.wait_ge(v_d, vD.n)
            tD.inc(t.transpose(ps_g[0:1, 0:128], gmax[:], ident[:]))

        @block.vector
        def _(v):
            v.wait_ge(t_d, tD.n)
            vD.inc(v.tensor_copy(lred[:], ps_g[0:1, 0:128]))
            v.wait_ge(v_d, vD.n)
            vD.inc(v.reduce_max(one1[0:1, dst_one1_col:dst_one1_col + 1], lred[:], AX.X))

    es1 = ExitStack()
    sb_x = es1.enter_context(nc.sbuf_tensor("sb_x", [128, NTC * D], F32))

    @block.sync
    def _(sy):
        for src, dst in ((c_ident, ident), (c_ones1, ones1), (c_ones2b, ones2b),
                         (c_iota4, iota4), (c_selz, selz), (c_wsb, wsb)):
            qSY.inc(sy.dma_start(out=dst[:], in_=src))
        for tc in range(NTC):
            qSY.inc(sy.dma_start(out=sb_x[:, tc * D:(tc + 1) * D],
                                 in_=d_x[tc * 128:(tc + 1) * 128, :]))
        qSY.inc(sy.dma_start(out=b1t[:], in_=d_b1.rearrange("(a b) -> b a", b=128),
                             allow_slow_non_contiguous=True))

    @block.vector
    def _(v):
        vD.inc(v.memset(zcol[:], 0.0))
        vD.inc(v.memset(epst[:], EPS))
        vD.inc(v.memset(rbt[:], RB))
        vD.inc(v.memset(attsc[:], 0.125))
        vD.inc(v.memset(junk[:], 0.0))

    barrier()

    es0a = ExitStack()
    wbuf = [es0a.enter_context(nc.sbuf_tensor(f"wbuf{i}", [128, 4096], F32)) for i in range(2)]
    wmx8 = es0a.enter_context(nc.sbuf_tensor("wmx8", [128, 16], F32))
    wmx = es0a.enter_context(nc.sbuf_tensor("wmxq", [128, 2], F32))
    pt0 = es0a.enter_context(nc.psum_tensor("pt0", [128, 128], F32))
    pt1 = es0a.enter_context(nc.psum_tensor("pt1", [128, 128], F32))

    es0b = ExitStack()
    wbufB = [es0b.enter_context(nc.sbuf_tensor(f"wbufB{i}", [128, 4096], F32)) for i in range(2)]
    wzb = [es0b.enter_context(nc.sbuf_tensor(f"wzbq{i}", [128, 4096], F16)) for i in range(2)]
    wqb = [es0b.enter_context(nc.sbuf_tensor(f"wqbq{i}", [128, 4096], BF16)) for i in range(2)]
    WB4 = wbuf + wbufB
    LD4 = LD2 + LDU2
    ld4s = ld2 + ldu2

    def unit1(u, q=False):
        w1, w2 = (s_w1q, s_w2q) if q else (d_w1, d_w2)
        if u < 8:
            src = w1[u * 512:(u + 1) * 512, :]
        else:
            j = u - 8
            src = w2[(j % 2) * 512:(j % 2) * 512 + 512, (j // 2) * 1024:(j // 2) * 1024 + 1024]
        return src.rearrange("(r p) d -> p r d", p=128)

    esP1 = ExitStack()
    b2row = esP1.enter_context(nc.sbuf_tensor("b2row", [1, D], F32, side="right"))
    b2hi32 = esP1.enter_context(nc.sbuf_tensor("b2hi32", [1, D], F32, side="right"))
    b2lob = esP1.enter_context(nc.sbuf_tensor("b2lob", [1, D], BF16, side="right"))

    @block.sync
    def _(sy):
        qSY.inc(sy.dma_start(out=b2row[:], in_=d_b2[None, :]))

    def su_absmax():
        b_rd = pRD.n
        b_ld = [LD4[i].n for i in range(4)]

        @block.sync
        def _(sy):
            for u in range(16):
                if u >= 4:
                    sy.wait_ge(p_rd, b_rd + u - 3)
                LD4[u % 4].inc(sy.dma_start(out=WB4[u % 4].rearrange("p (r d) -> p r d", r=4),
                                            in_=unit1(u)))

        @block.vector
        def _(v):
            for u in range(16):
                v.wait_ge(ld4s[u % 4], b_ld[u % 4] + 16 * (u // 4 + 1))
                pRD.inc(v.reduce_max(wmx8[:, u:u + 1], WB4[u % 4][:], AX.X,
                                     apply_absolute_value=True))

    def su_scales():
        @block.vector
        def _(v):
            vD.inc(v.reduce_max(wmx[:, 0:1], wmx8[:, 0:8], AX.X))
            vD.inc(v.reduce_max(wmx[:, 1:2], wmx8[:, 8:16], AX.X))

        @block.tensor
        def _(t):
            t.wait_ge(v_d, vD.n)
            tD.inc(t.transpose(pt0[0:1, 0:128], wmx[:, 0:1], ident[:]))
            tD.inc(t.transpose(pt1[0:1, 0:128], wmx[:, 1:2], ident[:]))

        @block.vector
        def _(v):
            v.wait_ge(t_d, tD.n)
            vD.inc(v.tensor_copy(lred[:], pt0[0:1, 0:128]))
            v.wait_ge(v_d, vD.n)
            vD.inc(v.reduce_max(one1[0:1, 0:1], lred[:], AX.X))
            v.wait_ge(v_d, vD.n)
            vD.inc(v.tensor_copy(lred[:], pt1[0:1, 0:128]))
            v.wait_ge(v_d, vD.n)
            vD.inc(v.reduce_max(one1[0:1, 1:2], lred[:], AX.X))
            v.wait_ge(v_d, vD.n)
            vD.inc(v.tensor_scalar(one1[0:1, 4:5], one1[0:1, 0:1], 1e-6, 1.0 / 127.0, AL.max, AL.mult))
            vD.inc(v.tensor_scalar(one1[0:1, 5:6], one1[0:1, 1:2], 1e-6, 1.0 / 127.0, AL.max, AL.mult))
            v.wait_ge(v_d, vD.n)
            vD.inc(v.reciprocal(one1[0:1, 6:7], one1[0:1, 4:5]))
            vD.inc(v.reciprocal(one1[0:1, 7:8], one1[0:1, 5:6]))

        bcast_cols(6, 2, 0)

    def su_wquant(units, bufs, LDs, lds, zb, qb, rr, scalar_store=False, stg=None):
        nb = len(bufs)
        nu = len(units)
        b_q, b_q2 = pQ.n, pQ2.n
        b_ld = [LDs[i].n for i in range(nb)]
        shape = (lambda a: a.rearrange("p (r d) -> p r d", r=rr)) if rr else (lambda a: a[:])
        STs, sts = stg if stg else ((STS2, sts2) if scalar_store else (ST4, st4))
        tgt_st = [STs[k % 2].n + 16 * (k // 2 + 1) for k in range(nu)]
        b_st = [STs[0].n, STs[1].n]
        b_st4 = [ST4[0].n, ST4[1].n]

        @block.sync
        def _(sy):
            for k in range(nu):
                if k >= nb:
                    sy.wait_ge(p_q, b_q + k - nb + 1)
                LDs[k % nb].inc(sy.dma_start(out=shape(bufs[k % nb]), in_=units[k][0]))

        @block.scalar
        def _(s):
            for k in range(nu):
                s.wait_ge(lds[k % nb], b_ld[k % nb] + 16 * (k // nb + 1))
                s.wait_ge(p_q2, b_q2 + k - 1 if k >= 2 else b_q2)
                pQ.inc(s.activation(zb[k % 2][:], bufs[k % nb][:], AF.Identity,
                                    bias=rbt[:, 0:1], scale=scal[:, units[k][2]:units[k][2] + 1]))
                if scalar_store and k >= 1:
                    s.wait_ge(p_q2, b_q2 + k)
                    STs[(k - 1) % 2].inc(s.dma_start(out=units[k - 1][1], in_=shape(qb[(k - 1) % 2])))
            if scalar_store:
                s.wait_ge(p_q2, b_q2 + nu)
                STs[(nu - 1) % 2].inc(s.dma_start(out=units[nu - 1][1], in_=shape(qb[(nu - 1) % 2])))

        @block.vector
        def _(v):
            for k in range(nu):
                v.wait_ge(p_q, b_q + k + 1)
                v.wait_ge(sts[k % 2], tgt_st[k - 2] if k >= 2 else b_st[k % 2])
                if k < 2 and scalar_store:
                    v.wait_ge(st4[k % 2], b_st4[k % 2])
                pQ2.inc(v.tensor_scalar(qb[k % 2][:], zb[k % 2][:], -RB, None, AL.add))

        if not scalar_store:
            @block.gpsimd
            def _(g):
                for k in range(nu):
                    g.wait_ge(p_q2, b_q2 + k + 1)
                    STs[k % 2].inc(g.dma_start(out=units[k][1], in_=shape(qb[k % 2])))

    def su_wquant1():
        su_wquant([(unit1(u), unit1(u, True), 0) for u in range(8)],
                  WB4, LD4, ld4s, wzb, wqb, 4)

    def close_es0b():
        es0b.close()
        es0a.close()
        esP1.close()

    es0c = ExitStack()

    def qtr2(u, qq, q=False):
        w2 = s_w2q if q else d_w2
        j = u - 8
        rr = (j % 2) * 512 + qq * 128
        cb = (j // 2) * 1024
        return w2[rr:rr + 128, cb:cb + 1024]

    def su_wquant2():
        wbuf2 = [es0c.enter_context(nc.sbuf_tensor(f"wbc{i}", [128, 1024], F32)) for i in range(3)]
        wzb2 = [es0c.enter_context(nc.sbuf_tensor(f"wzc{i}", [128, 1024], F16)) for i in range(2)]
        wqb2 = [es0c.enter_context(nc.sbuf_tensor(f"wqc{i}", [128, 1024], BF16)) for i in range(2)]
        units = [(qtr2(u, qq), qtr2(u, qq, True), 1) for u in range(8, 16) for qq in range(4)]
        su_wquant(units, wbuf2, LD4[0:3], ld4s[0:3], wzb2, wqb2, 0,
                  stg=(STS2, sts2))

    def close_es0():
        es0c.close()

    def mk_trans():
        es_tr = ExitStack()
        wrow = [es_tr.enter_context(nc.sbuf_tensor(f"wrow{i}", [128, 1024], F32)) for i in range(2)]
        tev = [es_tr.enter_context(nc.sbuf_tensor(f"tev{i}", [128, 1024], F32)) for i in range(2)]
        pt2 = [es_tr.enter_context(nc.psum_tensor(f"pt2{i}", [128, 128], F32)) for i in range(2)]
        pU = es_tr.enter_context(nc.psum_tensor("pUq", [128, 4], F32))
        jobs = [(w, ec) for w in (2, 3) for ec in range(8)]
        wsrc = {0: d_wq, 1: d_wo, 2: d_wk, 3: d_wv}
        EV0, RD0, TD0 = pEV.n, pRD.n, tD.n
        tgt_ld = [LDT2[j % 2].n + 16 * (j // 2 + 1) for j in range(16)]
        acc = [ST4[2].n, ST4[3].n]
        st_after = {}
        for j, (w, ec) in enumerate(jobs):
            acc[j % 2] += 16 * (1 if w < 2 else 2)
            st_after[j] = acc[j % 2]
        td_tp, td_umm = {}, {}
        cnt = TD0
        for j, (w, ec) in enumerate(jobs):
            for dc in range(8):
                cnt += 1
                td_tp[(j, dc)] = cnt
            if w == 2:
                for dc in range(8):
                    cnt += 1
                    td_umm[(j, dc)] = cnt

        def emit_chunk(tt, b_ev2=None):
            js = list(range(tt * 4, tt * 4 + 4))

            @block.sync
            def _(sy):
                for j in js:
                    if j >= 2:
                        sy.wait_ge(p_ev, EV0 + 8 * (j - 1))
                        if jobs[j - 2][0] == 2:
                            sy.wait_ge(t_d, td_umm[(j - 2, 7)])
                    LDT2[j % 2].inc(sy.dma_start(
                        out=wrow[j % 2][:],
                        in_=wsrc[jobs[j][0]][jobs[j][1] * 128:(jobs[j][1] + 1) * 128, :]))

            @block.tensor
            def _(t):
                for j in js:
                    w, ec = jobs[j]
                    t.wait_ge(ldt2[j % 2], tgt_ld[j])
                    for dc in range(8):
                        gi = j * 8 + dc
                        if gi >= 2:
                            t.wait_ge(p_ev, EV0 + gi - 1)
                        tD.inc(t.transpose(pt2[gi % 2][:, 0:128],
                                           wrow[j % 2][:, dc * 128:(dc + 1) * 128], ident[:]))
                    if w == 2:
                        for dc in range(8):
                            ku = j * 8 + dc
                            if ku >= 2:
                                t.wait_ge(p_rd, RD0 + ku - 1)
                            tD.inc(t.matmul(pU[:, (ku % 2) * 2:(ku % 2) * 2 + 2],
                                            wrow[j % 2][:, dc * 128:(dc + 1) * 128],
                                            wsb[:], start=True, stop=True))

            @block.scalar
            def _(s):
                for j in js:
                    if j >= 2:
                        s.wait_ge(st4[2 + (j % 2)], st_after[j - 2])
                    for dc in range(8):
                        gi = j * 8 + dc
                        s.wait_ge(t_d, td_tp[(j, dc)])
                        pEV.inc(s.copy(tev[j % 2][:, dc * 128:(dc + 1) * 128], pt2[gi % 2][:, 0:128]))

            @block.vector
            def _(v):
                for j in js:
                    w, ec = jobs[j]
                    if w != 2:
                        continue
                    for dc in range(8):
                        ku = j * 8 + dc
                        v.wait_ge(t_d, td_umm[(j, dc)])
                        pRD.inc(v.tensor_copy(Ut[:, dc * 16 + 2 * ec: dc * 16 + 2 * ec + 2],
                                              pU[:, (ku % 2) * 2:(ku % 2) * 2 + 2]))

            @block.gpsimd
            def _(g):
                for j in js:
                    w, ec = jobs[j]
                    g.wait_ge(p_ev, EV0 + 8 * (j + 1))
                    src3 = tev[j % 2].rearrange("p (dc c) -> p dc c", dc=8)
                    SS = ST4[2 + (j % 2)]
                    if w == 0:
                        SS.inc(g.dma_start(
                            out=s_wqT[:, ec * 128:(ec + 1) * 128].rearrange("(dc p) c -> p dc c", p=128),
                            in_=src3))
                    elif w == 1:
                        SS.inc(g.dma_start(
                            out=s_woT[:, ec * 128:(ec + 1) * 128].rearrange("(dc p) c -> p dc c", p=128),
                            in_=src3))
                    else:
                        off = 0 if w == 2 else 64
                        srcA = tev[j % 2].rearrange("p (dc h c) -> p dc h c", dc=8, h=2)
                        SS.inc(g.dma_start(
                            out=s_wkvT[:, (2 * ec) * 128 + off:(2 * ec) * 128 + off + 64]
                                .rearrange("(dc p) c -> p dc c", p=128),
                            in_=srcA[:, :, 0, :]))
                        SS.inc(g.dma_start(
                            out=s_wkvT[:, (2 * ec + 1) * 128 + off:(2 * ec + 1) * 128 + off + 64]
                                .rearrange("(dc p) c -> p dc c", p=128),
                            in_=srcA[:, :, 1, :]))

        return emit_chunk, es_tr.close

    def su_b2split():

        @block.vector
        def _(v):
            vD.inc(v.tensor_copy(b2s[0:1, :], b2row[:]))
            v.wait_ge(v_d, vD.n)
            vD.inc(v.tensor_copy(b2hi32[:], b2s[0:1, :]))
            v.wait_ge(v_d, vD.n)
            vD.inc(v.tensor_tensor(b2lob[:], b2row[:], b2hi32[:], AL.subtract))

        @block.gpsimd
        def _(g):
            g.wait_ge(v_d, vD.n)
            qGP.inc(g.dma_start(out=b2s[1:2, :], in_=b2lob[:]))

    def ln_stats(xbuf, with_amax, co_stats=(), co_math=(), skip_stats=False):
        if not skip_stats:
            @block.scalar
            def _(s):
                for tc in range(NTC):
                    xs = xbuf[:, tc * D:(tc + 1) * D]
                    s.wait_ge(p_q, pQ.n)
                    pQ.inc(s.activation(junk[:], xs, AF.Identity, bias=zcol[:, 0:1], scale=1.0,
                                        accum_out=stat[:, 0 + tc:0 + tc + 1]))
                    s.wait_ge(p_q, pQ.n)
                    pQ.inc(s.activation(junk[:], xs, AF.Square, bias=zcol[:, 0:1], scale=1.0,
                                        accum_out=stat[:, 16 + tc:16 + tc + 1]))

        if with_amax:
            @block.vector
            def _(v):
                for tc in range(NTC):
                    xs = xbuf[:, tc * D:(tc + 1) * D]
                    vD.inc(v.reduce_max(stat[:, 96 + tc:96 + tc + 1], xs, AX.X))
                    vD.inc(v.tensor_reduce(stat[:, 112 + tc:112 + tc + 1], xs, AX.X, AL.min))

        for f in co_stats:
            f()
        if (not skip_stats) or co_stats:
            barrier()

        @block.vector
        def _(v):
            MN, VAR, T4 = stat[:, 32:48], stat[:, 16:32], stat[:, 64:80]
            vD.inc(v.tensor_scalar(MN, stat[:, 0:16], 1.0 / D, None, AL.mult))
            vD.inc(v.tensor_scalar(VAR, VAR, 1.0 / D, None, AL.mult))
            v.wait_ge(v_d, vD.n)
            vD.inc(v.tensor_tensor(T4, MN, MN, AL.mult))
            v.wait_ge(v_d, vD.n)
            vD.inc(v.tensor_tensor(VAR, VAR, T4, AL.subtract))

        @block.scalar
        def _(s):
            s.wait_ge(v_d, vD.n)
            pQ.inc(s.activation(stat[:, 64:80], stat[:, 16:32], AF.Sqrt,
                                bias=epst[:, 0:1], scale=1.0))

        @block.vector
        def _(v):
            v.wait_ge(p_q, pQ.n)
            vD.inc(v.reciprocal(stat[:, 48:64], stat[:, 64:80]))

        if with_amax:
            @block.vector
            def _(v):
                MN, RS, T6, T7 = stat[:, 32:48], stat[:, 48:64], stat[:, 96:112], stat[:, 112:128]
                v.wait_ge(v_d, vD.n)
                vD.inc(v.tensor_tensor(T6, T6, MN, AL.subtract))
                vD.inc(v.tensor_tensor(T7, MN, T7, AL.subtract))
                v.wait_ge(v_d, vD.n)
                vD.inc(v.tensor_tensor(T6, T6, T7, AL.max))
                v.wait_ge(v_d, vD.n)
                vD.inc(v.tensor_tensor(T6, T6, RS, AL.mult))
                v.wait_ge(v_d, vD.n)
                vD.inc(v.reduce_max(gmax[:], T6, AX.X))

        for f in co_math:
            f()
        barrier()

    def ffn(tag, xbuf, co_stats=(), co_math=(), co_ar=(), post_ar=None,
            co_qx=(), post_qx=None, co_passA=(), post_passA=None,
            co_sg=(), post_sg=None, mk_tt=None):
        ln_stats(xbuf, with_amax=True, co_stats=co_stats, co_math=co_math)
        cross_part_max(8)
        barrier()
        allreduce_max(8, 9)
        for f in co_ar:
            f()
        barrier()
        if post_ar:
            post_ar()

        @block.vector
        def _(v):
            vD.inc(v.tensor_scalar(one1[0:1, 12:13], one1[0:1, 9:10], 1e-6, 1.0 / 127.0,
                                   AL.max, AL.mult))
            v.wait_ge(v_d, vD.n)
            vD.inc(v.reciprocal(one1[0:1, 13:14], one1[0:1, 12:13]))
            vD.inc(v.tensor_tensor(one1[0:1, 14:15], one1[0:1, 12:13], one1[0:1, 4:5], AL.mult))

        bcast_cols(13, 2, 2)
        barrier()

        @block.vector
        def _(v):
            A, B = stat[:, 64:80], stat[:, 80:96]
            vD.inc(v.tensor_scalar(A, stat[:, 48:64], scal[:, 2:3], None, AL.mult))
            v.wait_ge(v_d, vD.n)
            vD.inc(v.tensor_tensor(B, stat[:, 32:48], A, AL.mult))
            v.wait_ge(v_d, vD.n)
            vD.inc(v.tensor_scalar(B, B, -1.0, RB, AL.mult, AL.add))
            vD.inc(v.tensor_scalar(b1sc[:], b1t[:], scal[:, 3:4], None, AL.mult))

        barrier()

        esA = ExitStack()
        xqT = esA.enter_context(nc.sbuf_tensor(f"{tag}xqT", [128, NDC * T], BF16))
        w1qT = esA.enter_context(nc.sbuf_tensor(f"{tag}w1qT", [128, NDC * HID], BF16))
        gb = [esA.enter_context(nc.sbuf_tensor(f"{tag}gb{i}", [128, TTN], F32)) for i in range(4)]
        pA = [esA.enter_context(nc.psum_tensor(f"{tag}pA{i}", [128, TTN], F32)) for i in range(2)]
        W1T = (ST4[0].n, ST4[1].n)
        esq = ExitStack()
        zq = [esq.enter_context(nc.sbuf_tensor(f"{tag}zq{i}", [128, 1024], F16, side="right")) for i in range(2)]
        xqb = [esq.enter_context(nc.sbuf_tensor(f"{tag}xqb{i}", [128, 1024], BF16, side="right")) for i in range(2)]
        b_q, b_q2 = pQ.n, pQ2.n
        tgt_st = [ST4[tc % 2].n + 16 * (tc // 2 + 1) for tc in range(NTC)]

        @block.scalar
        def _(s):
            for tc in range(NTC):
                if tc >= 2:
                    s.wait_ge(p_q2, b_q2 + tc - 1)
                pQ.inc(s.activation(zq[tc % 2][:], xbuf[:, tc * D:(tc + 1) * D], AF.Identity,
                                    bias=stat[:, 80 + tc:80 + tc + 1],
                                    scale=stat[:, 64 + tc:64 + tc + 1]))

        @block.vector
        def _(v):
            for tc in range(NTC):
                v.wait_ge(p_q, b_q + tc + 1)
                if tc >= 2:
                    v.wait_ge(st4[tc % 2], tgt_st[tc - 2])
                pQ2.inc(v.tensor_scalar(xqb[tc % 2][:], zq[tc % 2][:], -RB, None, AL.add))

        @block.gpsimd
        def _(g):
            for tc in range(NTC):
                g.wait_ge(p_q2, b_q2 + tc + 1)
                ST4[tc % 2].inc(g.dma_start(out=s_xq[tc * 128:(tc + 1) * 128, :], in_=xqb[tc % 2][:]))

        @block.sync
        def _(sy):
            sy.wait_ge(st4[0], W1T[0])
            sy.wait_ge(st4[1], W1T[1])
            for dc in range(NDC):
                qSY.inc(sy.dma_start(out=w1qT[:, dc * HID:(dc + 1) * HID],
                                     in_=s_w1q[:, dc * 128:(dc + 1) * 128], transpose=True))
            sy.wait_ge(st4[0], ST4[0].n)
            sy.wait_ge(st4[1], ST4[1].n)
            for dc in range(NDC):
                qSY.inc(sy.dma_start(out=xqT[:, dc * T:(dc + 1) * T],
                                     in_=s_xq[:, dc * 128:(dc + 1) * 128], transpose=True))

        for f in co_qx:
            f(zq, xqb)
        barrier()
        if post_qx:
            post_qx()
        esq.close()


        @block.vector
        def _(v):
            vD.inc(v.memset(gmax[:], 0.0))
        b_vd = vD.n

        b_mm, b_ev, b_rd = pMM.n, pEV.n, pRD.n
        ntile = NTT * NHC
        tgt_st = [ST4[i % 4].n + 16 * (i // 4 + 1) for i in range(ntile)]

        @block.tensor
        def _(t):
            for i in range(ntile):
                tt, fc = divmod(i, NHC)
                if i >= 2:
                    t.wait_ge(p_ev, b_ev + i - 1)
                for dc in range(NDC):
                    mm = t.matmul(pA[i % 2][:],
                                  w1qT[:, dc * HID + fc * 128: dc * HID + (fc + 1) * 128],
                                  xqT[:, dc * T + tt * TTN: dc * T + (tt + 1) * TTN],
                                  start=(dc == 0), stop=(dc == NDC - 1))
                pMM.inc(mm)

        @block.scalar
        def _(s):
            for i in range(ntile):
                tt, fc = divmod(i, NHC)
                s.wait_ge(p_mm, b_mm + i + 1)
                if i >= 4:
                    s.wait_ge(p_rd, b_rd + (i - 3))
                    s.wait_ge(st4[i % 4], tgt_st[i - 4])
                pEV.inc(s.activation(gb[i % 4][:], pA[i % 2][:], AF.Gelu,
                                     bias=b1sc[:, fc:fc + 1], scale=scal[:, 3:4]))

        @block.vector
        def _(v):
            for i in range(ntile):
                v.wait_ge(p_ev, b_ev + i + 1)
                if i == 0:
                    v.wait_ge(v_d, b_vd)
                else:
                    v.wait_ge(p_rd, b_rd + i)
                vD.inc(v.reduce_max(gtmp[:], gb[i % 4][:], AX.X, apply_absolute_value=True))
                v.wait_ge(v_d, vD.n)
                pRD.inc(v.tensor_tensor(gmax[:], gmax[:], gtmp[:], AL.max))

        @block.gpsimd
        def _(g):
            for i in range(ntile):
                tt, fc = divmod(i, NHC)
                g.wait_ge(p_ev, b_ev + i + 1)
                ST4[i % 4].inc(g.dma_start(out=s_g[fc * 128:(fc + 1) * 128, tt * TTN:(tt + 1) * TTN],
                                           in_=gb[i % 4][:]))

        w2st = None
        if co_passA:
            w2st = (STS2[0].n, STS2[1].n)
        for f in co_passA:
            f()
        barrier()
        if post_passA:
            post_passA()
        esA.close()

        es_w2 = ExitStack()
        w2qT = es_w2.enter_context(nc.sbuf_tensor(f"{tag}w2qT", [128, NHC * D], BF16))
        cross_part_max(10)
        barrier()
        allreduce_max(10, 11)
        for f in co_sg:
            f()

        @block.sync
        def _(sy):
            for hc in range(NHC):
                if w2st and hc % 8 == 0:
                    cb = hc // 8
                    sy.wait_ge(sts2[0], w2st[0] + 16 * 4 * (cb + 1))
                    sy.wait_ge(sts2[1], w2st[1] + 16 * 4 * (cb + 1))
                qSY.inc(sy.dma_start(out=w2qT[:, hc * D:(hc + 1) * D],
                                     in_=s_w2q[:, hc * 128:(hc + 1) * 128], transpose=True))

        barrier()
        if post_sg:
            post_sg()

        @block.vector
        def _(v):
            vD.inc(v.tensor_scalar(one1[0:1, 12:13], one1[0:1, 11:12], 1e-6, 1.0 / 127.0,
                                   AL.max, AL.mult))
            v.wait_ge(v_d, vD.n)
            vD.inc(v.reciprocal(one1[0:1, 13:14], one1[0:1, 12:13]))
            vD.inc(v.tensor_tensor(one1[0:1, 14:15], one1[0:1, 12:13], one1[0:1, 5:6], AL.mult))
            v.wait_ge(v_d, vD.n)
            vD.inc(v.tensor_scalar(one1[0:1, 14:15], one1[0:1, 14:15], 0.5, None, AL.mult))

        bcast_cols(13, 2, 4)
        barrier()

        esB = ExitStack()
        gq = esB.enter_context(nc.sbuf_tensor(f"{tag}gq", [128, NHC * TTN], BF16))
        gl = [esB.enter_context(nc.sbuf_tensor(f"{tag}gl{i}", [128, TTN], F32)) for i in range(2)]
        zg = [esB.enter_context(nc.sbuf_tensor(f"{tag}zg{i}", [128, TTN], F16)) for i in range(2)]
        pB = [esB.enter_context(nc.psum_tensor(f"{tag}pB{i}", [128, TTN], F32)) for i in range(4)]

        tt_emit, tt_close = mk_tt() if mk_tt else (None, None)
        ev_hist = {}
        for tt in range(NTT):
            b_q, b_q2, b_mm, b_ev = pQ.n, pQ2.n, pMM.n, pEV2.n
            tgt_ld = [LD2[hc % 2].n + 16 * (hc // 2 + 1) for hc in range(NHC)]

            @block.sync
            def _(sy):
                for hc in range(NHC):
                    sy.wait_ge(p_q, max(0, b_q + hc - 1))
                    LD2[hc % 2].inc(sy.dma_start(out=gl[hc % 2][:],
                                                 in_=s_g[hc * 128:(hc + 1) * 128, tt * TTN:(tt + 1) * TTN]))

            @block.scalar
            def _(s):
                for hc in range(NHC):
                    s.wait_ge(ld2[hc % 2], tgt_ld[hc])
                    s.wait_ge(p_q2, max(0, b_q2 + hc - 1))
                    pQ.inc(s.activation(zg[hc % 2][:], gl[hc % 2][:], AF.Identity,
                                        bias=rbt[:, 0:1], scale=scal[:, 4:5]))

            @block.vector
            def _(v):
                for hc in range(NHC):
                    v.wait_ge(p_q, b_q + hc + 1)
                    v.wait_ge(p_mm, max(0, b_mm - (NHC + 4) + hc + 1))
                    pQ2.inc(v.tensor_scalar(gq[:, hc * TTN:(hc + 1) * TTN], zg[hc % 2][:],
                                            -RB, None, AL.add))

            @block.tensor
            def _(t):
                for ft in range(2):
                    for hc in range(NHC):
                        t.wait_ge(p_q2, b_q2 + hc + 1)
                        if hc == 0:
                            t.wait_ge(p_ev2, max(0, b_ev + 4 * ft))
                        for tc4 in range(4):
                            mm = t.matmul(pB[tc4][:],
                                          gq[:, hc * TTN + tc4 * 128: hc * TTN + (tc4 + 1) * 128],
                                          w2qT[:, hc * D + ft * TTN: hc * D + (ft + 1) * TTN],
                                          start=(hc == 0), stop=False)
                        pMM.inc(mm)
                    for tc4 in range(4):
                        pMM.inc(t.matmul(pB[tc4][:], ones2b[:],
                                         b2s[0:2, ft * TTN:(ft + 1) * TTN], start=False, stop=True))

            @block.vector
            def _(v):
                for ft in range(2):
                    for tc4 in range(4):
                        v.wait_ge(p_mm, b_mm + (NHC + 4) * (ft + 1))
                        tcg = tt * 4 + tc4
                        xs = xbuf[:, tcg * D + ft * TTN: tcg * D + (ft + 1) * TTN]
                        pEV2.inc(v.scalar_tensor_tensor(xs, pB[tc4][:], scal[:, 5:6],
                                                        xs, AL.mult, AL.add))

            if tt_emit and tt >= 1:
                tt_emit(tt - 1, ev_hist[tt - 1])
            if tt_emit:
                ev_hist[tt] = b_ev

        if tt_emit:
            tt_emit(NTT - 1, ev_hist[NTT - 1])
        barrier()
        if tt_close:
            tt_close()
        esB.close()
        es_w2.close()

    def finalize(xbuf):
        @block.gpsimd
        def _(g):
            for tc in range(NTC):
                qGP.inc(g.dma_start(out=d_y[tc * 128:(tc + 1) * 128, :],
                                    in_=xbuf[:, tc * D:(tc + 1) * D]))
        barrier()

    def ln2_stats_tt(tt, b_ev2):
        @block.scalar
        def _(s):
            for tc4 in range(4):
                tcg = tt * 4 + tc4
                xs = sb_x[:, tcg * D:(tcg + 1) * D]
                s.wait_ge(p_ev2, b_ev2 + 4 + tc4 + 1)
                s.wait_ge(p_q, pQ.n)
                pQ.inc(s.activation(junk[:], xs, AF.Identity, bias=zcol[:, 0:1], scale=1.0,
                                    accum_out=stat[:, tcg:tcg + 1]))
                s.wait_ge(p_q, pQ.n)
                pQ.inc(s.activation(junk[:], xs, AF.Square, bias=zcol[:, 0:1], scale=1.0,
                                    accum_out=stat[:, 16 + tcg:16 + tcg + 1]))

    def mk_trans_stats():
        emit_chunk, close = mk_trans()

        def emit(tt, b_ev2):
            emit_chunk(tt, b_ev2)
            ln2_stats_tt(tt, b_ev2)
        return emit, close

    ffn("f1", sb_x, co_stats=[su_absmax], co_math=[su_scales, su_b2split],
        co_ar=[su_wquant1], post_ar=close_es0b,
        co_passA=[su_wquant2], post_passA=close_es0, mk_tt=mk_trans_stats)

    if UPTO == "ffn1":
        finalize(sb_x)
        es1.close(); es.close()
        return nc

    ln_stats(sb_x, with_amax=False, skip_stats=True)

    es_xlt = ExitStack()
    xlt = es_xlt.enter_context(nc.sbuf_tensor("xlt", [128, NDC * T], F32, side="right"))
    es2 = ExitStack()
    xl2 = [es2.enter_context(nc.sbuf_tensor(f"xl2_{i}", [128, 1024], F32)) for i in range(2)]
    ptA = es2.enter_context(nc.psum_tensor("ptA", [128, 128], F32))
    ptB = es2.enter_context(nc.psum_tensor("ptB", [128, 128], F32))

    b_q, b_mm, b_ev = pQ.n, pMM.n, pEV.n
    tgt_st = [ST4[tc % 2].n + 16 * (tc // 2 + 1) for tc in range(NTC)]

    @block.vector
    def _(v):
        for tc in range(NTC):
            if tc >= 2:
                v.wait_ge(p_mm, b_mm + 8 * (tc - 1))
                v.wait_ge(st4[tc % 2], tgt_st[tc - 2])
            pQ.inc(v.tensor_scalar(xl2[tc % 2][:], sb_x[:, tc * D:(tc + 1) * D],
                                   stat[:, 32 + tc:32 + tc + 1], stat[:, 48 + tc:48 + tc + 1],
                                   AL.subtract, AL.mult))

    @block.gpsimd
    def _(g):
        for tc in range(NTC):
            g.wait_ge(p_q, b_q + tc + 1)
            ST4[tc % 2].inc(g.dma_start(out=s_xln2[tc * 128:(tc + 1) * 128, :], in_=xl2[tc % 2][:]))

    @block.tensor
    def _(t):
        for tc in range(NTC):
            t.wait_ge(p_q, b_q + tc + 1)
            for dc in range(NDC):
                i = tc * 8 + dc
                if i >= 2:
                    t.wait_ge(p_ev, b_ev + i - 1)
                pp = ptA if i % 2 == 0 else ptB
                pMM.inc(t.transpose(pp[:, 0:128], xl2[tc % 2][:, dc * 128:(dc + 1) * 128], ident[:]))

    @block.scalar
    def _(s):
        for tc in range(NTC):
            for dc in range(NDC):
                i = tc * 8 + dc
                s.wait_ge(p_mm, b_mm + i + 1)
                pp = ptA if i % 2 == 0 else ptB
                pEV.inc(s.copy(xlt[:, dc * T + tc * 128: dc * T + (tc + 1) * 128], pp[:, 0:128]))

    wrow2 = [es2.enter_context(nc.sbuf_tensor(f"wrw2_{i}", [128, 1024], F32)) for i in range(2)]
    tev2 = [es2.enter_context(nc.sbuf_tensor(f"tv2_{i}", [128, 1024], F32)) for i in range(2)]
    pt3 = [es2.enter_context(nc.psum_tensor(f"pt3{i}", [128, 128], F32)) for i in range(2)]
    jobsQ = [(w, ec) for w in (0, 1) for ec in range(8)]
    EV0q = pEV.n
    TD0q = tD.n
    tgt_ldq = [LDT2[j % 2].n + 16 * (j // 2 + 1) for j in range(16)]
    accq = [ST4[2].n, ST4[3].n]
    st_afterq = {}
    for j in range(16):
        accq[j % 2] += 16
        st_afterq[j] = accq[j % 2]

    @block.sync
    def _(sy):
        for j, (w, ec) in enumerate(jobsQ):
            if j >= 2:
                sy.wait_ge(p_ev, EV0q + 8 * (j - 1))
            LDT2[j % 2].inc(sy.dma_start(
                out=wrow2[j % 2][:],
                in_=(d_wq if w == 0 else d_wo)[ec * 128:(ec + 1) * 128, :]))

    @block.tensor
    def _(t):
        for j, (w, ec) in enumerate(jobsQ):
            t.wait_ge(ldt2[j % 2], tgt_ldq[j])
            for dc in range(8):
                gi = j * 8 + dc
                if gi >= 2:
                    t.wait_ge(p_ev, EV0q + gi - 1)
                tD.inc(t.transpose(pt3[gi % 2][:, 0:128],
                                   wrow2[j % 2][:, dc * 128:(dc + 1) * 128], ident[:]))

    @block.scalar
    def _(s):
        for j, (w, ec) in enumerate(jobsQ):
            if j >= 2:
                s.wait_ge(st4[2 + (j % 2)], st_afterq[j - 2])
            for dc in range(8):
                gi = j * 8 + dc
                s.wait_ge(t_d, TD0q + gi + 1)
                pEV.inc(s.copy(tev2[j % 2][:, dc * 128:(dc + 1) * 128], pt3[gi % 2][:, 0:128]))

    @block.gpsimd
    def _(g):
        for j, (w, ec) in enumerate(jobsQ):
            g.wait_ge(p_ev, EV0q + 8 * (j + 1))
            src3 = tev2[j % 2].rearrange("p (dc c) -> p dc c", dc=8)
            dst = (s_wqT if w == 0 else s_woT)[:, ec * 128:(ec + 1) * 128]
            ST4[2 + (j % 2)].inc(g.dma_start(
                out=dst.rearrange("(dc p) c -> p dc c", p=128), in_=src3))

    barrier()
    es2.close()

    @block.gpsimd
    def _(g):
        for tc in range(NTC):
            qGP.inc(g.dma_start(out=s_xpark[tc * 128:(tc + 1) * 128, :],
                                in_=sb_x[:, tc * D:(tc + 1) * D]))

    barrier()
    es1.close()

    es_q = ExitStack()
    qT = es_q.enter_context(nc.sbuf_tensor("qTt", [128, NDC * T], F32))
    es6 = ExitStack()
    wqe = [es6.enter_context(nc.sbuf_tensor(f"wqe{i}", [128, 1024], F32)) for i in range(2)]
    pQP = [es6.enter_context(nc.psum_tensor(f"pQP{i}", [128, TTN], F32)) for i in range(2)]
    qk = [0]
    qtd = {}
    q_ld = {}
    BQ = pQ.n

    def qproj_emit(n):
        ks = list(range(qk[0], min(8 * NTT, qk[0] + n)))
        qk[0] += len(ks)
        if not ks:
            return
        ecs = [ec for ec in sorted({k // NTT for k in ks}) if ec not in q_ld]

        @block.sync
        def _(sy):
            for ec in ecs:
                if ec >= 2:
                    sy.wait_ge(p_q, BQ + 4 * (ec - 1))
                LDT2[ec % 2].inc(sy.dma_start(
                    out=wqe[ec % 2].rearrange("p (dc c) -> p dc c", dc=8),
                    in_=s_wqT[:, ec * 128:(ec + 1) * 128].rearrange("(dc p) c -> p dc c", p=128)))
                q_ld[ec] = LDT2[ec % 2].n

        @block.tensor
        def _(t):
            for k in ks:
                ec, tt = divmod(k, NTT)
                if tt == 0 or k == ks[0]:
                    t.wait_ge(ldt2[ec % 2], q_ld[ec])
                if k >= 2:
                    t.wait_ge(p_q, BQ + k - 1)
                for dc in range(NDC):
                    mm = t.matmul(pQP[k % 2][:],
                                  wqe[ec % 2][:, dc * 128:(dc + 1) * 128],
                                  xlt[:, dc * T + tt * TTN: dc * T + (tt + 1) * TTN],
                                  start=(dc == 0), stop=(dc == NDC - 1))
                tD.inc(mm)
                qtd[k] = tD.n

        @block.scalar
        def _(s):
            for k in ks:
                ec, tt = divmod(k, NTT)
                s.wait_ge(t_d, qtd[k])
                pQ.inc(s.copy(qT[:, ec * T + tt * TTN: ec * T + (tt + 1) * TTN], pQP[k % 2][:]))

    es3 = ExitStack()
    mval = es3.enter_context(nc.sbuf_tensor("mval", [16, 64], F32, side="right"))
    midx = es3.enter_context(nc.sbuf_tensor("midx", [16, 64], U32, side="right"))
    midf = es3.enter_context(nc.sbuf_tensor("midf", [16, 64], F32, side="right"))
    idxT = es3.enter_context(nc.sbuf_tensor("idxT", [64, 16], F32, side="right"))
    idxTi = es3.enter_context(nc.sbuf_tensor("idxTi", [64, 16], I32, side="right"))
    psc = es3.enter_context(nc.psum_tensor("psc", [16, TTN], F32))
    pstx = es3.enter_context(nc.psum_tensor("pstx", [64, 16], F32))
    es3a = ExitStack()
    sc0 = es3a.enter_context(nc.sbuf_tensor("sc0", [16, T], F32, side="right"))
    sc1 = es3a.enter_context(nc.sbuf_tensor("sc1", [16, T], F32, side="right"))

    b_mm, b_ev = pMM.n, pEV2.n

    @block.tensor
    def _(t):
        for tt in range(NTT):
            if tt >= 1:
                t.wait_ge(p_ev2, b_ev + tt)
            for dc in range(NDC):
                mm = t.matmul(psc[:], Ut[:, dc * 16:(dc + 1) * 16],
                              xlt[:, dc * T + tt * TTN: dc * T + (tt + 1) * TTN],
                              start=(dc == 0), stop=(dc == NDC - 1))
            pMM.inc(mm)

    @block.vector
    def _(v):
        for tt in range(NTT):
            v.wait_ge(p_mm, b_mm + tt + 1)
            pEV2.inc(v.tensor_copy(sc0[:, tt * TTN:(tt + 1) * TTN], psc[:]))

    barrier()

    @block.vector
    def _(v):
        cur, nxt = sc0, sc1
        for r in range(8):
            v.wait_ge(v_d, vD.n)
            vD.inc(v.max(out=mval[:, r * 8:(r + 1) * 8], in_=cur[:]))
            v.wait_ge(v_d, vD.n)
            vD.inc(v.max_index(midx[:, r * 8:(r + 1) * 8], mval[:, r * 8:(r + 1) * 8], cur[:]))
            v.wait_ge(v_d, vD.n)
            vD.inc(v.match_replace(nxt[:], in_to_replace=mval[:, r * 8:(r + 1) * 8],
                                   in_values=cur[:], imm_value=NEG))
            cur, nxt = nxt, cur
        v.wait_ge(v_d, vD.n)
        vD.inc(v.tensor_copy(midf[:], midx[:]))

    qproj_emit(13)
    barrier()
    es3a.close()

    @block.tensor
    def _(t):
        tD.inc(t.transpose(pstx[:], midf[:], ident[0:16, 0:16]))

    @block.vector
    def _(v):
        v.wait_ge(t_d, tD.n)
        vD.inc(v.tensor_copy(idxT[:], pstx[:]))
        v.wait_ge(v_d, vD.n)
        vD.inc(v.tensor_copy(idxTi[:], idxT[:]))

    barrier()

    es4 = ExitStack()
    xcd = [es4.enter_context(nc.sbuf_tensor(f"xcd{i}", [64, 1024], F32, side="right")) for i in range(2)]
    xct = es4.enter_context(nc.sbuf_tensor("xct", [128, NDC * 1024], F32, side="right"))
    es4p = ExitStack()
    ptA2 = es4p.enter_context(nc.psum_tensor("ptA2", [128, 128], F32))
    ptB2 = es4p.enter_context(nc.psum_tensor("ptB2", [128, 128], F32))

    qproj_emit(6)
    b_mm, b_ev = pMM.n, pEV.n
    tgt_ld = [IG2[h % 2].n + 16 * (h // 2 + 1) for h in range(NH)]

    @block.gpsimd
    def _(g):
        for h in range(NH):
            if h >= 2:
                g.wait_ge(p_mm, b_mm + 8 * (h - 1))
            IG2[h % 2].inc(g.indirect_dma_start(
                out=xcd[h % 2][:], out_offset=None,
                in_=s_xln2[:, :],
                in_offset=IndirectOffsetOnAxis(idxTi[0:64, h:h + 1], axis=0)))

    @block.tensor
    def _(t):
        for h in range(NH):
            t.wait_ge(ig2[h % 2], tgt_ld[h])
            for dc in range(NDC):
                i = h * 8 + dc
                if i >= 2:
                    t.wait_ge(p_ev, b_ev + i - 1)
                pp = ptA2 if i % 2 == 0 else ptB2
                pMM.inc(t.transpose(pp[:, 0:64], xcd[h % 2][:, dc * 128:(dc + 1) * 128],
                                    ident[0:64, 0:64]))

    @block.scalar
    def _(s):
        for h in range(NH):
            for dc in range(NDC):
                i = h * 8 + dc
                s.wait_ge(p_mm, b_mm + i + 1)
                pp = ptA2 if i % 2 == 0 else ptB2
                pEV.inc(s.copy(xct[:, dc * 1024 + h * 64: dc * 1024 + (h + 1) * 64], pp[:, 0:64]))

    barrier()
    es4p.close()

    es5 = ExitStack()
    wkve = [es5.enter_context(nc.sbuf_tensor(f"wkve{i}", [128, 1024], F32, side="right")) for i in range(2)]
    pay = es5.enter_context(nc.sbuf_tensor("pays", [64, 128], F32, side="right"))
    pKV = [es5.enter_context(nc.psum_tensor(f"pKV{i}", [64, 128], F32)) for i in range(2)]

    b_mm, b_ev = pMM.n, pEV.n
    b_st0 = ST4[0].n
    tgt_kv = [LDU2[h % 2].n + 16 * (h // 2 + 1) for h in range(NH)]

    @block.sync
    def _(sy):
        for h in range(NH):
            if h >= 2:
                sy.wait_ge(p_mm, b_mm + h - 1)
            LDU2[h % 2].inc(sy.dma_start(
                out=wkve[h % 2].rearrange("p (dc c) -> p dc c", dc=8),
                in_=s_wkvT[:, h * 128:(h + 1) * 128].rearrange("(dc p) c -> p dc c", p=128)))

    @block.tensor
    def _(t):
        for h in range(NH):
            t.wait_ge(ldu2[h % 2], tgt_kv[h])
            if h >= 2:
                t.wait_ge(p_ev, b_ev + h - 1)
            for dc in range(NDC):
                mm = t.matmul(pKV[h % 2][:],
                              xct[:, dc * 1024 + h * 64: dc * 1024 + (h + 1) * 64],
                              wkve[h % 2][:, dc * 128:(dc + 1) * 128],
                              start=(dc == 0), stop=(dc == NDC - 1))
            pMM.inc(mm)

    @block.scalar
    def _(s):
        for h in range(NH):
            s.wait_ge(p_mm, b_mm + h + 1)
            if h >= 1:
                s.wait_ge(st4[0], b_st0 + 16 * h)
            pEV.inc(s.copy(pay[:], pKV[h % 2][:]))

    qproj_emit(6)

    @block.gpsimd
    def _(g):
        for h in range(NH):
            g.wait_ge(p_ev, b_ev + h + 1)
            ST4[0].inc(g.dma_start(out=ag_in[h * 64:(h + 1) * 64, :], in_=pay[:]))
        g.wait_ge(v_d, vD.n)
        qGP.inc(g.dma_start(out=ag_in[1024:1032, :].rearrange("a (b c) -> (a b) c", b=2),
                            in_=mval[:]))

    barrier()
    es5.close()
    es4.close()
    es3.close()

    es7 = ExitStack()
    hsel = es7.enter_context(nc.sbuf_tensor("hsel", [16, 2048], F32, side="right"))
    va0 = es7.enter_context(nc.sbuf_tensor("va0", [16, 512], F32, side="right"))
    va1 = es7.enter_context(nc.sbuf_tensor("va1", [16, 512], F32, side="right"))
    gval = es7.enter_context(nc.sbuf_tensor("gval", [16, 64], F32, side="right"))
    gposu = es7.enter_context(nc.sbuf_tensor("gposu", [16, 64], U32, side="right"))
    gposf = es7.enter_context(nc.sbuf_tensor("gposf", [16, 64], F32, side="right"))

    @block.gpsimd
    def _(g):
        gD.inc(g.collective_compute("AllGather", AL.bypass, replica_groups=RG,
                                    ins=[ag_in.opt()], outs=[ag_out.opt()]))

    qproj_emit(32)

    @block.sync
    def _(sy):
        sy.wait_ge(g_d, gD.n)
        qSY.inc(sy.dma_start(out=hsel[:], in_=c_hsel))
        for c in range(8):
            qSY.inc(sy.dma_start(
                out=va0[:, c * 64:(c + 1) * 64],
                in_=ag_out[c * 1032 + 1024: c * 1032 + 1032, :].rearrange("a (b c) -> (a b) c", b=2)))

    @block.vector
    def _(v):
        v.wait_ge(q_sy, qSY.n)
        cur, nxt = va0, va1
        for r in range(8):
            v.wait_ge(v_d, vD.n)
            vD.inc(v.max(out=gval[:, r * 8:(r + 1) * 8], in_=cur[:]))
            v.wait_ge(v_d, vD.n)
            vD.inc(v.max_index(gposu[:, r * 8:(r + 1) * 8], gval[:, r * 8:(r + 1) * 8], cur[:]))
            v.wait_ge(v_d, vD.n)
            vD.inc(v.match_replace(nxt[:], in_to_replace=gval[:, r * 8:(r + 1) * 8],
                                   in_values=cur[:], imm_value=NEG))
            cur, nxt = nxt, cur
        v.wait_ge(v_d, vD.n)
        vD.inc(v.tensor_copy(gposf[:], gposu[:]))

    barrier()
    es6.close()

    es7p = ExitStack()
    kva = xlt[:, 0:8192]
    e2 = xlt[:, 8192:12288]
    ppos = es7p.enter_context(nc.psum_tensor("ppos", [128, 1024], F32))
    pSELb = es7p.enter_context(nc.psum_tensor("pSELb", [128, 128], F32))
    pKTb = es7p.enter_context(nc.psum_tensor("pKTb", [128, 128], F32))
    pSEL = [pSELb[:, 0:64], pSELb[:, 64:128]]
    pKT = [pKTb[:, 0:64], pKTb[:, 64:128]]

    b_mm, b_e2 = pMM.n, pEV2.n

    @block.tensor
    def _(t):
        t.wait_ge(v_d, vD.n)
        for h in range(NH):
            mm = t.matmul(ppos[:, h * 64:(h + 1) * 64], hsel[:, h * 128:(h + 1) * 128],
                          gposf[:], start=True, stop=True, skip_group_check=True)
        pMM.inc(mm)

    @block.vector
    def _(v):
        v.wait_ge(p_mm, b_mm + 1)
        for h in range(NH):
            for q4 in range(4):
                vD.inc(v.tensor_scalar(e2[:, (h * 4 + q4) * 64:(h * 4 + q4 + 1) * 64],
                                       ppos[:, h * 64:(h + 1) * 64],
                                       iota4[:, q4:q4 + 1], None, AL.is_equal))

    @block.sync
    def _(sy):
        for c in range(8):
            dst = kva[(c % 2) * 64:(c % 2) * 64 + 64, :].rearrange(
                "p (h q k) -> p h q k", h=16, q=4)[:, :, c // 2, :]
            qSY.inc(sy.dma_start(
                out=dst,
                in_=ag_out[c * 1032: c * 1032 + 1024, :].rearrange("(h r) k -> r h k", h=16)))

    b_mm2 = pMM.n

    @block.tensor
    def _(t):
        t.wait_ge(v_d, vD.n)
        t.wait_ge(q_sy, qSY.n)
        for m in range(NH):
            hp, sub = divmod(m, 2)
            h = m
            pr = slice(64 * sub, 64 * sub + 64)
            if m >= 2:
                t.wait_ge(p_ev2, b_e2 + 2 * (m - 1))
            for q4 in range(4):
                t.matmul(pSEL[m % 2][pr, :], e2[:, (h * 4 + q4) * 64:(h * 4 + q4 + 1) * 64],
                         kva[:, (h * 4 + q4) * 128 + 64:(h * 4 + q4) * 128 + 128],
                         start=(q4 == 0), stop=(q4 == 3), skip_group_check=True)
                mm = t.matmul(pKT[m % 2][pr, :], kva[:, (h * 4 + q4) * 128:(h * 4 + q4) * 128 + 64],
                              e2[:, (h * 4 + q4) * 64:(h * 4 + q4 + 1) * 64],
                              start=(q4 == 0), stop=(q4 == 3), skip_group_check=True)
            pMM.inc(mm)

    @block.vector
    def _(v):
        for m in range(NH):
            hp, sub = divmod(m, 2)
            pr = slice(64 * sub, 64 * sub + 64)
            v.wait_ge(p_mm, b_mm2 + m + 1)
            pEV2.inc(v.tensor_copy(vselt[pr, hp * 64:(hp + 1) * 64], pSEL[m % 2][pr, :]))
            pEV2.inc(v.tensor_copy(kselT[pr, hp * 64:(hp + 1) * 64], pKT[m % 2][pr, :]))

    barrier()
    es7p.close()
    es7.close()
    es3.close()
    es_xlt.close()

    es_x2 = ExitStack()
    sb_x2 = es_x2.enter_context(nc.sbuf_tensor("sb_x2", [128, NTC * D], F32, side="right"))
    es8 = ExitStack()
    woTs = es8.enter_context(nc.sbuf_tensor("woTs", [128, NDC * 1024], F32))
    selrz = es8.enter_context(nc.sbuf_tensor("selrz", [16, 1024], F32))
    oT = [es8.enter_context(nc.sbuf_tensor(f"oTt{i}", [128, TTN], F32)) for i in range(8)]
    pb_ = [es8.enter_context(nc.sbuf_tensor(f"pbt{i}", [128, TTN], F32)) for i in range(2)]
    zt_ = es8.enter_context(nc.sbuf_tensor("ztt", [16, TTN], F32))
    rzt = es8.enter_context(nc.sbuf_tensor("rztt", [16, TTN], F32))
    pS = [es8.enter_context(nc.psum_tensor(f"pSs{i}", [128, TTN], F32)) for i in range(2)]
    pO = [es8.enter_context(nc.psum_tensor(f"pOo{i}", [128, TTN], F32)) for i in range(2)]
    pZ = es8.enter_context(nc.psum_tensor("pZz", [16, TTN], F32))
    pY = [es8.enter_context(nc.psum_tensor(f"pYy{i}", [128, TTN], F32)) for i in range(2)]

    @block.sync
    def _(sy):
        qSY.inc(sy.dma_start(out=selrz[:], in_=c_selrz))
        for dc in range(NDC):
            qSY.inc(sy.dma_start(out=woTs[:, dc * 1024:(dc + 1) * 1024],
                                 in_=s_woT[dc * 128:(dc + 1) * 128, :]))

    @block.gpsimd
    def _(g):
        for tc in range(NTC):
            qGP.inc(g.dma_start(out=sb_x2[:, tc * D:(tc + 1) * D],
                                in_=s_xpark[tc * 128:(tc + 1) * 128, :]))

    barrier()

    for tt in range(NTT):
        b_mm, b_ev, b_q, b_q2, b_rd, b_td = pMM.n, pEV.n, pQ.n, pQ2.n, pRD.n, tD.n

        @block.tensor
        def _(t):
            for hp in range(8):
                t.wait_ge(p_q, max(0, b_q + hp - 1))
                t.matmul(pS[hp % 2][0:64, :], kselT[0:64, hp * 64:(hp + 1) * 64],
                         qT[0:64, hp * T + tt * TTN: hp * T + (tt + 1) * TTN],
                         start=True, stop=True, skip_group_check=True)
                pMM.inc(t.matmul(pS[hp % 2][64:128, :], kselT[64:128, hp * 64:(hp + 1) * 64],
                                 qT[64:128, hp * T + tt * TTN: hp * T + (tt + 1) * TTN],
                                 start=True, stop=True, skip_group_check=True))
                t.wait_ge(p_q, b_q + hp + 1)
                t.wait_ge(p_ev, max(0, b_ev + hp - 1))
                if hp == 0:
                    t.wait_ge(p_q2, b_q2)
                t.matmul(pO[hp % 2][0:64, :], vselt[0:64, hp * 64:(hp + 1) * 64],
                         pb_[hp % 2][0:64, :], start=True, stop=True, skip_group_check=True)
                t.matmul(pO[hp % 2][64:128, :], vselt[64:128, hp * 64:(hp + 1) * 64],
                         pb_[hp % 2][64:128, :], start=True, stop=True, skip_group_check=True)
                pMM.inc(t.matmul(pZ[:], selz[:, hp * 16:(hp + 1) * 16], pb_[hp % 2][:],
                                 start=(hp == 0), stop=(hp == 7), skip_group_check=True))

        @block.scalar
        def _(s):
            for hp in range(8):
                s.wait_ge(p_mm, b_mm + 2 * hp + 1)
                s.wait_ge(p_mm, max(0, b_mm + 2 * hp - 2))
                pQ.inc(s.activation(pb_[hp % 2][:], pS[hp % 2][:], AF.Exp,
                                    bias=zcol[:, 0:1], scale=attsc[:, 0:1]))
                if hp >= 1:
                    s.wait_ge(p_mm, b_mm + 2 * (hp - 1) + 2)
                    pEV.inc(s.copy(oT[hp - 1][:], pO[(hp - 1) % 2][:]))
            s.wait_ge(p_mm, b_mm + 16)
            pEV.inc(s.copy(oT[7][:], pO[7 % 2][:]))

        @block.vector
        def _(v):
            v.wait_ge(p_mm, b_mm + 16)
            v.wait_ge(t_d, b_td)
            vD.inc(v.tensor_copy(zt_[:], pZ[:]))
            v.wait_ge(v_d, vD.n)
            pQ2.inc(v.reciprocal(rzt[:], zt_[:]))

        @block.tensor
        def _(t):
            for hp in range(8):
                if hp == 0:
                    t.wait_ge(p_q2, b_q2 + 1)
                else:
                    t.wait_ge(p_rd, b_rd + hp)
                tD.inc(t.matmul(pY[0][:], selrz[:, hp * 128:(hp + 1) * 128], rzt[:],
                                start=True, stop=True))

        tD_base = tD.n - 8

        @block.vector
        def _(v):
            for hp in range(8):
                v.wait_ge(t_d, tD_base + hp + 1)
                v.wait_ge(p_ev, b_ev + hp + 1)
                pRD.inc(v.tensor_tensor(oT[hp][:], oT[hp][:], pY[0][:], AL.mult))

        b_mm2, b_evv = pMM.n, pEV2.n

        @block.tensor
        def _(t):
            t.wait_ge(p_rd, b_rd + 8)
            for gidx in range(8):
                tc4, ft = divmod(gidx, 2)
                t.wait_ge(p_ev2, max(0, b_evv + gidx - 1))
                for hp in range(8):
                    mm = t.matmul(pY[gidx % 2][:], oT[hp][:, tc4 * 128:(tc4 + 1) * 128],
                                  woTs[:, hp * 1024 + ft * TTN: hp * 1024 + (ft + 1) * TTN],
                                  start=(hp == 0), stop=(hp == 7))
                pMM.inc(mm)

        @block.vector
        def _(v):
            for gidx in range(8):
                tc4, ft = divmod(gidx, 2)
                v.wait_ge(p_mm, b_mm2 + gidx + 1)
                tcg = tt * 4 + tc4
                xs = sb_x2[:, tcg * D + ft * TTN: tcg * D + (ft + 1) * TTN]
                pEV2.inc(v.scalar_tensor_tensor(xs, pY[gidx % 2][:], 1.0, xs, AL.mult, AL.add))

    barrier()
    es8.close()
    es_q.close()

    if UPTO == "attn":
        finalize(sb_x2)
        es_x2.close(); es.close()
        return nc

    ffn("f2", sb_x2)

    finalize(sb_x2)
    es_x2.close()
    es.close()
    return nc


def _consts():
    import ml_dtypes
    ident = np.eye(128, dtype=np.float32)
    ones1 = np.ones((1, 128), np.float32)
    ones2b = np.ones((2, 128), ml_dtypes.bfloat16)
    iota4 = (np.arange(128, dtype=np.float32)[:, None] +
             128.0 * np.arange(4, dtype=np.float32)[None, :])
    selz = np.zeros((128, 128), np.float32)
    for hp in range(8):
        selz[0:64, hp * 16 + 2 * hp] = 1.0
        selz[64:128, hp * 16 + 2 * hp + 1] = 1.0
    selrz = np.zeros((16, 1024), np.float32)
    for hp in range(8):
        for p in range(128):
            selrz[2 * hp + p // 64, hp * 128 + p] = 1.0
    hselc = np.zeros((16, 2048), np.float32)
    for h in range(16):
        hselc[h, h * 128:(h + 1) * 128] = 1.0
    return ident, ones1, ones2b, iota4, selz, selrz, hselc


def kernel(**inputs):
    x = np.asarray(inputs["L0_attn_x"], np.float32).reshape(NCORES * T, D)
    ws = np.asarray(inputs["L0_attn_w_score"], np.float32)
    wsbc = np.zeros((128, 2), np.float32)
    wsbc[0:64, 0] = ws
    wsbc[64:128, 1] = ws
    ident, ones1, ones2b, iota4, selz, selrz, hselc = _consts()

    nc = _build()
    shared = {}
    for k in ("L0_attn_Wq", "L0_attn_Wk", "L0_attn_Wv", "L0_attn_Wo",
              "L0_ffn_w1", "L0_ffn_b1", "L0_ffn_w2", "L0_ffn_b2"):
        shared[k] = np.ascontiguousarray(np.asarray(inputs[k], np.float32))
    shared.update(c_ident=ident, c_ones1=ones1, c_ones2b=ones2b, c_iota4=iota4,
                  c_selz=selz, c_selrz=selrz, c_hsel=hselc, c_wsb=wsbc)
    in_maps = []
    for c in range(NCORES):
        m = dict(shared)
        m["L0_attn_x"] = np.ascontiguousarray(x[c * T:(c + 1) * T, :])
        in_maps.append(m)
    res = run_bass_kernel_spmd(nc, in_maps, core_ids=list(range(NCORES)))
    y = np.concatenate([np.asarray(r["y"], np.float32) for r in res.results], axis=0)
    return y.reshape(1, NCORES * T, D)


# version 19
# speedup vs baseline: 1.0092x; 1.0092x over previous
import numpy as np
import concourse.bass as bass
import concourse.mybir as mybir
from concourse.bass_utils import run_bass_kernel_spmd
from concourse.bass_core_rust import IndirectOffsetOnAxis
from contextlib import ExitStack

F32 = mybir.dt.float32
F16 = mybir.dt.float16
BF16 = mybir.dt.bfloat16
I32 = mybir.dt.int32
U32 = mybir.dt.uint32
AL = mybir.AluOpType
AF = mybir.ActivationFunctionType
AX = mybir.AxisListType

NCORES = 8
T = 2048
D = 1024
HID = 4096
NH = 16
HD = 64
NTC = T // 128
NDC = D // 128
NHC = HID // 128
TTN = 512
NTT = T // TTN
EPS = 1e-5
RB = 1536.0
NEG = -3.0e38

UPTO = None


class Ctr:
    def __init__(self, sem, step=1):
        self.sem, self.n, self.step = sem, 0, step

    def inc(self, instr, k=1):
        instr.then_inc(self.sem, k * self.step)
        self.n += k * self.step
        return self.n


def _build():
    nc = bass.Bass("TRN2", target_bir_lowering=False)

    d_x = nc.dram_tensor("L0_attn_x", [T, D], F32, kind="ExternalInput").ap()
    d_wq = nc.dram_tensor("L0_attn_Wq", [D, D], F32, kind="ExternalInput").ap()
    d_wk = nc.dram_tensor("L0_attn_Wk", [D, D], F32, kind="ExternalInput").ap()
    d_wv = nc.dram_tensor("L0_attn_Wv", [D, D], F32, kind="ExternalInput").ap()
    d_wo = nc.dram_tensor("L0_attn_Wo", [D, D], F32, kind="ExternalInput").ap()
    d_w1 = nc.dram_tensor("L0_ffn_w1", [HID, D], F32, kind="ExternalInput").ap()
    d_b1 = nc.dram_tensor("L0_ffn_b1", [HID], F32, kind="ExternalInput").ap()
    d_w2 = nc.dram_tensor("L0_ffn_w2", [D, HID], F32, kind="ExternalInput").ap()
    d_b2 = nc.dram_tensor("L0_ffn_b2", [D], F32, kind="ExternalInput").ap()
    c_ident = nc.dram_tensor("c_ident", [128, 128], F32, kind="ExternalInput").ap()
    c_ones1 = nc.dram_tensor("c_ones1", [1, 128], F32, kind="ExternalInput").ap()
    c_ones2b = nc.dram_tensor("c_ones2b", [2, 128], BF16, kind="ExternalInput").ap()
    c_iota4 = nc.dram_tensor("c_iota4", [128, 4], F32, kind="ExternalInput").ap()
    c_selz = nc.dram_tensor("c_selz", [128, 128], F32, kind="ExternalInput").ap()
    c_selrz = nc.dram_tensor("c_selrz", [16, 1024], F32, kind="ExternalInput").ap()
    c_hsel = nc.dram_tensor("c_hsel", [16, 2048], F32, kind="ExternalInput").ap()
    c_wsb = nc.dram_tensor("c_wsb", [128, 2], F32, kind="ExternalInput").ap()

    d_y = nc.dram_tensor("y", [T, D], F32, kind="ExternalOutput").ap()

    s_xq = nc.dram_tensor("s_xq", [T, D], BF16).ap()
    s_g = nc.dram_tensor("s_g", [HID, T], F32).ap()
    s_xln2 = nc.dram_tensor("s_xln2", [T, D], F32).ap()
    s_xpark = nc.dram_tensor("s_xpark", [T, D], F32).ap()
    s_w1q = nc.dram_tensor("s_w1q", [HID, D], BF16).ap()
    s_w2q = nc.dram_tensor("s_w2q", [D, HID], BF16).ap()
    s_wqT = nc.dram_tensor("s_wqT", [D, D], F32).ap()
    s_woT = nc.dram_tensor("s_woT", [D, D], F32).ap()
    s_wkvT = nc.dram_tensor("s_wkvT", [D, 2 * D], F32).ap()
    cc_in = nc.dram_tensor("cc_in", [1, 8], F32).ap()
    cc_out = nc.dram_tensor("cc_out", [1, 8], F32).ap()
    ag_in = nc.dram_tensor("ag_in", [1032, 128], F32).ap()
    ag_out = nc.dram_tensor("ag_out", [8 * 1032, 128], F32).ap()
    RG = [list(range(NCORES))]

    es = ExitStack()
    sbuf = lambda n, s, d=F32: es.enter_context(nc.sbuf_tensor(n, s, d))
    sem = lambda n: es.enter_context(nc.semaphore(n))

    ident = sbuf("ident", [128, 128])
    ones1 = sbuf("ones1", [1, 128])
    ones2b = sbuf("ones2b", [2, 128], BF16)
    iota4 = sbuf("iota4", [128, 4])
    selz = sbuf("selz", [128, 128])
    wsb = sbuf("wsb", [128, 2])
    Ut = sbuf("Ut", [128, 128])
    b1t = sbuf("b1t", [128, NHC])
    b1sc = sbuf("b1sc", [128, NHC])
    b2s = sbuf("b2s", [2, D], BF16)
    scal = sbuf("scal", [128, 16])
    one1 = sbuf("one1", [1, 16])
    cct = sbuf("cct", [1, 8])
    ccr = sbuf("ccr", [1, 8])
    stat = sbuf("stat", [128, 16 * 8])
    junk = es.enter_context(nc.sbuf_tensor("junkt", [128, 1024], F32, side="right"))
    zcol = sbuf("zcol", [128, 1])
    epst = sbuf("epst", [128, 1])
    rbt = sbuf("rbt", [128, 1])
    attsc = sbuf("attsc", [128, 1])
    kselT = sbuf("kselTt", [128, 8 * 64])
    vselt = sbuf("vselt", [128, 8 * 64])
    gmax = sbuf("gmaxt", [128, 1])
    gtmp = sbuf("gtmpt", [128, 1])
    lred = sbuf("lredt", [1, 128])

    PH = sem("PH")
    t_d, v_d, s_d, g_d = sem("t_d"), sem("v_d"), sem("s_d"), sem("g_d")
    q_sy, q_gp = sem("q_sy"), sem("q_gp")
    p_mm, p_ev, p_rd = sem("p_mm"), sem("p_ev"), sem("p_rd")
    p_ev2 = sem("p_ev2")
    p_q, p_q2 = sem("p_q"), sem("p_q2")
    st4 = [sem(f"st4_{i}") for i in range(4)]
    ld2 = [sem(f"ld2_{i}") for i in range(2)]
    ig2 = [sem(f"ig2_{i}") for i in range(2)]
    ldt2 = [sem(f"ldt2_{i}") for i in range(2)]
    ldu2 = [sem(f"ldu2_{i}") for i in range(2)]
    sts2 = [sem(f"sts2_{i}") for i in range(2)]

    ps_g = es.enter_context(nc.psum_tensor("ps_g", [128, 128], F32))

    block = es.enter_context(nc.Block())

    tD, vD, sD, gD = Ctr(t_d), Ctr(v_d), Ctr(s_d), Ctr(g_d)
    qSY, qGP = Ctr(q_sy, 16), Ctr(q_gp, 16)
    pMM, pEV, pRD = Ctr(p_mm), Ctr(p_ev), Ctr(p_rd)
    pEV2 = Ctr(p_ev2)
    ST4 = [Ctr(x, 16) for x in st4]
    LD2 = [Ctr(x, 16) for x in ld2]
    IG2 = [Ctr(x, 16) for x in ig2]
    LDT2 = [Ctr(x, 16) for x in ldt2]
    LDU2 = [Ctr(x, 16) for x in ldu2]
    STS2 = [Ctr(x, 16) for x in sts2]
    pQ, pQ2 = Ctr(p_q), Ctr(p_q2)

    phase = [0]

    def barrier():
        phase[0] += 1
        tgt = 5 * phase[0]

        @block.tensor
        def _(t):
            t.wait_ge(t_d, tD.n)
            t.wait_ge(p_mm, pMM.n)
            t.sem_inc(PH, 1); t.wait_ge(PH, tgt)

        @block.vector
        def _(v):
            v.wait_ge(v_d, vD.n)
            v.wait_ge(p_q2, pQ2.n)
            v.wait_ge(p_rd, pRD.n)
            v.wait_ge(p_ev2, pEV2.n)
            v.sem_inc(PH, 1); v.wait_ge(PH, tgt)

        @block.scalar
        def _(s):
            s.wait_ge(s_d, sD.n)
            s.wait_ge(p_q, pQ.n)
            s.wait_ge(p_ev, pEV.n)
            s.sem_inc(PH, 1); s.wait_ge(PH, tgt)

        @block.sync
        def _(sy):
            sy.wait_ge(q_sy, qSY.n)
            sy.sem_inc(PH, 1); sy.wait_ge(PH, tgt)

        @block.gpsimd
        def _(g):
            g.wait_ge(q_gp, qGP.n)
            g.wait_ge(g_d, gD.n)
            for k in range(4):
                g.wait_ge(st4[k], ST4[k].n)
            for k in range(2):
                g.wait_ge(ld2[k], LD2[k].n)
                g.wait_ge(ig2[k], IG2[k].n)
                g.wait_ge(ldt2[k], LDT2[k].n)
                g.wait_ge(ldu2[k], LDU2[k].n)
                g.wait_ge(sts2[k], STS2[k].n)
            g.sem_inc(PH, 1); g.wait_ge(PH, tgt)

    def bcast_cols(src_c0, n, dst_c0):
        @block.tensor
        def _(t):
            t.wait_ge(v_d, vD.n)
            tD.inc(t.matmul(ps_g[:, 0:n], ones1[:], one1[0:1, src_c0:src_c0 + n],
                            start=True, stop=True))

        @block.vector
        def _(v):
            v.wait_ge(t_d, tD.n)
            vD.inc(v.tensor_copy(scal[:, dst_c0:dst_c0 + n], ps_g[:, 0:n]))

    def allreduce_max(src_col, dst_col):
        @block.vector
        def _(v):
            vD.inc(v.memset(cct[:], 0.0))
            v.wait_ge(v_d, vD.n)
            vD.inc(v.tensor_copy(cct[0:1, 0:1], one1[0:1, src_col:src_col + 1]))

        @block.gpsimd
        def _(g):
            g.wait_ge(v_d, vD.n)
            qGP.inc(g.dma_start(out=cc_in, in_=cct[:]))
            g.wait_ge(q_gp, qGP.n)
            gD.inc(g.collective_compute("AllReduce", AL.max, replica_groups=RG,
                                        ins=[cc_in.opt()], outs=[cc_out.opt()]))
            g.wait_ge(g_d, gD.n)
            qGP.inc(g.dma_start(out=ccr[:], in_=cc_out))

        @block.vector
        def _(v):
            v.wait_ge(q_gp, qGP.n)
            vD.inc(v.tensor_copy(one1[0:1, dst_col:dst_col + 1], ccr[0:1, 0:1]))

    def cross_part_max(dst_one1_col):
        @block.tensor
        def _(t):
            t.wait_ge(v_d, vD.n)
            tD.inc(t.transpose(ps_g[0:1, 0:128], gmax[:], ident[:]))

        @block.vector
        def _(v):
            v.wait_ge(t_d, tD.n)
            vD.inc(v.tensor_copy(lred[:], ps_g[0:1, 0:128]))
            v.wait_ge(v_d, vD.n)
            vD.inc(v.reduce_max(one1[0:1, dst_one1_col:dst_one1_col + 1], lred[:], AX.X))

    es1 = ExitStack()
    sb_x = es1.enter_context(nc.sbuf_tensor("sb_x", [128, NTC * D], F32))

    @block.sync
    def _(sy):
        for src, dst in ((c_ident, ident), (c_ones1, ones1), (c_ones2b, ones2b),
                         (c_iota4, iota4), (c_selz, selz), (c_wsb, wsb)):
            qSY.inc(sy.dma_start(out=dst[:], in_=src))
        for tc in range(NTC):
            qSY.inc(sy.dma_start(out=sb_x[:, tc * D:(tc + 1) * D],
                                 in_=d_x[tc * 128:(tc + 1) * 128, :]))
        qSY.inc(sy.dma_start(out=b1t[:], in_=d_b1.rearrange("(a b) -> b a", b=128),
                             allow_slow_non_contiguous=True))

    @block.vector
    def _(v):
        vD.inc(v.memset(zcol[:], 0.0))
        vD.inc(v.memset(epst[:], EPS))
        vD.inc(v.memset(rbt[:], RB))
        vD.inc(v.memset(attsc[:], 0.125))
        vD.inc(v.memset(junk[:], 0.0))

    barrier()

    es0a = ExitStack()
    wbuf = [es0a.enter_context(nc.sbuf_tensor(f"wbuf{i}", [128, 4096], F32)) for i in range(2)]
    wmx8 = es0a.enter_context(nc.sbuf_tensor("wmx8", [128, 16], F32))
    wmx = es0a.enter_context(nc.sbuf_tensor("wmxq", [128, 2], F32))
    pt0 = es0a.enter_context(nc.psum_tensor("pt0", [128, 128], F32))
    pt1 = es0a.enter_context(nc.psum_tensor("pt1", [128, 128], F32))

    es0b = ExitStack()
    wbufB = [es0b.enter_context(nc.sbuf_tensor(f"wbufB{i}", [128, 4096], F32)) for i in range(2)]
    wzb = [es0b.enter_context(nc.sbuf_tensor(f"wzbq{i}", [128, 4096], F16)) for i in range(2)]
    wqb = [es0b.enter_context(nc.sbuf_tensor(f"wqbq{i}", [128, 4096], BF16)) for i in range(2)]
    WB4 = wbuf + wbufB
    LD4 = LD2 + LDU2
    ld4s = ld2 + ldu2

    def unit1(u, q=False):
        w1, w2 = (s_w1q, s_w2q) if q else (d_w1, d_w2)
        if u < 8:
            src = w1[u * 512:(u + 1) * 512, :]
        else:
            j = u - 8
            src = w2[(j % 2) * 512:(j % 2) * 512 + 512, (j // 2) * 1024:(j // 2) * 1024 + 1024]
        return src.rearrange("(r p) d -> p r d", p=128)

    esP1 = ExitStack()
    b2row = esP1.enter_context(nc.sbuf_tensor("b2row", [1, D], F32, side="right"))
    b2hi32 = esP1.enter_context(nc.sbuf_tensor("b2hi32", [1, D], F32, side="right"))
    b2lob = esP1.enter_context(nc.sbuf_tensor("b2lob", [1, D], BF16, side="right"))

    @block.sync
    def _(sy):
        qSY.inc(sy.dma_start(out=b2row[:], in_=d_b2[None, :]))

    def su_absmax():
        b_rd = pRD.n
        b_ld = [LD4[i].n for i in range(4)]

        @block.sync
        def _(sy):
            for u in range(16):
                if u >= 4:
                    sy.wait_ge(p_rd, b_rd + u - 3)
                LD4[u % 4].inc(sy.dma_start(out=WB4[u % 4].rearrange("p (r d) -> p r d", r=4),
                                            in_=unit1(u)))

        @block.vector
        def _(v):
            for u in range(16):
                v.wait_ge(ld4s[u % 4], b_ld[u % 4] + 16 * (u // 4 + 1))
                pRD.inc(v.reduce_max(wmx8[:, u:u + 1], WB4[u % 4][:], AX.X,
                                     apply_absolute_value=True))

    def su_scales():
        @block.vector
        def _(v):
            vD.inc(v.reduce_max(wmx[:, 0:1], wmx8[:, 0:8], AX.X))
            vD.inc(v.reduce_max(wmx[:, 1:2], wmx8[:, 8:16], AX.X))

        @block.tensor
        def _(t):
            t.wait_ge(v_d, vD.n)
            tD.inc(t.transpose(pt0[0:1, 0:128], wmx[:, 0:1], ident[:]))
            tD.inc(t.transpose(pt1[0:1, 0:128], wmx[:, 1:2], ident[:]))

        @block.vector
        def _(v):
            v.wait_ge(t_d, tD.n)
            vD.inc(v.tensor_copy(lred[:], pt0[0:1, 0:128]))
            v.wait_ge(v_d, vD.n)
            vD.inc(v.reduce_max(one1[0:1, 0:1], lred[:], AX.X))
            v.wait_ge(v_d, vD.n)
            vD.inc(v.tensor_copy(lred[:], pt1[0:1, 0:128]))
            v.wait_ge(v_d, vD.n)
            vD.inc(v.reduce_max(one1[0:1, 1:2], lred[:], AX.X))
            v.wait_ge(v_d, vD.n)
            vD.inc(v.tensor_scalar(one1[0:1, 4:5], one1[0:1, 0:1], 1e-6, 1.0 / 127.0, AL.max, AL.mult))
            vD.inc(v.tensor_scalar(one1[0:1, 5:6], one1[0:1, 1:2], 1e-6, 1.0 / 127.0, AL.max, AL.mult))
            v.wait_ge(v_d, vD.n)
            vD.inc(v.reciprocal(one1[0:1, 6:7], one1[0:1, 4:5]))
            vD.inc(v.reciprocal(one1[0:1, 7:8], one1[0:1, 5:6]))

        bcast_cols(6, 2, 0)

    def su_wquant(units, bufs, LDs, lds, zb, qb, rr, scalar_store=False):
        nb = len(bufs)
        nu = len(units)
        b_q, b_q2 = pQ.n, pQ2.n
        b_ld = [LDs[i].n for i in range(nb)]
        shape = (lambda a: a.rearrange("p (r d) -> p r d", r=rr)) if rr else (lambda a: a[:])
        STs = STS2 if scalar_store else ST4
        sts = sts2 if scalar_store else st4
        tgt_st = [STs[k % 2].n + 16 * (k // 2 + 1) for k in range(nu)]
        b_st = [STs[0].n, STs[1].n]
        b_st4 = [ST4[0].n, ST4[1].n]

        @block.sync
        def _(sy):
            for k in range(nu):
                if k >= nb:
                    sy.wait_ge(p_q, b_q + k - nb + 1)
                LDs[k % nb].inc(sy.dma_start(out=shape(bufs[k % nb]), in_=units[k][0]))

        @block.scalar
        def _(s):
            for k in range(nu):
                s.wait_ge(lds[k % nb], b_ld[k % nb] + 16 * (k // nb + 1))
                s.wait_ge(p_q2, b_q2 + k - 1 if k >= 2 else b_q2)
                pQ.inc(s.activation(zb[k % 2][:], bufs[k % nb][:], AF.Identity,
                                    bias=rbt[:, 0:1], scale=scal[:, units[k][2]:units[k][2] + 1]))
                if scalar_store and k >= 1:
                    s.wait_ge(p_q2, b_q2 + k)
                    STs[(k - 1) % 2].inc(s.dma_start(out=units[k - 1][1], in_=shape(qb[(k - 1) % 2])))
            if scalar_store:
                s.wait_ge(p_q2, b_q2 + nu)
                STs[(nu - 1) % 2].inc(s.dma_start(out=units[nu - 1][1], in_=shape(qb[(nu - 1) % 2])))

        @block.vector
        def _(v):
            for k in range(nu):
                v.wait_ge(p_q, b_q + k + 1)
                v.wait_ge(sts[k % 2], tgt_st[k - 2] if k >= 2 else b_st[k % 2])
                if k < 2 and scalar_store:
                    v.wait_ge(st4[k % 2], b_st4[k % 2])
                pQ2.inc(v.tensor_scalar(qb[k % 2][:], zb[k % 2][:], -RB, None, AL.add))

        if not scalar_store:
            @block.gpsimd
            def _(g):
                for k in range(nu):
                    g.wait_ge(p_q2, b_q2 + k + 1)
                    STs[k % 2].inc(g.dma_start(out=units[k][1], in_=shape(qb[k % 2])))

    def su_wquant1():
        su_wquant([(unit1(u), unit1(u, True), 0) for u in range(8)],
                  WB4, LD4, ld4s, wzb, wqb, 4)

    def close_es0b():
        es0b.close()
        es0a.close()
        esP1.close()

    es0c = ExitStack()

    def qtr2(u, qq, q=False):
        w2 = s_w2q if q else d_w2
        j = u - 8
        rr = (j % 2) * 512 + qq * 128
        cb = (j // 2) * 1024
        return w2[rr:rr + 128, cb:cb + 1024]

    def su_wquant2():
        wbuf2 = [es0c.enter_context(nc.sbuf_tensor(f"wbc{i}", [128, 4096], F32)) for i in range(2)]
        wzb2 = [es0c.enter_context(nc.sbuf_tensor(f"wzc{i}", [128, 4096], F16)) for i in range(2)]
        wqb2 = [es0c.enter_context(nc.sbuf_tensor(f"wqc{i}", [128, 4096], BF16)) for i in range(2)]
        su_wquant([(unit1(u), unit1(u, True), 1) for u in range(8, 16)],
                  wbuf2, LD4[0:2], ld4s[0:2], wzb2, wqb2, 4)

    def close_es0():
        es0c.close()

    def mk_trans():
        es_tr = ExitStack()
        wrow = [es_tr.enter_context(nc.sbuf_tensor(f"wrow{i}", [128, 1024], F32)) for i in range(2)]
        tev = [es_tr.enter_context(nc.sbuf_tensor(f"tev{i}", [128, 1024], F32)) for i in range(2)]
        pt2 = [es_tr.enter_context(nc.psum_tensor(f"pt2{i}", [128, 128], F32)) for i in range(2)]
        pU = es_tr.enter_context(nc.psum_tensor("pUq", [128, 4], F32))
        jobs = [(w, ec) for w in (2, 3) for ec in range(8)]
        wsrc = {0: d_wq, 1: d_wo, 2: d_wk, 3: d_wv}
        EV0, RD0, TD0 = pEV.n, pRD.n, tD.n
        tgt_ld = [LDT2[j % 2].n + 16 * (j // 2 + 1) for j in range(16)]
        acc = [ST4[2].n, ST4[3].n]
        st_after = {}
        for j, (w, ec) in enumerate(jobs):
            acc[j % 2] += 16 * (1 if w < 2 else 2)
            st_after[j] = acc[j % 2]
        td_tp, td_umm = {}, {}
        cnt = TD0
        for j, (w, ec) in enumerate(jobs):
            for dc in range(8):
                cnt += 1
                td_tp[(j, dc)] = cnt
            if w == 2:
                for dc in range(8):
                    cnt += 1
                    td_umm[(j, dc)] = cnt

        def emit_chunk(tt, b_ev2=None):
            js = list(range(tt * 4, tt * 4 + 4))

            @block.sync
            def _(sy):
                for j in js:
                    if j >= 2:
                        sy.wait_ge(p_ev, EV0 + 8 * (j - 1))
                        if jobs[j - 2][0] == 2:
                            sy.wait_ge(t_d, td_umm[(j - 2, 7)])
                    LDT2[j % 2].inc(sy.dma_start(
                        out=wrow[j % 2][:],
                        in_=wsrc[jobs[j][0]][jobs[j][1] * 128:(jobs[j][1] + 1) * 128, :]))

            @block.tensor
            def _(t):
                for j in js:
                    w, ec = jobs[j]
                    t.wait_ge(ldt2[j % 2], tgt_ld[j])
                    for dc in range(8):
                        gi = j * 8 + dc
                        if gi >= 2:
                            t.wait_ge(p_ev, EV0 + gi - 1)
                        tD.inc(t.transpose(pt2[gi % 2][:, 0:128],
                                           wrow[j % 2][:, dc * 128:(dc + 1) * 128], ident[:]))
                    if w == 2:
                        for dc in range(8):
                            ku = j * 8 + dc
                            if ku >= 2:
                                t.wait_ge(p_rd, RD0 + ku - 1)
                            tD.inc(t.matmul(pU[:, (ku % 2) * 2:(ku % 2) * 2 + 2],
                                            wrow[j % 2][:, dc * 128:(dc + 1) * 128],
                                            wsb[:], start=True, stop=True))

            @block.scalar
            def _(s):
                for j in js:
                    if j >= 2:
                        s.wait_ge(st4[2 + (j % 2)], st_after[j - 2])
                    for dc in range(8):
                        gi = j * 8 + dc
                        s.wait_ge(t_d, td_tp[(j, dc)])
                        pEV.inc(s.copy(tev[j % 2][:, dc * 128:(dc + 1) * 128], pt2[gi % 2][:, 0:128]))

            @block.vector
            def _(v):
                for j in js:
                    w, ec = jobs[j]
                    if w != 2:
                        continue
                    for dc in range(8):
                        ku = j * 8 + dc
                        v.wait_ge(t_d, td_umm[(j, dc)])
                        pRD.inc(v.tensor_copy(Ut[:, dc * 16 + 2 * ec: dc * 16 + 2 * ec + 2],
                                              pU[:, (ku % 2) * 2:(ku % 2) * 2 + 2]))

            @block.gpsimd
            def _(g):
                for j in js:
                    w, ec = jobs[j]
                    g.wait_ge(p_ev, EV0 + 8 * (j + 1))
                    src3 = tev[j % 2].rearrange("p (dc c) -> p dc c", dc=8)
                    SS = ST4[2 + (j % 2)]
                    if w == 0:
                        SS.inc(g.dma_start(
                            out=s_wqT[:, ec * 128:(ec + 1) * 128].rearrange("(dc p) c -> p dc c", p=128),
                            in_=src3))
                    elif w == 1:
                        SS.inc(g.dma_start(
                            out=s_woT[:, ec * 128:(ec + 1) * 128].rearrange("(dc p) c -> p dc c", p=128),
                            in_=src3))
                    else:
                        off = 0 if w == 2 else 64
                        srcA = tev[j % 2].rearrange("p (dc h c) -> p dc h c", dc=8, h=2)
                        SS.inc(g.dma_start(
                            out=s_wkvT[:, (2 * ec) * 128 + off:(2 * ec) * 128 + off + 64]
                                .rearrange("(dc p) c -> p dc c", p=128),
                            in_=srcA[:, :, 0, :]))
                        SS.inc(g.dma_start(
                            out=s_wkvT[:, (2 * ec + 1) * 128 + off:(2 * ec + 1) * 128 + off + 64]
                                .rearrange("(dc p) c -> p dc c", p=128),
                            in_=srcA[:, :, 1, :]))

        return emit_chunk, es_tr.close

    def su_b2split():

        @block.vector
        def _(v):
            vD.inc(v.tensor_copy(b2s[0:1, :], b2row[:]))
            v.wait_ge(v_d, vD.n)
            vD.inc(v.tensor_copy(b2hi32[:], b2s[0:1, :]))
            v.wait_ge(v_d, vD.n)
            vD.inc(v.tensor_tensor(b2lob[:], b2row[:], b2hi32[:], AL.subtract))

        @block.gpsimd
        def _(g):
            g.wait_ge(v_d, vD.n)
            qGP.inc(g.dma_start(out=b2s[1:2, :], in_=b2lob[:]))

    def ln_stats(xbuf, with_amax, co_stats=(), co_math=(), skip_stats=False):
        if not skip_stats:
            @block.scalar
            def _(s):
                for tc in range(NTC):
                    xs = xbuf[:, tc * D:(tc + 1) * D]
                    s.wait_ge(p_q, pQ.n)
                    pQ.inc(s.activation(junk[:], xs, AF.Identity, bias=zcol[:, 0:1], scale=1.0,
                                        accum_out=stat[:, 0 + tc:0 + tc + 1]))
                    s.wait_ge(p_q, pQ.n)
                    pQ.inc(s.activation(junk[:], xs, AF.Square, bias=zcol[:, 0:1], scale=1.0,
                                        accum_out=stat[:, 16 + tc:16 + tc + 1]))

        if with_amax and not skip_stats:
            @block.vector
            def _(v):
                for tc in range(NTC):
                    xs = xbuf[:, tc * D:(tc + 1) * D]
                    vD.inc(v.reduce_max(stat[:, 96 + tc:96 + tc + 1], xs, AX.X))
                    vD.inc(v.tensor_reduce(stat[:, 112 + tc:112 + tc + 1], xs, AX.X, AL.min))

        for f in co_stats:
            f()
        if (not skip_stats) or co_stats:
            barrier()

        @block.vector
        def _(v):
            MN, VAR, T4 = stat[:, 32:48], stat[:, 16:32], stat[:, 64:80]
            vD.inc(v.tensor_scalar(MN, stat[:, 0:16], 1.0 / D, None, AL.mult))
            vD.inc(v.tensor_scalar(VAR, VAR, 1.0 / D, None, AL.mult))
            v.wait_ge(v_d, vD.n)
            vD.inc(v.tensor_tensor(T4, MN, MN, AL.mult))
            v.wait_ge(v_d, vD.n)
            vD.inc(v.tensor_tensor(VAR, VAR, T4, AL.subtract))

        @block.scalar
        def _(s):
            s.wait_ge(v_d, vD.n)
            pQ.inc(s.activation(stat[:, 64:80], stat[:, 16:32], AF.Sqrt,
                                bias=epst[:, 0:1], scale=1.0))

        @block.vector
        def _(v):
            v.wait_ge(p_q, pQ.n)
            vD.inc(v.reciprocal(stat[:, 48:64], stat[:, 64:80]))

        if with_amax:
            @block.vector
            def _(v):
                MN, RS, T6, T7 = stat[:, 32:48], stat[:, 48:64], stat[:, 96:112], stat[:, 112:128]
                v.wait_ge(v_d, vD.n)
                vD.inc(v.tensor_tensor(T6, T6, MN, AL.subtract))
                vD.inc(v.tensor_tensor(T7, MN, T7, AL.subtract))
                v.wait_ge(v_d, vD.n)
                vD.inc(v.tensor_tensor(T6, T6, T7, AL.max))
                v.wait_ge(v_d, vD.n)
                vD.inc(v.tensor_tensor(T6, T6, RS, AL.mult))
                v.wait_ge(v_d, vD.n)
                vD.inc(v.reduce_max(gmax[:], T6, AX.X))

        for f in co_math:
            f()
        barrier()

    def ffn(tag, xbuf, co_stats=(), co_math=(), co_ar=(), post_ar=None,
            co_qx=(), post_qx=None, co_sg=(), post_sg=None, mk_tt=None,
            skip_stats=False):
        ln_stats(xbuf, with_amax=True, co_stats=co_stats, co_math=co_math,
                 skip_stats=skip_stats)
        cross_part_max(8)
        barrier()
        allreduce_max(8, 9)
        for f in co_ar:
            f()
        barrier()
        if post_ar:
            post_ar()

        @block.vector
        def _(v):
            vD.inc(v.tensor_scalar(one1[0:1, 12:13], one1[0:1, 9:10], 1e-6, 1.0 / 127.0,
                                   AL.max, AL.mult))
            v.wait_ge(v_d, vD.n)
            vD.inc(v.reciprocal(one1[0:1, 13:14], one1[0:1, 12:13]))
            vD.inc(v.tensor_tensor(one1[0:1, 14:15], one1[0:1, 12:13], one1[0:1, 4:5], AL.mult))

        bcast_cols(13, 2, 2)
        barrier()

        @block.vector
        def _(v):
            A, B = stat[:, 64:80], stat[:, 80:96]
            vD.inc(v.tensor_scalar(A, stat[:, 48:64], scal[:, 2:3], None, AL.mult))
            v.wait_ge(v_d, vD.n)
            vD.inc(v.tensor_tensor(B, stat[:, 32:48], A, AL.mult))
            v.wait_ge(v_d, vD.n)
            vD.inc(v.tensor_scalar(B, B, -1.0, RB, AL.mult, AL.add))
            vD.inc(v.tensor_scalar(b1sc[:], b1t[:], scal[:, 3:4], None, AL.mult))

        barrier()

        esA = ExitStack()
        xqT = esA.enter_context(nc.sbuf_tensor(f"{tag}xqT", [128, NDC * T], BF16))
        w1qT = esA.enter_context(nc.sbuf_tensor(f"{tag}w1qT", [128, NDC * HID], BF16))
        gb = [esA.enter_context(nc.sbuf_tensor(f"{tag}gb{i}", [128, TTN], F32)) for i in range(4)]
        pA = [esA.enter_context(nc.psum_tensor(f"{tag}pA{i}", [128, TTN], F32)) for i in range(2)]
        W1T = (ST4[0].n, ST4[1].n)
        esq = ExitStack()
        zq = [esq.enter_context(nc.sbuf_tensor(f"{tag}zq{i}", [128, 1024], F16, side="right")) for i in range(2)]
        xqb = [esq.enter_context(nc.sbuf_tensor(f"{tag}xqb{i}", [128, 1024], BF16, side="right")) for i in range(2)]
        b_q, b_q2 = pQ.n, pQ2.n
        tgt_st = [ST4[tc % 2].n + 16 * (tc // 2 + 1) for tc in range(NTC)]

        @block.scalar
        def _(s):
            for tc in range(NTC):
                if tc >= 2:
                    s.wait_ge(p_q2, b_q2 + tc - 1)
                pQ.inc(s.activation(zq[tc % 2][:], xbuf[:, tc * D:(tc + 1) * D], AF.Identity,
                                    bias=stat[:, 80 + tc:80 + tc + 1],
                                    scale=stat[:, 64 + tc:64 + tc + 1]))

        @block.vector
        def _(v):
            for tc in range(NTC):
                v.wait_ge(p_q, b_q + tc + 1)
                if tc >= 2:
                    v.wait_ge(st4[tc % 2], tgt_st[tc - 2])
                pQ2.inc(v.tensor_scalar(xqb[tc % 2][:], zq[tc % 2][:], -RB, None, AL.add))

        @block.gpsimd
        def _(g):
            for tc in range(NTC):
                g.wait_ge(p_q2, b_q2 + tc + 1)
                ST4[tc % 2].inc(g.dma_start(out=s_xq[tc * 128:(tc + 1) * 128, :], in_=xqb[tc % 2][:]))

        @block.sync
        def _(sy):
            sy.wait_ge(st4[0], W1T[0])
            sy.wait_ge(st4[1], W1T[1])
            for dc in range(NDC):
                qSY.inc(sy.dma_start(out=w1qT[:, dc * HID:(dc + 1) * HID],
                                     in_=s_w1q[:, dc * 128:(dc + 1) * 128], transpose=True))
            sy.wait_ge(st4[0], ST4[0].n)
            sy.wait_ge(st4[1], ST4[1].n)
            for dc in range(NDC):
                qSY.inc(sy.dma_start(out=xqT[:, dc * T:(dc + 1) * T],
                                     in_=s_xq[:, dc * 128:(dc + 1) * 128], transpose=True))

        for f in co_qx:
            f(zq, xqb)
        barrier()
        if post_qx:
            post_qx()
        esq.close()


        @block.vector
        def _(v):
            vD.inc(v.memset(gmax[:], 0.0))
        b_vd = vD.n

        b_mm, b_ev, b_rd = pMM.n, pEV.n, pRD.n
        ntile = NTT * NHC
        tgt_st = [ST4[i % 4].n + 16 * (i // 4 + 1) for i in range(ntile)]

        @block.tensor
        def _(t):
            for i in range(ntile):
                tt, fc = divmod(i, NHC)
                if i >= 2:
                    t.wait_ge(p_ev, b_ev + i - 1)
                for dc in range(NDC):
                    mm = t.matmul(pA[i % 2][:],
                                  w1qT[:, dc * HID + fc * 128: dc * HID + (fc + 1) * 128],
                                  xqT[:, dc * T + tt * TTN: dc * T + (tt + 1) * TTN],
                                  start=(dc == 0), stop=(dc == NDC - 1))
                pMM.inc(mm)

        @block.scalar
        def _(s):
            for i in range(ntile):
                tt, fc = divmod(i, NHC)
                s.wait_ge(p_mm, b_mm + i + 1)
                if i >= 4:
                    s.wait_ge(p_rd, b_rd + (i - 3))
                    s.wait_ge(st4[i % 4], tgt_st[i - 4])
                pEV.inc(s.activation(gb[i % 4][:], pA[i % 2][:], AF.Gelu,
                                     bias=b1sc[:, fc:fc + 1], scale=scal[:, 3:4]))

        @block.vector
        def _(v):
            for i in range(ntile):
                v.wait_ge(p_ev, b_ev + i + 1)
                if i == 0:
                    v.wait_ge(v_d, b_vd)
                else:
                    v.wait_ge(p_rd, b_rd + i)
                vD.inc(v.reduce_max(gtmp[:], gb[i % 4][:], AX.X, apply_absolute_value=True))
                v.wait_ge(v_d, vD.n)
                pRD.inc(v.tensor_tensor(gmax[:], gmax[:], gtmp[:], AL.max))

        @block.gpsimd
        def _(g):
            for i in range(ntile):
                tt, fc = divmod(i, NHC)
                g.wait_ge(p_ev, b_ev + i + 1)
                ST4[i % 4].inc(g.dma_start(out=s_g[fc * 128:(fc + 1) * 128, tt * TTN:(tt + 1) * TTN],
                                           in_=gb[i % 4][:]))

        barrier()
        esA.close()

        es_w2 = ExitStack()
        w2qT = es_w2.enter_context(nc.sbuf_tensor(f"{tag}w2qT", [128, NHC * D], BF16))
        cross_part_max(10)
        barrier()
        allreduce_max(10, 11)
        b2st = (ST4[0].n, ST4[1].n)
        for f in co_sg:
            f()
        w2busy = ST4[0].n != b2st[0]

        @block.sync
        def _(sy):
            for hc in range(NHC):
                if w2busy and hc % 8 == 0:
                    cb = hc // 8
                    sy.wait_ge(st4[0], b2st[0] + 16 * (cb + 1))
                    sy.wait_ge(st4[1], b2st[1] + 16 * (cb + 1))
                qSY.inc(sy.dma_start(out=w2qT[:, hc * D:(hc + 1) * D],
                                     in_=s_w2q[:, hc * 128:(hc + 1) * 128], transpose=True))

        barrier()
        if post_sg:
            post_sg()

        @block.vector
        def _(v):
            vD.inc(v.tensor_scalar(one1[0:1, 12:13], one1[0:1, 11:12], 1e-6, 1.0 / 127.0,
                                   AL.max, AL.mult))
            v.wait_ge(v_d, vD.n)
            vD.inc(v.reciprocal(one1[0:1, 13:14], one1[0:1, 12:13]))
            vD.inc(v.tensor_tensor(one1[0:1, 14:15], one1[0:1, 12:13], one1[0:1, 5:6], AL.mult))
            v.wait_ge(v_d, vD.n)
            vD.inc(v.tensor_scalar(one1[0:1, 14:15], one1[0:1, 14:15], 0.5, None, AL.mult))

        bcast_cols(13, 2, 4)
        barrier()

        esB = ExitStack()
        gq = esB.enter_context(nc.sbuf_tensor(f"{tag}gq", [128, NHC * TTN], BF16))
        gl = [esB.enter_context(nc.sbuf_tensor(f"{tag}gl{i}", [128, TTN], F32)) for i in range(2)]
        zg = [esB.enter_context(nc.sbuf_tensor(f"{tag}zg{i}", [128, TTN], F16)) for i in range(2)]
        pB = [esB.enter_context(nc.psum_tensor(f"{tag}pB{i}", [128, TTN], F32)) for i in range(4)]

        tt_emit, tt_close = mk_tt() if mk_tt else (None, None)
        ev_hist = {}
        for tt in range(NTT):
            b_q, b_q2, b_mm, b_ev = pQ.n, pQ2.n, pMM.n, pEV2.n
            tgt_ld = [LD2[hc % 2].n + 16 * (hc // 2 + 1) for hc in range(NHC)]

            @block.sync
            def _(sy):
                for hc in range(NHC):
                    sy.wait_ge(p_q, max(0, b_q + hc - 1))
                    LD2[hc % 2].inc(sy.dma_start(out=gl[hc % 2][:],
                                                 in_=s_g[hc * 128:(hc + 1) * 128, tt * TTN:(tt + 1) * TTN]))

            @block.scalar
            def _(s):
                for hc in range(NHC):
                    s.wait_ge(ld2[hc % 2], tgt_ld[hc])
                    s.wait_ge(p_q2, max(0, b_q2 + hc - 1))
                    pQ.inc(s.activation(zg[hc % 2][:], gl[hc % 2][:], AF.Identity,
                                        bias=rbt[:, 0:1], scale=scal[:, 4:5]))

            @block.vector
            def _(v):
                for hc in range(NHC):
                    v.wait_ge(p_q, b_q + hc + 1)
                    v.wait_ge(p_mm, max(0, b_mm - (NHC + 4) + hc + 1))
                    pQ2.inc(v.tensor_scalar(gq[:, hc * TTN:(hc + 1) * TTN], zg[hc % 2][:],
                                            -RB, None, AL.add))

            @block.tensor
            def _(t):
                for ft in range(2):
                    for hc in range(NHC):
                        t.wait_ge(p_q2, b_q2 + hc + 1)
                        if hc == 0:
                            t.wait_ge(p_ev2, max(0, b_ev + 4 * ft))
                        for tc4 in range(4):
                            mm = t.matmul(pB[tc4][:],
                                          gq[:, hc * TTN + tc4 * 128: hc * TTN + (tc4 + 1) * 128],
                                          w2qT[:, hc * D + ft * TTN: hc * D + (ft + 1) * TTN],
                                          start=(hc == 0), stop=False)
                        pMM.inc(mm)
                    for tc4 in range(4):
                        pMM.inc(t.matmul(pB[tc4][:], ones2b[:],
                                         b2s[0:2, ft * TTN:(ft + 1) * TTN], start=False, stop=True))

            @block.vector
            def _(v):
                for ft in range(2):
                    for tc4 in range(4):
                        v.wait_ge(p_mm, b_mm + (NHC + 4) * (ft + 1))
                        tcg = tt * 4 + tc4
                        xs = xbuf[:, tcg * D + ft * TTN: tcg * D + (ft + 1) * TTN]
                        pEV2.inc(v.scalar_tensor_tensor(xs, pB[tc4][:], scal[:, 5:6],
                                                        xs, AL.mult, AL.add))

            if tt_emit and tt >= 1:
                tt_emit(tt - 1, ev_hist[tt - 1])
            if tt_emit:
                ev_hist[tt] = b_ev

        if tt_emit:
            tt_emit(NTT - 1, ev_hist[NTT - 1])
        barrier()
        if tt_close:
            tt_close()
        esB.close()
        es_w2.close()

    def finalize(xbuf):
        @block.gpsimd
        def _(g):
            for tc in range(NTC):
                qGP.inc(g.dma_start(out=d_y[tc * 128:(tc + 1) * 128, :],
                                    in_=xbuf[:, tc * D:(tc + 1) * D]))
        barrier()

    def ln2_stats_tt(tt, b_ev2):
        @block.scalar
        def _(s):
            for tc4 in range(4):
                tcg = tt * 4 + tc4
                xs = sb_x[:, tcg * D:(tcg + 1) * D]
                s.wait_ge(p_ev2, b_ev2 + 4 + tc4 + 1)
                s.wait_ge(p_q, pQ.n)
                pQ.inc(s.activation(junk[:], xs, AF.Identity, bias=zcol[:, 0:1], scale=1.0,
                                    accum_out=stat[:, tcg:tcg + 1]))
                s.wait_ge(p_q, pQ.n)
                pQ.inc(s.activation(junk[:], xs, AF.Square, bias=zcol[:, 0:1], scale=1.0,
                                    accum_out=stat[:, 16 + tcg:16 + tcg + 1]))

    def mk_trans_stats():
        emit_chunk, close = mk_trans()

        def emit(tt, b_ev2):
            emit_chunk(tt, b_ev2)
            ln2_stats_tt(tt, b_ev2)
        return emit, close

    ffn("f1", sb_x, co_stats=[su_absmax], co_math=[su_scales, su_b2split],
        co_ar=[su_wquant1], post_ar=close_es0b,
        co_sg=[su_wquant2], post_sg=close_es0, mk_tt=mk_trans_stats)

    if UPTO == "ffn1":
        finalize(sb_x)
        es1.close(); es.close()
        return nc

    ln_stats(sb_x, with_amax=False, skip_stats=True)

    es_xlt = ExitStack()
    xlt = es_xlt.enter_context(nc.sbuf_tensor("xlt", [128, NDC * T], F32, side="right"))
    es2 = ExitStack()
    xl2 = [es2.enter_context(nc.sbuf_tensor(f"xl2_{i}", [128, 1024], F32)) for i in range(2)]
    ptA = es2.enter_context(nc.psum_tensor("ptA", [128, 128], F32))
    ptB = es2.enter_context(nc.psum_tensor("ptB", [128, 128], F32))

    b_q, b_mm, b_ev = pQ.n, pMM.n, pEV.n
    tgt_st = [ST4[tc % 2].n + 16 * (tc // 2 + 1) for tc in range(NTC)]

    @block.vector
    def _(v):
        for tc in range(NTC):
            if tc >= 2:
                v.wait_ge(p_mm, b_mm + 8 * (tc - 1))
                v.wait_ge(st4[tc % 2], tgt_st[tc - 2])
            pQ.inc(v.tensor_scalar(xl2[tc % 2][:], sb_x[:, tc * D:(tc + 1) * D],
                                   stat[:, 32 + tc:32 + tc + 1], stat[:, 48 + tc:48 + tc + 1],
                                   AL.subtract, AL.mult))

    @block.gpsimd
    def _(g):
        for tc in range(NTC):
            g.wait_ge(p_q, b_q + tc + 1)
            ST4[tc % 2].inc(g.dma_start(out=s_xln2[tc * 128:(tc + 1) * 128, :], in_=xl2[tc % 2][:]))

    @block.tensor
    def _(t):
        for tc in range(NTC):
            t.wait_ge(p_q, b_q + tc + 1)
            for dc in range(NDC):
                i = tc * 8 + dc
                if i >= 2:
                    t.wait_ge(p_ev, b_ev + i - 1)
                pp = ptA if i % 2 == 0 else ptB
                pMM.inc(t.transpose(pp[:, 0:128], xl2[tc % 2][:, dc * 128:(dc + 1) * 128], ident[:]))

    @block.scalar
    def _(s):
        for tc in range(NTC):
            for dc in range(NDC):
                i = tc * 8 + dc
                s.wait_ge(p_mm, b_mm + i + 1)
                pp = ptA if i % 2 == 0 else ptB
                pEV.inc(s.copy(xlt[:, dc * T + tc * 128: dc * T + (tc + 1) * 128], pp[:, 0:128]))

    wrow2 = [es2.enter_context(nc.sbuf_tensor(f"wrw2_{i}", [128, 1024], F32)) for i in range(2)]
    tev2 = [es2.enter_context(nc.sbuf_tensor(f"tv2_{i}", [128, 1024], F32)) for i in range(2)]
    pt3 = [es2.enter_context(nc.psum_tensor(f"pt3{i}", [128, 128], F32)) for i in range(2)]
    jobsQ = [(w, ec) for w in (0, 1) for ec in range(8)]
    EV0q = pEV.n
    TD0q = tD.n
    tgt_ldq = [LDT2[j % 2].n + 16 * (j // 2 + 1) for j in range(16)]
    accq = [ST4[2].n, ST4[3].n]
    st_afterq = {}
    for j in range(16):
        accq[j % 2] += 16
        st_afterq[j] = accq[j % 2]

    @block.sync
    def _(sy):
        for j, (w, ec) in enumerate(jobsQ):
            if j >= 2:
                sy.wait_ge(p_ev, EV0q + 8 * (j - 1))
            LDT2[j % 2].inc(sy.dma_start(
                out=wrow2[j % 2][:],
                in_=(d_wq if w == 0 else d_wo)[ec * 128:(ec + 1) * 128, :]))

    @block.tensor
    def _(t):
        for j, (w, ec) in enumerate(jobsQ):
            t.wait_ge(ldt2[j % 2], tgt_ldq[j])
            for dc in range(8):
                gi = j * 8 + dc
                if gi >= 2:
                    t.wait_ge(p_ev, EV0q + gi - 1)
                tD.inc(t.transpose(pt3[gi % 2][:, 0:128],
                                   wrow2[j % 2][:, dc * 128:(dc + 1) * 128], ident[:]))

    @block.scalar
    def _(s):
        for j, (w, ec) in enumerate(jobsQ):
            if j >= 2:
                s.wait_ge(st4[2 + (j % 2)], st_afterq[j - 2])
            for dc in range(8):
                gi = j * 8 + dc
                s.wait_ge(t_d, TD0q + gi + 1)
                pEV.inc(s.copy(tev2[j % 2][:, dc * 128:(dc + 1) * 128], pt3[gi % 2][:, 0:128]))

    @block.gpsimd
    def _(g):
        for j, (w, ec) in enumerate(jobsQ):
            g.wait_ge(p_ev, EV0q + 8 * (j + 1))
            src3 = tev2[j % 2].rearrange("p (dc c) -> p dc c", dc=8)
            dst = (s_wqT if w == 0 else s_woT)[:, ec * 128:(ec + 1) * 128]
            ST4[2 + (j % 2)].inc(g.dma_start(
                out=dst.rearrange("(dc p) c -> p dc c", p=128), in_=src3))

    barrier()
    es2.close()

    @block.gpsimd
    def _(g):
        for tc in range(NTC):
            qGP.inc(g.dma_start(out=s_xpark[tc * 128:(tc + 1) * 128, :],
                                in_=sb_x[:, tc * D:(tc + 1) * D]))

    barrier()
    es1.close()

    es_q = ExitStack()
    qT = es_q.enter_context(nc.sbuf_tensor("qTt", [128, NDC * T], F32))
    es6 = ExitStack()
    wqe = [es6.enter_context(nc.sbuf_tensor(f"wqe{i}", [128, 1024], F32)) for i in range(2)]
    pQP = [es6.enter_context(nc.psum_tensor(f"pQP{i}", [128, TTN], F32)) for i in range(2)]
    qk = [0]
    qtd = {}
    q_ld = {}
    BQ = pQ.n

    def qproj_emit(n):
        ks = list(range(qk[0], min(8 * NTT, qk[0] + n)))
        qk[0] += len(ks)
        if not ks:
            return
        ecs = [ec for ec in sorted({k // NTT for k in ks}) if ec not in q_ld]

        @block.sync
        def _(sy):
            for ec in ecs:
                if ec >= 2:
                    sy.wait_ge(p_q, BQ + 4 * (ec - 1))
                LDT2[ec % 2].inc(sy.dma_start(
                    out=wqe[ec % 2].rearrange("p (dc c) -> p dc c", dc=8),
                    in_=s_wqT[:, ec * 128:(ec + 1) * 128].rearrange("(dc p) c -> p dc c", p=128)))
                q_ld[ec] = LDT2[ec % 2].n

        @block.tensor
        def _(t):
            for k in ks:
                ec, tt = divmod(k, NTT)
                if tt == 0 or k == ks[0]:
                    t.wait_ge(ldt2[ec % 2], q_ld[ec])
                if k >= 2:
                    t.wait_ge(p_q, BQ + k - 1)
                for dc in range(NDC):
                    mm = t.matmul(pQP[k % 2][:],
                                  wqe[ec % 2][:, dc * 128:(dc + 1) * 128],
                                  xlt[:, dc * T + tt * TTN: dc * T + (tt + 1) * TTN],
                                  start=(dc == 0), stop=(dc == NDC - 1))
                tD.inc(mm)
                qtd[k] = tD.n

        @block.scalar
        def _(s):
            for k in ks:
                ec, tt = divmod(k, NTT)
                s.wait_ge(t_d, qtd[k])
                pQ.inc(s.copy(qT[:, ec * T + tt * TTN: ec * T + (tt + 1) * TTN], pQP[k % 2][:]))

    es3 = ExitStack()
    mval = es3.enter_context(nc.sbuf_tensor("mval", [16, 64], F32, side="right"))
    midx = es3.enter_context(nc.sbuf_tensor("midx", [16, 64], U32, side="right"))
    midf = es3.enter_context(nc.sbuf_tensor("midf", [16, 64], F32, side="right"))
    idxT = es3.enter_context(nc.sbuf_tensor("idxT", [64, 16], F32, side="right"))
    idxTi = es3.enter_context(nc.sbuf_tensor("idxTi", [64, 16], I32, side="right"))
    psc = es3.enter_context(nc.psum_tensor("psc", [16, TTN], F32))
    pstx = es3.enter_context(nc.psum_tensor("pstx", [64, 16], F32))
    es3a = ExitStack()
    sc0 = es3a.enter_context(nc.sbuf_tensor("sc0", [16, T], F32, side="right"))
    sc1 = es3a.enter_context(nc.sbuf_tensor("sc1", [16, T], F32, side="right"))

    b_mm, b_ev = pMM.n, pEV2.n

    @block.tensor
    def _(t):
        for tt in range(NTT):
            if tt >= 1:
                t.wait_ge(p_ev2, b_ev + tt)
            for dc in range(NDC):
                mm = t.matmul(psc[:], Ut[:, dc * 16:(dc + 1) * 16],
                              xlt[:, dc * T + tt * TTN: dc * T + (tt + 1) * TTN],
                              start=(dc == 0), stop=(dc == NDC - 1))
            pMM.inc(mm)

    @block.vector
    def _(v):
        for tt in range(NTT):
            v.wait_ge(p_mm, b_mm + tt + 1)
            pEV2.inc(v.tensor_copy(sc0[:, tt * TTN:(tt + 1) * TTN], psc[:]))

    barrier()

    @block.vector
    def _(v):
        cur, nxt = sc0, sc1
        for r in range(8):
            v.wait_ge(v_d, vD.n)
            vD.inc(v.max(out=mval[:, r * 8:(r + 1) * 8], in_=cur[:]))
            v.wait_ge(v_d, vD.n)
            vD.inc(v.max_index(midx[:, r * 8:(r + 1) * 8], mval[:, r * 8:(r + 1) * 8], cur[:]))
            v.wait_ge(v_d, vD.n)
            vD.inc(v.match_replace(nxt[:], in_to_replace=mval[:, r * 8:(r + 1) * 8],
                                   in_values=cur[:], imm_value=NEG))
            cur, nxt = nxt, cur
        v.wait_ge(v_d, vD.n)
        vD.inc(v.tensor_copy(midf[:], midx[:]))

    qproj_emit(13)
    barrier()
    es3a.close()

    @block.tensor
    def _(t):
        tD.inc(t.transpose(pstx[:], midf[:], ident[0:16, 0:16]))

    @block.vector
    def _(v):
        v.wait_ge(t_d, tD.n)
        vD.inc(v.tensor_copy(idxT[:], pstx[:]))
        v.wait_ge(v_d, vD.n)
        vD.inc(v.tensor_copy(idxTi[:], idxT[:]))

    barrier()

    es4 = ExitStack()
    xcd = [es4.enter_context(nc.sbuf_tensor(f"xcd{i}", [64, 1024], F32, side="right")) for i in range(2)]
    xct = es4.enter_context(nc.sbuf_tensor("xct", [128, NDC * 1024], F32, side="right"))
    es4p = ExitStack()
    ptA2 = es4p.enter_context(nc.psum_tensor("ptA2", [128, 128], F32))
    ptB2 = es4p.enter_context(nc.psum_tensor("ptB2", [128, 128], F32))

    qproj_emit(6)
    b_mm, b_ev = pMM.n, pEV.n
    tgt_ld = [IG2[h % 2].n + 16 * (h // 2 + 1) for h in range(NH)]

    @block.gpsimd
    def _(g):
        for h in range(NH):
            if h >= 2:
                g.wait_ge(p_mm, b_mm + 8 * (h - 1))
            IG2[h % 2].inc(g.indirect_dma_start(
                out=xcd[h % 2][:], out_offset=None,
                in_=s_xln2[:, :],
                in_offset=IndirectOffsetOnAxis(idxTi[0:64, h:h + 1], axis=0)))

    @block.tensor
    def _(t):
        for h in range(NH):
            t.wait_ge(ig2[h % 2], tgt_ld[h])
            for dc in range(NDC):
                i = h * 8 + dc
                if i >= 2:
                    t.wait_ge(p_ev, b_ev + i - 1)
                pp = ptA2 if i % 2 == 0 else ptB2
                pMM.inc(t.transpose(pp[:, 0:64], xcd[h % 2][:, dc * 128:(dc + 1) * 128],
                                    ident[0:64, 0:64]))

    @block.scalar
    def _(s):
        for h in range(NH):
            for dc in range(NDC):
                i = h * 8 + dc
                s.wait_ge(p_mm, b_mm + i + 1)
                pp = ptA2 if i % 2 == 0 else ptB2
                pEV.inc(s.copy(xct[:, dc * 1024 + h * 64: dc * 1024 + (h + 1) * 64], pp[:, 0:64]))

    barrier()
    es4p.close()

    es5 = ExitStack()
    wkve = [es5.enter_context(nc.sbuf_tensor(f"wkve{i}", [128, 1024], F32, side="right")) for i in range(2)]
    pay = es5.enter_context(nc.sbuf_tensor("pays", [64, 128], F32, side="right"))
    pKV = [es5.enter_context(nc.psum_tensor(f"pKV{i}", [64, 128], F32)) for i in range(2)]

    b_mm, b_ev = pMM.n, pEV.n
    b_st0 = ST4[0].n
    tgt_kv = [LDU2[h % 2].n + 16 * (h // 2 + 1) for h in range(NH)]

    @block.sync
    def _(sy):
        for h in range(NH):
            if h >= 2:
                sy.wait_ge(p_mm, b_mm + h - 1)
            LDU2[h % 2].inc(sy.dma_start(
                out=wkve[h % 2].rearrange("p (dc c) -> p dc c", dc=8),
                in_=s_wkvT[:, h * 128:(h + 1) * 128].rearrange("(dc p) c -> p dc c", p=128)))

    @block.tensor
    def _(t):
        for h in range(NH):
            t.wait_ge(ldu2[h % 2], tgt_kv[h])
            if h >= 2:
                t.wait_ge(p_ev, b_ev + h - 1)
            for dc in range(NDC):
                mm = t.matmul(pKV[h % 2][:],
                              xct[:, dc * 1024 + h * 64: dc * 1024 + (h + 1) * 64],
                              wkve[h % 2][:, dc * 128:(dc + 1) * 128],
                              start=(dc == 0), stop=(dc == NDC - 1))
            pMM.inc(mm)

    @block.scalar
    def _(s):
        for h in range(NH):
            s.wait_ge(p_mm, b_mm + h + 1)
            if h >= 1:
                s.wait_ge(st4[0], b_st0 + 16 * h)
            pEV.inc(s.copy(pay[:], pKV[h % 2][:]))

    qproj_emit(6)

    @block.gpsimd
    def _(g):
        for h in range(NH):
            g.wait_ge(p_ev, b_ev + h + 1)
            ST4[0].inc(g.dma_start(out=ag_in[h * 64:(h + 1) * 64, :], in_=pay[:]))
        g.wait_ge(v_d, vD.n)
        qGP.inc(g.dma_start(out=ag_in[1024:1032, :].rearrange("a (b c) -> (a b) c", b=2),
                            in_=mval[:]))

    barrier()
    es5.close()
    es4.close()
    es3.close()

    es7 = ExitStack()
    hsel = es7.enter_context(nc.sbuf_tensor("hsel", [16, 2048], F32, side="right"))
    va0 = es7.enter_context(nc.sbuf_tensor("va0", [16, 512], F32, side="right"))
    va1 = es7.enter_context(nc.sbuf_tensor("va1", [16, 512], F32, side="right"))
    gval = es7.enter_context(nc.sbuf_tensor("gval", [16, 64], F32, side="right"))
    gposu = es7.enter_context(nc.sbuf_tensor("gposu", [16, 64], U32, side="right"))
    gposf = es7.enter_context(nc.sbuf_tensor("gposf", [16, 64], F32, side="right"))

    @block.gpsimd
    def _(g):
        gD.inc(g.collective_compute("AllGather", AL.bypass, replica_groups=RG,
                                    ins=[ag_in.opt()], outs=[ag_out.opt()]))

    qproj_emit(32)

    @block.sync
    def _(sy):
        sy.wait_ge(g_d, gD.n)
        qSY.inc(sy.dma_start(out=hsel[:], in_=c_hsel))
        for c in range(8):
            qSY.inc(sy.dma_start(
                out=va0[:, c * 64:(c + 1) * 64],
                in_=ag_out[c * 1032 + 1024: c * 1032 + 1032, :].rearrange("a (b c) -> (a b) c", b=2)))

    @block.vector
    def _(v):
        v.wait_ge(q_sy, qSY.n)
        cur, nxt = va0, va1
        for r in range(8):
            v.wait_ge(v_d, vD.n)
            vD.inc(v.max(out=gval[:, r * 8:(r + 1) * 8], in_=cur[:]))
            v.wait_ge(v_d, vD.n)
            vD.inc(v.max_index(gposu[:, r * 8:(r + 1) * 8], gval[:, r * 8:(r + 1) * 8], cur[:]))
            v.wait_ge(v_d, vD.n)
            vD.inc(v.match_replace(nxt[:], in_to_replace=gval[:, r * 8:(r + 1) * 8],
                                   in_values=cur[:], imm_value=NEG))
            cur, nxt = nxt, cur
        v.wait_ge(v_d, vD.n)
        vD.inc(v.tensor_copy(gposf[:], gposu[:]))

    barrier()
    es6.close()

    es7p = ExitStack()
    kva = xlt[:, 0:8192]
    e2 = xlt[:, 8192:12288]
    ppos = es7p.enter_context(nc.psum_tensor("ppos", [128, 1024], F32))
    pSELb = es7p.enter_context(nc.psum_tensor("pSELb", [128, 128], F32))
    pKTb = es7p.enter_context(nc.psum_tensor("pKTb", [128, 128], F32))
    pSEL = [pSELb[:, 0:64], pSELb[:, 64:128]]
    pKT = [pKTb[:, 0:64], pKTb[:, 64:128]]

    b_mm, b_e2 = pMM.n, pEV2.n

    @block.tensor
    def _(t):
        t.wait_ge(v_d, vD.n)
        for h in range(NH):
            mm = t.matmul(ppos[:, h * 64:(h + 1) * 64], hsel[:, h * 128:(h + 1) * 128],
                          gposf[:], start=True, stop=True, skip_group_check=True)
        pMM.inc(mm)

    @block.vector
    def _(v):
        v.wait_ge(p_mm, b_mm + 1)
        for h in range(NH):
            for q4 in range(4):
                vD.inc(v.tensor_scalar(e2[:, (h * 4 + q4) * 64:(h * 4 + q4 + 1) * 64],
                                       ppos[:, h * 64:(h + 1) * 64],
                                       iota4[:, q4:q4 + 1], None, AL.is_equal))

    @block.sync
    def _(sy):
        for c in range(8):
            dst = kva[(c % 2) * 64:(c % 2) * 64 + 64, :].rearrange(
                "p (h q k) -> p h q k", h=16, q=4)[:, :, c // 2, :]
            qSY.inc(sy.dma_start(
                out=dst,
                in_=ag_out[c * 1032: c * 1032 + 1024, :].rearrange("(h r) k -> r h k", h=16)))

    b_mm2 = pMM.n

    @block.tensor
    def _(t):
        t.wait_ge(v_d, vD.n)
        t.wait_ge(q_sy, qSY.n)
        for m in range(NH):
            hp, sub = divmod(m, 2)
            h = m
            pr = slice(64 * sub, 64 * sub + 64)
            if m >= 2:
                t.wait_ge(p_ev2, b_e2 + 2 * (m - 1))
            for q4 in range(4):
                t.matmul(pSEL[m % 2][pr, :], e2[:, (h * 4 + q4) * 64:(h * 4 + q4 + 1) * 64],
                         kva[:, (h * 4 + q4) * 128 + 64:(h * 4 + q4) * 128 + 128],
                         start=(q4 == 0), stop=(q4 == 3), skip_group_check=True)
                mm = t.matmul(pKT[m % 2][pr, :], kva[:, (h * 4 + q4) * 128:(h * 4 + q4) * 128 + 64],
                              e2[:, (h * 4 + q4) * 64:(h * 4 + q4 + 1) * 64],
                              start=(q4 == 0), stop=(q4 == 3), skip_group_check=True)
            pMM.inc(mm)

    @block.vector
    def _(v):
        for m in range(NH):
            hp, sub = divmod(m, 2)
            pr = slice(64 * sub, 64 * sub + 64)
            v.wait_ge(p_mm, b_mm2 + m + 1)
            pEV2.inc(v.tensor_copy(vselt[pr, hp * 64:(hp + 1) * 64], pSEL[m % 2][pr, :]))
            pEV2.inc(v.tensor_copy(kselT[pr, hp * 64:(hp + 1) * 64], pKT[m % 2][pr, :]))

    barrier()
    es7p.close()
    es7.close()
    es3.close()
    es_xlt.close()

    es_x2 = ExitStack()
    sb_x2 = es_x2.enter_context(nc.sbuf_tensor("sb_x2", [128, NTC * D], F32, side="right"))
    es8 = ExitStack()
    woTs = es8.enter_context(nc.sbuf_tensor("woTs", [128, NDC * 1024], F32))
    selrz = es8.enter_context(nc.sbuf_tensor("selrz", [16, 1024], F32))
    oT = [es8.enter_context(nc.sbuf_tensor(f"oTt{i}", [128, TTN], F32)) for i in range(8)]
    pb_ = [es8.enter_context(nc.sbuf_tensor(f"pbt{i}", [128, TTN], F32)) for i in range(2)]
    rzt = es8.enter_context(nc.sbuf_tensor("rztt", [16, TTN], F32))
    pS = [es8.enter_context(nc.psum_tensor(f"pSs{i}", [128, TTN], F32)) for i in range(2)]
    pO = [es8.enter_context(nc.psum_tensor(f"pOo{i}", [128, TTN], F32)) for i in range(2)]
    pZ = es8.enter_context(nc.psum_tensor("pZz", [16, TTN], F32))
    pY = [es8.enter_context(nc.psum_tensor(f"pYy{i}", [128, TTN], F32)) for i in range(2)]

    @block.sync
    def _(sy):
        qSY.inc(sy.dma_start(out=selrz[:], in_=c_selrz))
        for dc in range(NDC):
            qSY.inc(sy.dma_start(out=woTs[:, dc * 1024:(dc + 1) * 1024],
                                 in_=s_woT[dc * 128:(dc + 1) * 128, :]))

    @block.gpsimd
    def _(g):
        for tc in range(NTC):
            qGP.inc(g.dma_start(out=sb_x2[:, tc * D:(tc + 1) * D],
                                in_=s_xpark[tc * 128:(tc + 1) * 128, :]))

    barrier()

    for tt in range(NTT):
        b_mm, b_ev, b_q, b_q2, b_rd, b_td = pMM.n, pEV.n, pQ.n, pQ2.n, pRD.n, tD.n

        @block.tensor
        def _(t):
            for hp in range(8):
                t.wait_ge(p_q, max(0, b_q + hp - 1))
                t.matmul(pS[hp % 2][0:64, :], kselT[0:64, hp * 64:(hp + 1) * 64],
                         qT[0:64, hp * T + tt * TTN: hp * T + (tt + 1) * TTN],
                         start=True, stop=True, skip_group_check=True)
                pMM.inc(t.matmul(pS[hp % 2][64:128, :], kselT[64:128, hp * 64:(hp + 1) * 64],
                                 qT[64:128, hp * T + tt * TTN: hp * T + (tt + 1) * TTN],
                                 start=True, stop=True, skip_group_check=True))
                t.wait_ge(p_q, b_q + hp + 1)
                t.wait_ge(p_ev, max(0, b_ev + hp - 1))
                if hp == 0:
                    t.wait_ge(p_q2, b_q2)
                t.matmul(pO[hp % 2][0:64, :], vselt[0:64, hp * 64:(hp + 1) * 64],
                         pb_[hp % 2][0:64, :], start=True, stop=True, skip_group_check=True)
                t.matmul(pO[hp % 2][64:128, :], vselt[64:128, hp * 64:(hp + 1) * 64],
                         pb_[hp % 2][64:128, :], start=True, stop=True, skip_group_check=True)
                pMM.inc(t.matmul(pZ[:], selz[:, hp * 16:(hp + 1) * 16], pb_[hp % 2][:],
                                 start=(hp == 0), stop=(hp == 7), skip_group_check=True))

        @block.scalar
        def _(s):
            for hp in range(8):
                s.wait_ge(p_mm, b_mm + 2 * hp + 1)
                s.wait_ge(p_mm, max(0, b_mm + 2 * hp - 2))
                pQ.inc(s.activation(pb_[hp % 2][:], pS[hp % 2][:], AF.Exp,
                                    bias=zcol[:, 0:1], scale=attsc[:, 0:1]))
                if hp >= 1:
                    s.wait_ge(p_mm, b_mm + 2 * (hp - 1) + 2)
                    pEV.inc(s.copy(oT[hp - 1][:], pO[(hp - 1) % 2][:]))
            s.wait_ge(p_mm, b_mm + 16)
            pEV.inc(s.copy(oT[7][:], pO[7 % 2][:]))

        @block.vector
        def _(v):
            v.wait_ge(p_mm, b_mm + 16)
            pQ2.inc(v.reciprocal(rzt[:], pZ[:]))

        rot = [pS[0], pS[1], pO[0], pO[1]]

        @block.tensor
        def _(t):
            for hp in range(8):
                if hp == 0:
                    t.wait_ge(p_q2, b_q2 + 1)
                elif hp == 2:
                    t.wait_ge(p_ev, b_ev + 7)
                elif hp == 3:
                    t.wait_ge(p_ev, b_ev + 8)
                elif hp >= 4:
                    t.wait_ge(p_rd, b_rd + hp - 3)
                tD.inc(t.matmul(rot[hp % 4][:], selrz[:, hp * 128:(hp + 1) * 128], rzt[:],
                                start=True, stop=True))

        tD_base = tD.n - 8

        @block.vector
        def _(v):
            for hp in range(8):
                v.wait_ge(t_d, tD_base + hp + 1)
                v.wait_ge(p_ev, b_ev + hp + 1)
                pRD.inc(v.tensor_tensor(oT[hp][:], oT[hp][:], rot[hp % 4][:], AL.mult))

        b_mm2, b_evv = pMM.n, pEV2.n

        @block.tensor
        def _(t):
            t.wait_ge(p_rd, b_rd + 8)
            for gidx in range(8):
                tc4, ft = divmod(gidx, 2)
                t.wait_ge(p_ev2, max(0, b_evv + gidx - 1))
                for hp in range(8):
                    mm = t.matmul(pY[gidx % 2][:], oT[hp][:, tc4 * 128:(tc4 + 1) * 128],
                                  woTs[:, hp * 1024 + ft * TTN: hp * 1024 + (ft + 1) * TTN],
                                  start=(hp == 0), stop=(hp == 7))
                pMM.inc(mm)

        @block.vector
        def _(v):
            for gidx in range(8):
                tc4, ft = divmod(gidx, 2)
                v.wait_ge(p_mm, b_mm2 + gidx + 1)
                tcg = tt * 4 + tc4
                xs = sb_x2[:, tcg * D + ft * TTN: tcg * D + (ft + 1) * TTN]
                pEV2.inc(v.scalar_tensor_tensor(xs, pY[gidx % 2][:], 1.0, xs, AL.mult, AL.add))

        @block.scalar
        def _(s):
            for tc4 in range(4):
                tcg = tt * 4 + tc4
                xs2 = sb_x2[:, tcg * D:(tcg + 1) * D]
                s.wait_ge(p_ev2, b_evv + 2 * tc4 + 2)
                s.wait_ge(p_q, pQ.n)
                pQ.inc(s.activation(junk[:], xs2, AF.Identity, bias=zcol[:, 0:1], scale=1.0,
                                    accum_out=stat[:, tcg:tcg + 1]))
                s.wait_ge(p_q, pQ.n)
                pQ.inc(s.activation(junk[:], xs2, AF.Square, bias=zcol[:, 0:1], scale=1.0,
                                    accum_out=stat[:, 16 + tcg:16 + tcg + 1]))

        @block.vector
        def _(v):
            for tc4 in range(4):
                tcg = tt * 4 + tc4
                xs2 = sb_x2[:, tcg * D:(tcg + 1) * D]
                v.wait_ge(p_ev2, b_evv + 2 * tc4 + 2)
                vD.inc(v.reduce_max(stat[:, 96 + tcg:96 + tcg + 1], xs2, AX.X))
                vD.inc(v.tensor_reduce(stat[:, 112 + tcg:112 + tcg + 1], xs2, AX.X, AL.min))

    barrier()
    es8.close()
    es_q.close()

    if UPTO == "attn":
        finalize(sb_x2)
        es_x2.close(); es.close()
        return nc

    ffn("f2", sb_x2, skip_stats=True)

    finalize(sb_x2)
    es_x2.close()
    es.close()
    return nc


def _consts():
    import ml_dtypes
    ident = np.eye(128, dtype=np.float32)
    ones1 = np.ones((1, 128), np.float32)
    ones2b = np.ones((2, 128), ml_dtypes.bfloat16)
    iota4 = (np.arange(128, dtype=np.float32)[:, None] +
             128.0 * np.arange(4, dtype=np.float32)[None, :])
    selz = np.zeros((128, 128), np.float32)
    for hp in range(8):
        selz[0:64, hp * 16 + 2 * hp] = 1.0
        selz[64:128, hp * 16 + 2 * hp + 1] = 1.0
    selrz = np.zeros((16, 1024), np.float32)
    for hp in range(8):
        for p in range(128):
            selrz[2 * hp + p // 64, hp * 128 + p] = 1.0
    hselc = np.zeros((16, 2048), np.float32)
    for h in range(16):
        hselc[h, h * 128:(h + 1) * 128] = 1.0
    return ident, ones1, ones2b, iota4, selz, selrz, hselc


def kernel(**inputs):
    x = np.asarray(inputs["L0_attn_x"], np.float32).reshape(NCORES * T, D)
    ws = np.asarray(inputs["L0_attn_w_score"], np.float32)
    wsbc = np.zeros((128, 2), np.float32)
    wsbc[0:64, 0] = ws
    wsbc[64:128, 1] = ws
    ident, ones1, ones2b, iota4, selz, selrz, hselc = _consts()

    nc = _build()
    shared = {}
    for k in ("L0_attn_Wq", "L0_attn_Wk", "L0_attn_Wv", "L0_attn_Wo",
              "L0_ffn_w1", "L0_ffn_b1", "L0_ffn_w2", "L0_ffn_b2"):
        shared[k] = np.ascontiguousarray(np.asarray(inputs[k], np.float32))
    shared.update(c_ident=ident, c_ones1=ones1, c_ones2b=ones2b, c_iota4=iota4,
                  c_selz=selz, c_selrz=selrz, c_hsel=hselc, c_wsb=wsbc)
    in_maps = []
    for c in range(NCORES):
        m = dict(shared)
        m["L0_attn_x"] = np.ascontiguousarray(x[c * T:(c + 1) * T, :])
        in_maps.append(m)
    res = run_bass_kernel_spmd(nc, in_maps, core_ids=list(range(NCORES)))
    y = np.concatenate([np.asarray(r["y"], np.float32) for r in res.results], axis=0)
    return y.reshape(1, NCORES * T, D)


# version 20
# speedup vs baseline: 1.0277x; 1.0183x over previous
import numpy as np
import concourse.bass as bass
import concourse.mybir as mybir
from concourse.bass_utils import run_bass_kernel_spmd
from concourse.bass_core_rust import IndirectOffsetOnAxis
from contextlib import ExitStack

F32 = mybir.dt.float32
F16 = mybir.dt.float16
BF16 = mybir.dt.bfloat16
I32 = mybir.dt.int32
U32 = mybir.dt.uint32
AL = mybir.AluOpType
AF = mybir.ActivationFunctionType
AX = mybir.AxisListType

NCORES = 8
T = 2048
D = 1024
HID = 4096
NH = 16
HD = 64
NTC = T // 128
NDC = D // 128
NHC = HID // 128
TTN = 512
NTT = T // TTN
EPS = 1e-5
RB = 1536.0
NEG = -3.0e38

UPTO = None


class Ctr:
    def __init__(self, sem, step=1):
        self.sem, self.n, self.step = sem, 0, step

    def inc(self, instr, k=1):
        instr.then_inc(self.sem, k * self.step)
        self.n += k * self.step
        return self.n


def _build():
    nc = bass.Bass("TRN2", target_bir_lowering=False)

    d_x = nc.dram_tensor("L0_attn_x", [T, D], F32, kind="ExternalInput").ap()
    d_wq = nc.dram_tensor("L0_attn_Wq", [D, D], F32, kind="ExternalInput").ap()
    d_wk = nc.dram_tensor("L0_attn_Wk", [D, D], F32, kind="ExternalInput").ap()
    d_wv = nc.dram_tensor("L0_attn_Wv", [D, D], F32, kind="ExternalInput").ap()
    d_wo = nc.dram_tensor("L0_attn_Wo", [D, D], F32, kind="ExternalInput").ap()
    d_w1 = nc.dram_tensor("L0_ffn_w1", [HID, D], F32, kind="ExternalInput").ap()
    d_b1 = nc.dram_tensor("L0_ffn_b1", [HID], F32, kind="ExternalInput").ap()
    d_w2 = nc.dram_tensor("L0_ffn_w2", [D, HID], F32, kind="ExternalInput").ap()
    d_b2 = nc.dram_tensor("L0_ffn_b2", [D], F32, kind="ExternalInput").ap()
    c_ident = nc.dram_tensor("c_ident", [128, 128], F32, kind="ExternalInput").ap()
    c_ones1 = nc.dram_tensor("c_ones1", [1, 128], F32, kind="ExternalInput").ap()
    c_ones2b = nc.dram_tensor("c_ones2b", [2, 128], BF16, kind="ExternalInput").ap()
    c_iota4 = nc.dram_tensor("c_iota4", [128, 4], F32, kind="ExternalInput").ap()
    c_selz = nc.dram_tensor("c_selz", [128, 128], F32, kind="ExternalInput").ap()
    c_selrz = nc.dram_tensor("c_selrz", [16, 1024], F32, kind="ExternalInput").ap()
    c_hsel = nc.dram_tensor("c_hsel", [16, 2048], F32, kind="ExternalInput").ap()
    c_wsb = nc.dram_tensor("c_wsb", [128, 2], F32, kind="ExternalInput").ap()

    d_y = nc.dram_tensor("y", [T, D], F32, kind="ExternalOutput").ap()

    s_xq = nc.dram_tensor("s_xq", [T, D], BF16).ap()
    s_g = nc.dram_tensor("s_g", [HID, T], F32).ap()
    s_xln2 = nc.dram_tensor("s_xln2", [T, D], F32).ap()
    s_xpark = nc.dram_tensor("s_xpark", [T, D], F32).ap()
    s_w1q = nc.dram_tensor("s_w1q", [HID, D], BF16).ap()
    s_w2q = nc.dram_tensor("s_w2q", [D, HID], BF16).ap()
    s_wqT = nc.dram_tensor("s_wqT", [D, D], F32).ap()
    s_woT = nc.dram_tensor("s_woT", [D, D], F32).ap()
    s_wkvT = nc.dram_tensor("s_wkvT", [D, 2 * D], F32).ap()
    cc_in = nc.dram_tensor("cc_in", [1, 8], F32).ap()
    cc_out = nc.dram_tensor("cc_out", [1, 8], F32).ap()
    ag_in = nc.dram_tensor("ag_in", [1032, 128], F32).ap()
    ag_out = nc.dram_tensor("ag_out", [8 * 1032, 128], F32).ap()
    RG = [list(range(NCORES))]

    es = ExitStack()
    sbuf = lambda n, s, d=F32: es.enter_context(nc.sbuf_tensor(n, s, d))
    sem = lambda n: es.enter_context(nc.semaphore(n))

    ident = sbuf("ident", [128, 128])
    ones1 = sbuf("ones1", [1, 128])
    ones2b = sbuf("ones2b", [2, 128], BF16)
    iota4 = sbuf("iota4", [128, 4])
    selz = sbuf("selz", [128, 128])
    wsb = sbuf("wsb", [128, 2])
    Ut = sbuf("Ut", [128, 128])
    b1t = sbuf("b1t", [128, NHC])
    b1sc = sbuf("b1sc", [128, NHC])
    b2s = sbuf("b2s", [2, D], BF16)
    scal = sbuf("scal", [128, 16])
    one1 = sbuf("one1", [1, 16])
    cct = sbuf("cct", [1, 8])
    ccr = sbuf("ccr", [1, 8])
    stat = sbuf("stat", [128, 16 * 8])
    junk = es.enter_context(nc.sbuf_tensor("junkt", [128, 1024], F32, side="right"))
    zcol = sbuf("zcol", [128, 1])
    epst = sbuf("epst", [128, 1])
    rbt = sbuf("rbt", [128, 1])
    attsc = sbuf("attsc", [128, 1])
    kselT = sbuf("kselTt", [128, 8 * 64])
    vselt = sbuf("vselt", [128, 8 * 64])
    gmax = sbuf("gmaxt", [128, 1])
    gtmp = sbuf("gtmpt", [128, 1])
    lred = sbuf("lredt", [1, 128])

    PH = sem("PH")
    t_d, v_d, s_d, g_d = sem("t_d"), sem("v_d"), sem("s_d"), sem("g_d")
    q_sy, q_gp = sem("q_sy"), sem("q_gp")
    p_mm, p_ev, p_rd = sem("p_mm"), sem("p_ev"), sem("p_rd")
    p_ev2 = sem("p_ev2")
    p_q, p_q2 = sem("p_q"), sem("p_q2")
    st4 = [sem(f"st4_{i}") for i in range(4)]
    ld2 = [sem(f"ld2_{i}") for i in range(2)]
    ig2 = [sem(f"ig2_{i}") for i in range(2)]
    ldt2 = [sem(f"ldt2_{i}") for i in range(2)]
    ldu2 = [sem(f"ldu2_{i}") for i in range(2)]
    sts2 = [sem(f"sts2_{i}") for i in range(2)]

    ps_g = es.enter_context(nc.psum_tensor("ps_g", [128, 128], F32))

    block = es.enter_context(nc.Block())

    tD, vD, sD, gD = Ctr(t_d), Ctr(v_d), Ctr(s_d), Ctr(g_d)
    qSY, qGP = Ctr(q_sy, 16), Ctr(q_gp, 16)
    pMM, pEV, pRD = Ctr(p_mm), Ctr(p_ev), Ctr(p_rd)
    pEV2 = Ctr(p_ev2)
    ST4 = [Ctr(x, 16) for x in st4]
    LD2 = [Ctr(x, 16) for x in ld2]
    IG2 = [Ctr(x, 16) for x in ig2]
    LDT2 = [Ctr(x, 16) for x in ldt2]
    LDU2 = [Ctr(x, 16) for x in ldu2]
    STS2 = [Ctr(x, 16) for x in sts2]
    pQ, pQ2 = Ctr(p_q), Ctr(p_q2)

    phase = [0]

    def barrier():
        phase[0] += 1
        tgt = 5 * phase[0]

        @block.tensor
        def _(t):
            t.wait_ge(t_d, tD.n)
            t.wait_ge(p_mm, pMM.n)
            t.sem_inc(PH, 1); t.wait_ge(PH, tgt)

        @block.vector
        def _(v):
            v.wait_ge(v_d, vD.n)
            v.wait_ge(p_q2, pQ2.n)
            v.wait_ge(p_rd, pRD.n)
            v.wait_ge(p_ev2, pEV2.n)
            v.sem_inc(PH, 1); v.wait_ge(PH, tgt)

        @block.scalar
        def _(s):
            s.wait_ge(s_d, sD.n)
            s.wait_ge(p_q, pQ.n)
            s.wait_ge(p_ev, pEV.n)
            s.sem_inc(PH, 1); s.wait_ge(PH, tgt)

        @block.sync
        def _(sy):
            sy.wait_ge(q_sy, qSY.n)
            sy.sem_inc(PH, 1); sy.wait_ge(PH, tgt)

        @block.gpsimd
        def _(g):
            g.wait_ge(q_gp, qGP.n)
            g.wait_ge(g_d, gD.n)
            for k in range(4):
                g.wait_ge(st4[k], ST4[k].n)
            for k in range(2):
                g.wait_ge(ld2[k], LD2[k].n)
                g.wait_ge(ig2[k], IG2[k].n)
                g.wait_ge(ldt2[k], LDT2[k].n)
                g.wait_ge(ldu2[k], LDU2[k].n)
                g.wait_ge(sts2[k], STS2[k].n)
            g.sem_inc(PH, 1); g.wait_ge(PH, tgt)

    def bcast_cols(src_c0, n, dst_c0):
        @block.tensor
        def _(t):
            t.wait_ge(v_d, vD.n)
            tD.inc(t.matmul(ps_g[:, 0:n], ones1[:], one1[0:1, src_c0:src_c0 + n],
                            start=True, stop=True))

        @block.vector
        def _(v):
            v.wait_ge(t_d, tD.n)
            vD.inc(v.tensor_copy(scal[:, dst_c0:dst_c0 + n], ps_g[:, 0:n]))

    def allreduce_max(src_col, dst_col):
        @block.vector
        def _(v):
            vD.inc(v.memset(cct[:], 0.0))
            v.wait_ge(v_d, vD.n)
            vD.inc(v.tensor_copy(cct[0:1, 0:1], one1[0:1, src_col:src_col + 1]))

        @block.gpsimd
        def _(g):
            g.wait_ge(v_d, vD.n)
            qGP.inc(g.dma_start(out=cc_in, in_=cct[:]))
            g.wait_ge(q_gp, qGP.n)
            gD.inc(g.collective_compute("AllReduce", AL.max, replica_groups=RG,
                                        ins=[cc_in.opt()], outs=[cc_out.opt()]))
            g.wait_ge(g_d, gD.n)
            qGP.inc(g.dma_start(out=ccr[:], in_=cc_out))

        @block.vector
        def _(v):
            v.wait_ge(q_gp, qGP.n)
            vD.inc(v.tensor_copy(one1[0:1, dst_col:dst_col + 1], ccr[0:1, 0:1]))

    def cross_part_max(dst_one1_col):
        @block.tensor
        def _(t):
            t.wait_ge(v_d, vD.n)
            tD.inc(t.transpose(ps_g[0:1, 0:128], gmax[:], ident[:]))

        @block.vector
        def _(v):
            v.wait_ge(t_d, tD.n)
            vD.inc(v.tensor_copy(lred[:], ps_g[0:1, 0:128]))
            v.wait_ge(v_d, vD.n)
            vD.inc(v.reduce_max(one1[0:1, dst_one1_col:dst_one1_col + 1], lred[:], AX.X))

    es1 = ExitStack()
    sb_x = es1.enter_context(nc.sbuf_tensor("sb_x", [128, NTC * D], F32))

    @block.sync
    def _(sy):
        for src, dst in ((c_ident, ident), (c_ones1, ones1), (c_ones2b, ones2b),
                         (c_iota4, iota4), (c_selz, selz), (c_wsb, wsb)):
            qSY.inc(sy.dma_start(out=dst[:], in_=src))
        for tc in range(NTC):
            qSY.inc(sy.dma_start(out=sb_x[:, tc * D:(tc + 1) * D],
                                 in_=d_x[tc * 128:(tc + 1) * 128, :]))
        qSY.inc(sy.dma_start(out=b1t[:], in_=d_b1.rearrange("(a b) -> b a", b=128),
                             allow_slow_non_contiguous=True))

    @block.vector
    def _(v):
        vD.inc(v.memset(zcol[:], 0.0))
        vD.inc(v.memset(epst[:], EPS))
        vD.inc(v.memset(rbt[:], RB))
        vD.inc(v.memset(attsc[:], 0.125))
        vD.inc(v.memset(junk[:], 0.0))

    barrier()

    es0a = ExitStack()
    wbuf = [es0a.enter_context(nc.sbuf_tensor(f"wbuf{i}", [128, 4096], F32)) for i in range(2)]
    wmx8 = es0a.enter_context(nc.sbuf_tensor("wmx8", [128, 16], F32))
    wmx = es0a.enter_context(nc.sbuf_tensor("wmxq", [128, 2], F32))
    pt0 = es0a.enter_context(nc.psum_tensor("pt0", [128, 128], F32))
    pt1 = es0a.enter_context(nc.psum_tensor("pt1", [128, 128], F32))

    es0b = ExitStack()
    wbufB = [es0b.enter_context(nc.sbuf_tensor(f"wbufB{i}", [128, 4096], F32)) for i in range(2)]
    wzb = [es0b.enter_context(nc.sbuf_tensor(f"wzbq{i}", [128, 4096], F16)) for i in range(2)]
    wqb = [es0b.enter_context(nc.sbuf_tensor(f"wqbq{i}", [128, 4096], BF16)) for i in range(2)]
    WB4 = wbuf + wbufB
    LD4 = LD2 + LDU2
    ld4s = ld2 + ldu2

    def unit1(u, q=False):
        w1, w2 = (s_w1q, s_w2q) if q else (d_w1, d_w2)
        if u < 8:
            src = w1[u * 512:(u + 1) * 512, :]
        else:
            j = u - 8
            src = w2[(j % 2) * 512:(j % 2) * 512 + 512, (j // 2) * 1024:(j // 2) * 1024 + 1024]
        return src.rearrange("(r p) d -> p r d", p=128)

    esP1 = ExitStack()
    b2row = esP1.enter_context(nc.sbuf_tensor("b2row", [1, D], F32, side="right"))
    b2hi32 = esP1.enter_context(nc.sbuf_tensor("b2hi32", [1, D], F32, side="right"))
    b2lob = esP1.enter_context(nc.sbuf_tensor("b2lob", [1, D], BF16, side="right"))

    @block.sync
    def _(sy):
        qSY.inc(sy.dma_start(out=b2row[:], in_=d_b2[None, :]))

    def su_absmax():
        b_rd = pRD.n
        b_ld = [LD4[i].n for i in range(4)]

        @block.sync
        def _(sy):
            for u in range(16):
                if u >= 4:
                    sy.wait_ge(p_rd, b_rd + u - 3)
                LD4[u % 4].inc(sy.dma_start(out=WB4[u % 4].rearrange("p (r d) -> p r d", r=4),
                                            in_=unit1(u)))

        @block.vector
        def _(v):
            for u in range(16):
                v.wait_ge(ld4s[u % 4], b_ld[u % 4] + 16 * (u // 4 + 1))
                pRD.inc(v.reduce_max(wmx8[:, u:u + 1], WB4[u % 4][:], AX.X,
                                     apply_absolute_value=True))

    def su_scales():
        @block.vector
        def _(v):
            vD.inc(v.reduce_max(wmx[:, 0:1], wmx8[:, 0:8], AX.X))
            vD.inc(v.reduce_max(wmx[:, 1:2], wmx8[:, 8:16], AX.X))

        @block.tensor
        def _(t):
            t.wait_ge(v_d, vD.n)
            tD.inc(t.transpose(pt0[0:1, 0:128], wmx[:, 0:1], ident[:]))
            tD.inc(t.transpose(pt1[0:1, 0:128], wmx[:, 1:2], ident[:]))

        @block.vector
        def _(v):
            v.wait_ge(t_d, tD.n)
            vD.inc(v.tensor_copy(lred[:], pt0[0:1, 0:128]))
            v.wait_ge(v_d, vD.n)
            vD.inc(v.reduce_max(one1[0:1, 0:1], lred[:], AX.X))
            v.wait_ge(v_d, vD.n)
            vD.inc(v.tensor_copy(lred[:], pt1[0:1, 0:128]))
            v.wait_ge(v_d, vD.n)
            vD.inc(v.reduce_max(one1[0:1, 1:2], lred[:], AX.X))
            v.wait_ge(v_d, vD.n)
            vD.inc(v.tensor_scalar(one1[0:1, 4:5], one1[0:1, 0:1], 1e-6, 1.0 / 127.0, AL.max, AL.mult))
            vD.inc(v.tensor_scalar(one1[0:1, 5:6], one1[0:1, 1:2], 1e-6, 1.0 / 127.0, AL.max, AL.mult))
            v.wait_ge(v_d, vD.n)
            vD.inc(v.reciprocal(one1[0:1, 6:7], one1[0:1, 4:5]))
            vD.inc(v.reciprocal(one1[0:1, 7:8], one1[0:1, 5:6]))

        bcast_cols(6, 2, 0)

    def su_wquant(units, bufs, LDs, lds, zb, qb, rr, scalar_store=False):
        nb = len(bufs)
        nu = len(units)
        b_q, b_q2 = pQ.n, pQ2.n
        b_ld = [LDs[i].n for i in range(nb)]
        shape = (lambda a: a.rearrange("p (r d) -> p r d", r=rr)) if rr else (lambda a: a[:])
        STs = STS2 if scalar_store else ST4
        sts = sts2 if scalar_store else st4
        tgt_st = [STs[k % 2].n + 16 * (k // 2 + 1) for k in range(nu)]
        b_st = [STs[0].n, STs[1].n]
        b_st4 = [ST4[0].n, ST4[1].n]

        @block.sync
        def _(sy):
            for k in range(nu):
                if k >= nb:
                    sy.wait_ge(p_q, b_q + k - nb + 1)
                LDs[k % nb].inc(sy.dma_start(out=shape(bufs[k % nb]), in_=units[k][0]))

        @block.scalar
        def _(s):
            for k in range(nu):
                s.wait_ge(lds[k % nb], b_ld[k % nb] + 16 * (k // nb + 1))
                s.wait_ge(p_q2, b_q2 + k - 1 if k >= 2 else b_q2)
                pQ.inc(s.activation(zb[k % 2][:], bufs[k % nb][:], AF.Identity,
                                    bias=rbt[:, 0:1], scale=scal[:, units[k][2]:units[k][2] + 1]))
                if scalar_store and k >= 1:
                    s.wait_ge(p_q2, b_q2 + k)
                    STs[(k - 1) % 2].inc(s.dma_start(out=units[k - 1][1], in_=shape(qb[(k - 1) % 2])))
            if scalar_store:
                s.wait_ge(p_q2, b_q2 + nu)
                STs[(nu - 1) % 2].inc(s.dma_start(out=units[nu - 1][1], in_=shape(qb[(nu - 1) % 2])))

        @block.vector
        def _(v):
            for k in range(nu):
                v.wait_ge(p_q, b_q + k + 1)
                v.wait_ge(sts[k % 2], tgt_st[k - 2] if k >= 2 else b_st[k % 2])
                if k < 2 and scalar_store:
                    v.wait_ge(st4[k % 2], b_st4[k % 2])
                pQ2.inc(v.tensor_scalar(qb[k % 2][:], zb[k % 2][:], -RB, None, AL.add))

        if not scalar_store:
            @block.gpsimd
            def _(g):
                for k in range(nu):
                    g.wait_ge(p_q2, b_q2 + k + 1)
                    STs[k % 2].inc(g.dma_start(out=units[k][1], in_=shape(qb[k % 2])))

    def su_wquant1():
        su_wquant([(unit1(u), unit1(u, True), 0) for u in range(8)],
                  WB4, LD4, ld4s, wzb, wqb, 4)

    def close_es0b():
        es0b.close()
        es0a.close()
        esP1.close()

    es0c = ExitStack()

    def qtr2(u, qq, q=False):
        w2 = s_w2q if q else d_w2
        j = u - 8
        rr = (j % 2) * 512 + qq * 128
        cb = (j // 2) * 1024
        return w2[rr:rr + 128, cb:cb + 1024]

    def su_wquant2():
        wbuf2 = [es0c.enter_context(nc.sbuf_tensor(f"wbc{i}", [128, 4096], F32)) for i in range(2)]
        wzb2 = [es0c.enter_context(nc.sbuf_tensor(f"wzc{i}", [128, 4096], F16)) for i in range(2)]
        wqb2 = [es0c.enter_context(nc.sbuf_tensor(f"wqc{i}", [128, 4096], BF16)) for i in range(2)]
        su_wquant([(unit1(u), unit1(u, True), 1) for u in range(8, 16)],
                  wbuf2, LD4[0:2], ld4s[0:2], wzb2, wqb2, 4)

    def close_es0():
        es0c.close()

    def mk_trans():
        es_tr = ExitStack()
        wrow = [es_tr.enter_context(nc.sbuf_tensor(f"wrow{i}", [128, 1024], F32)) for i in range(2)]
        tev = [es_tr.enter_context(nc.sbuf_tensor(f"tev{i}", [128, 1024], F32)) for i in range(2)]
        pt2 = [es_tr.enter_context(nc.psum_tensor(f"pt2{i}", [128, 128], F32)) for i in range(2)]
        pU = es_tr.enter_context(nc.psum_tensor("pUq", [128, 4], F32))
        jobs = [(w, ec) for w in (2, 3) for ec in range(8)]
        wsrc = {0: d_wq, 1: d_wo, 2: d_wk, 3: d_wv}
        EV0, RD0, TD0 = pEV.n, pRD.n, tD.n
        tgt_ld = [LDT2[j % 2].n + 16 * (j // 2 + 1) for j in range(16)]
        acc = [ST4[2].n, ST4[3].n]
        st_after = {}
        for j, (w, ec) in enumerate(jobs):
            acc[j % 2] += 16 * (1 if w < 2 else 2)
            st_after[j] = acc[j % 2]
        td_tp, td_umm = {}, {}
        cnt = TD0
        for j, (w, ec) in enumerate(jobs):
            for dc in range(8):
                cnt += 1
                td_tp[(j, dc)] = cnt
            if w == 2:
                for dc in range(8):
                    cnt += 1
                    td_umm[(j, dc)] = cnt

        def emit_chunk(tt, b_ev2=None):
            js = list(range(tt * 4, tt * 4 + 4))

            @block.sync
            def _(sy):
                for j in js:
                    if j >= 2:
                        sy.wait_ge(p_ev, EV0 + 8 * (j - 1))
                        if jobs[j - 2][0] == 2:
                            sy.wait_ge(t_d, td_umm[(j - 2, 7)])
                    LDT2[j % 2].inc(sy.dma_start(
                        out=wrow[j % 2][:],
                        in_=wsrc[jobs[j][0]][jobs[j][1] * 128:(jobs[j][1] + 1) * 128, :]))

            @block.tensor
            def _(t):
                for j in js:
                    w, ec = jobs[j]
                    t.wait_ge(ldt2[j % 2], tgt_ld[j])
                    for dc in range(8):
                        gi = j * 8 + dc
                        if gi >= 2:
                            t.wait_ge(p_ev, EV0 + gi - 1)
                        tD.inc(t.transpose(pt2[gi % 2][:, 0:128],
                                           wrow[j % 2][:, dc * 128:(dc + 1) * 128], ident[:]))
                    if w == 2:
                        for dc in range(8):
                            ku = j * 8 + dc
                            if ku >= 2:
                                t.wait_ge(p_rd, RD0 + ku - 1)
                            tD.inc(t.matmul(pU[:, (ku % 2) * 2:(ku % 2) * 2 + 2],
                                            wrow[j % 2][:, dc * 128:(dc + 1) * 128],
                                            wsb[:], start=True, stop=True))

            @block.scalar
            def _(s):
                for j in js:
                    if j >= 2:
                        s.wait_ge(st4[2 + (j % 2)], st_after[j - 2])
                    for dc in range(8):
                        gi = j * 8 + dc
                        s.wait_ge(t_d, td_tp[(j, dc)])
                        pEV.inc(s.copy(tev[j % 2][:, dc * 128:(dc + 1) * 128], pt2[gi % 2][:, 0:128]))

            @block.vector
            def _(v):
                for j in js:
                    w, ec = jobs[j]
                    if w != 2:
                        continue
                    for dc in range(8):
                        ku = j * 8 + dc
                        v.wait_ge(t_d, td_umm[(j, dc)])
                        pRD.inc(v.tensor_copy(Ut[:, dc * 16 + 2 * ec: dc * 16 + 2 * ec + 2],
                                              pU[:, (ku % 2) * 2:(ku % 2) * 2 + 2]))

            @block.gpsimd
            def _(g):
                for j in js:
                    w, ec = jobs[j]
                    g.wait_ge(p_ev, EV0 + 8 * (j + 1))
                    src3 = tev[j % 2].rearrange("p (dc c) -> p dc c", dc=8)
                    SS = ST4[2 + (j % 2)]
                    if w == 0:
                        SS.inc(g.dma_start(
                            out=s_wqT[:, ec * 128:(ec + 1) * 128].rearrange("(dc p) c -> p dc c", p=128),
                            in_=src3))
                    elif w == 1:
                        SS.inc(g.dma_start(
                            out=s_woT[:, ec * 128:(ec + 1) * 128].rearrange("(dc p) c -> p dc c", p=128),
                            in_=src3))
                    else:
                        off = 0 if w == 2 else 64
                        srcA = tev[j % 2].rearrange("p (dc h c) -> p dc h c", dc=8, h=2)
                        SS.inc(g.dma_start(
                            out=s_wkvT[:, (2 * ec) * 128 + off:(2 * ec) * 128 + off + 64]
                                .rearrange("(dc p) c -> p dc c", p=128),
                            in_=srcA[:, :, 0, :]))
                        SS.inc(g.dma_start(
                            out=s_wkvT[:, (2 * ec + 1) * 128 + off:(2 * ec + 1) * 128 + off + 64]
                                .rearrange("(dc p) c -> p dc c", p=128),
                            in_=srcA[:, :, 1, :]))

        return emit_chunk, es_tr.close

    def su_b2split():

        @block.vector
        def _(v):
            vD.inc(v.tensor_copy(b2s[0:1, :], b2row[:]))
            v.wait_ge(v_d, vD.n)
            vD.inc(v.tensor_copy(b2hi32[:], b2s[0:1, :]))
            v.wait_ge(v_d, vD.n)
            vD.inc(v.tensor_tensor(b2lob[:], b2row[:], b2hi32[:], AL.subtract))

        @block.gpsimd
        def _(g):
            g.wait_ge(v_d, vD.n)
            qGP.inc(g.dma_start(out=b2s[1:2, :], in_=b2lob[:]))

    def ln_stats(xbuf, with_amax, co_stats=(), co_math=(), skip_stats=False):
        if not skip_stats:
            @block.scalar
            def _(s):
                for tc in range(NTC):
                    xs = xbuf[:, tc * D:(tc + 1) * D]
                    s.wait_ge(p_q, pQ.n)
                    pQ.inc(s.activation(junk[:], xs, AF.Identity, bias=zcol[:, 0:1], scale=1.0,
                                        accum_out=stat[:, 0 + tc:0 + tc + 1]))
                    s.wait_ge(p_q, pQ.n)
                    pQ.inc(s.activation(junk[:], xs, AF.Square, bias=zcol[:, 0:1], scale=1.0,
                                        accum_out=stat[:, 16 + tc:16 + tc + 1]))

        if with_amax and not skip_stats:
            @block.vector
            def _(v):
                for tc in range(NTC):
                    xs = xbuf[:, tc * D:(tc + 1) * D]
                    vD.inc(v.reduce_max(stat[:, 96 + tc:96 + tc + 1], xs, AX.X))
                    vD.inc(v.tensor_reduce(stat[:, 112 + tc:112 + tc + 1], xs, AX.X, AL.min))

        for f in co_stats:
            f()
        if (not skip_stats) or co_stats:
            barrier()

        @block.vector
        def _(v):
            MN, VAR, T4 = stat[:, 32:48], stat[:, 16:32], stat[:, 64:80]
            vD.inc(v.tensor_scalar(MN, stat[:, 0:16], 1.0 / D, None, AL.mult))
            vD.inc(v.tensor_scalar(VAR, VAR, 1.0 / D, None, AL.mult))
            v.wait_ge(v_d, vD.n)
            vD.inc(v.tensor_tensor(T4, MN, MN, AL.mult))
            v.wait_ge(v_d, vD.n)
            vD.inc(v.tensor_tensor(VAR, VAR, T4, AL.subtract))

        @block.scalar
        def _(s):
            s.wait_ge(v_d, vD.n)
            pQ.inc(s.activation(stat[:, 64:80], stat[:, 16:32], AF.Sqrt,
                                bias=epst[:, 0:1], scale=1.0))

        @block.vector
        def _(v):
            v.wait_ge(p_q, pQ.n)
            vD.inc(v.reciprocal(stat[:, 48:64], stat[:, 64:80]))

        if with_amax:
            @block.vector
            def _(v):
                MN, RS, T6, T7 = stat[:, 32:48], stat[:, 48:64], stat[:, 96:112], stat[:, 112:128]
                v.wait_ge(v_d, vD.n)
                vD.inc(v.tensor_tensor(T6, T6, MN, AL.subtract))
                vD.inc(v.tensor_tensor(T7, MN, T7, AL.subtract))
                v.wait_ge(v_d, vD.n)
                vD.inc(v.tensor_tensor(T6, T6, T7, AL.max))
                v.wait_ge(v_d, vD.n)
                vD.inc(v.tensor_tensor(T6, T6, RS, AL.mult))
                v.wait_ge(v_d, vD.n)
                vD.inc(v.reduce_max(gmax[:], T6, AX.X))

        for f in co_math:
            f()
        barrier()

    def ffn(tag, xbuf, co_stats=(), co_math=(), co_ar=(), post_ar=None,
            co_qx=(), post_qx=None, co_sg=(), post_sg=None, mk_tt=None,
            skip_stats=False, early_w1=False):
        ln_stats(xbuf, with_amax=True, co_stats=co_stats, co_math=co_math,
                 skip_stats=skip_stats)
        esA = ExitStack()
        if early_w1:
            xqT = esA.enter_context(nc.sbuf_tensor(f"{tag}xqT", [128, NDC * T], BF16))
            w1qT = esA.enter_context(nc.sbuf_tensor(f"{tag}w1qT", [128, NDC * HID], BF16))
            gb = [esA.enter_context(nc.sbuf_tensor(f"{tag}gb{i}", [128, TTN], F32)) for i in range(4)]
            pA = [esA.enter_context(nc.psum_tensor(f"{tag}pA{i}", [128, TTN], F32)) for i in range(2)]

            @block.sync
            def _(sy):
                for dc in range(NDC):
                    qSY.inc(sy.dma_start(out=w1qT[:, dc * HID:(dc + 1) * HID],
                                         in_=s_w1q[:, dc * 128:(dc + 1) * 128], transpose=True))
        cross_part_max(8)
        allreduce_max(8, 9)
        for f in co_ar:
            f()

        @block.vector
        def _(v):
            vD.inc(v.tensor_scalar(one1[0:1, 12:13], one1[0:1, 9:10], 1e-6, 1.0 / 127.0,
                                   AL.max, AL.mult))
            v.wait_ge(v_d, vD.n)
            vD.inc(v.reciprocal(one1[0:1, 13:14], one1[0:1, 12:13]))
            vD.inc(v.tensor_tensor(one1[0:1, 14:15], one1[0:1, 12:13], one1[0:1, 4:5], AL.mult))

        bcast_cols(13, 2, 2)

        @block.vector
        def _(v):
            A, B = stat[:, 64:80], stat[:, 80:96]
            vD.inc(v.tensor_scalar(A, stat[:, 48:64], scal[:, 2:3], None, AL.mult))
            v.wait_ge(v_d, vD.n)
            vD.inc(v.tensor_tensor(B, stat[:, 32:48], A, AL.mult))
            v.wait_ge(v_d, vD.n)
            vD.inc(v.tensor_scalar(B, B, -1.0, RB, AL.mult, AL.add))
            vD.inc(v.tensor_scalar(b1sc[:], b1t[:], scal[:, 3:4], None, AL.mult))

        barrier()
        if post_ar:
            post_ar()

        if not early_w1:
            xqT = esA.enter_context(nc.sbuf_tensor(f"{tag}xqT", [128, NDC * T], BF16))
            w1qT = esA.enter_context(nc.sbuf_tensor(f"{tag}w1qT", [128, NDC * HID], BF16))
            gb = [esA.enter_context(nc.sbuf_tensor(f"{tag}gb{i}", [128, TTN], F32)) for i in range(4)]
            pA = [esA.enter_context(nc.psum_tensor(f"{tag}pA{i}", [128, TTN], F32)) for i in range(2)]
        W1T = (ST4[0].n, ST4[1].n)
        esq = ExitStack()
        zq = [esq.enter_context(nc.sbuf_tensor(f"{tag}zq{i}", [128, 1024], F16, side="right")) for i in range(2)]
        xqb = [esq.enter_context(nc.sbuf_tensor(f"{tag}xqb{i}", [128, 1024], BF16, side="right")) for i in range(2)]
        b_q, b_q2 = pQ.n, pQ2.n
        tgt_st = [ST4[tc % 2].n + 16 * (tc // 2 + 1) for tc in range(NTC)]

        @block.scalar
        def _(s):
            for tc in range(NTC):
                if tc >= 2:
                    s.wait_ge(p_q2, b_q2 + tc - 1)
                pQ.inc(s.activation(zq[tc % 2][:], xbuf[:, tc * D:(tc + 1) * D], AF.Identity,
                                    bias=stat[:, 80 + tc:80 + tc + 1],
                                    scale=stat[:, 64 + tc:64 + tc + 1]))

        @block.vector
        def _(v):
            for tc in range(NTC):
                v.wait_ge(p_q, b_q + tc + 1)
                if tc >= 2:
                    v.wait_ge(st4[tc % 2], tgt_st[tc - 2])
                pQ2.inc(v.tensor_scalar(xqb[tc % 2][:], zq[tc % 2][:], -RB, None, AL.add))

        @block.gpsimd
        def _(g):
            for tc in range(NTC):
                g.wait_ge(p_q2, b_q2 + tc + 1)
                ST4[tc % 2].inc(g.dma_start(out=s_xq[tc * 128:(tc + 1) * 128, :], in_=xqb[tc % 2][:]))

        @block.sync
        def _(sy):
            if not early_w1:
                sy.wait_ge(st4[0], W1T[0])
                sy.wait_ge(st4[1], W1T[1])
                for dc in range(NDC):
                    qSY.inc(sy.dma_start(out=w1qT[:, dc * HID:(dc + 1) * HID],
                                         in_=s_w1q[:, dc * 128:(dc + 1) * 128], transpose=True))
            sy.wait_ge(st4[0], tgt_st[6])
            sy.wait_ge(st4[1], tgt_st[7])
            for dc in range(NDC):
                qSY.inc(sy.dma_start(out=xqT[:, dc * T: dc * T + 1024],
                                     in_=s_xq[0:1024, dc * 128:(dc + 1) * 128], transpose=True))
            sy.wait_ge(st4[0], ST4[0].n)
            sy.wait_ge(st4[1], ST4[1].n)
            for dc in range(NDC):
                qSY.inc(sy.dma_start(out=xqT[:, dc * T + 1024: dc * T + T],
                                     in_=s_xq[1024:2048, dc * 128:(dc + 1) * 128], transpose=True))

        for f in co_qx:
            f(zq, xqb)
        barrier()
        if post_qx:
            post_qx()
        esq.close()


        @block.vector
        def _(v):
            vD.inc(v.memset(gmax[:], 0.0))
        b_vd = vD.n

        b_mm, b_ev, b_rd = pMM.n, pEV.n, pRD.n
        ntile = NTT * NHC
        tgt_st = [ST4[i % 4].n + 16 * (i // 4 + 1) for i in range(ntile)]

        @block.tensor
        def _(t):
            for i in range(ntile):
                tt, fc = divmod(i, NHC)
                if i >= 2:
                    t.wait_ge(p_ev, b_ev + i - 1)
                for dc in range(NDC):
                    mm = t.matmul(pA[i % 2][:],
                                  w1qT[:, dc * HID + fc * 128: dc * HID + (fc + 1) * 128],
                                  xqT[:, dc * T + tt * TTN: dc * T + (tt + 1) * TTN],
                                  start=(dc == 0), stop=(dc == NDC - 1))
                pMM.inc(mm)

        @block.scalar
        def _(s):
            for i in range(ntile):
                tt, fc = divmod(i, NHC)
                s.wait_ge(p_mm, b_mm + i + 1)
                if i >= 4:
                    s.wait_ge(p_rd, b_rd + (i - 3))
                    s.wait_ge(st4[i % 4], tgt_st[i - 4])
                pEV.inc(s.activation(gb[i % 4][:], pA[i % 2][:], AF.Gelu,
                                     bias=b1sc[:, fc:fc + 1], scale=scal[:, 3:4]))

        @block.vector
        def _(v):
            for i in range(ntile):
                v.wait_ge(p_ev, b_ev + i + 1)
                if i == 0:
                    v.wait_ge(v_d, b_vd)
                else:
                    v.wait_ge(p_rd, b_rd + i)
                vD.inc(v.reduce_max(gtmp[:], gb[i % 4][:], AX.X, apply_absolute_value=True))
                v.wait_ge(v_d, vD.n)
                pRD.inc(v.tensor_tensor(gmax[:], gmax[:], gtmp[:], AL.max))

        @block.gpsimd
        def _(g):
            for i in range(ntile):
                tt, fc = divmod(i, NHC)
                g.wait_ge(p_ev, b_ev + i + 1)
                ST4[i % 4].inc(g.dma_start(out=s_g[fc * 128:(fc + 1) * 128, tt * TTN:(tt + 1) * TTN],
                                           in_=gb[i % 4][:]))

        barrier()
        esA.close()

        es_w2 = ExitStack()
        w2qT = es_w2.enter_context(nc.sbuf_tensor(f"{tag}w2qT", [128, NHC * D], BF16))
        cross_part_max(10)
        allreduce_max(10, 11)
        b2st = (ST4[0].n, ST4[1].n)
        for f in co_sg:
            f()
        w2busy = ST4[0].n != b2st[0]

        @block.sync
        def _(sy):
            for hc in range(NHC):
                if w2busy and hc % 8 == 0:
                    cb = hc // 8
                    sy.wait_ge(st4[0], b2st[0] + 16 * (cb + 1))
                    sy.wait_ge(st4[1], b2st[1] + 16 * (cb + 1))
                qSY.inc(sy.dma_start(out=w2qT[:, hc * D:(hc + 1) * D],
                                     in_=s_w2q[:, hc * 128:(hc + 1) * 128], transpose=True))

        @block.vector
        def _(v):
            vD.inc(v.tensor_scalar(one1[0:1, 12:13], one1[0:1, 11:12], 1e-6, 1.0 / 127.0,
                                   AL.max, AL.mult))
            v.wait_ge(v_d, vD.n)
            vD.inc(v.reciprocal(one1[0:1, 13:14], one1[0:1, 12:13]))
            vD.inc(v.tensor_tensor(one1[0:1, 14:15], one1[0:1, 12:13], one1[0:1, 5:6], AL.mult))
            v.wait_ge(v_d, vD.n)
            vD.inc(v.tensor_scalar(one1[0:1, 14:15], one1[0:1, 14:15], 0.5, None, AL.mult))

        bcast_cols(13, 2, 4)
        barrier()
        if post_sg:
            post_sg()

        esB = ExitStack()
        gq = esB.enter_context(nc.sbuf_tensor(f"{tag}gq", [128, NHC * TTN], BF16))
        gl = [esB.enter_context(nc.sbuf_tensor(f"{tag}gl{i}", [128, TTN], F32)) for i in range(2)]
        zg = [esB.enter_context(nc.sbuf_tensor(f"{tag}zg{i}", [128, TTN], F16)) for i in range(2)]
        pB = [esB.enter_context(nc.psum_tensor(f"{tag}pB{i}", [128, TTN], F32)) for i in range(4)]

        tt_emit, tt_close = mk_tt() if mk_tt else (None, None)
        ev_hist = {}
        for tt in range(NTT):
            b_q, b_q2, b_mm, b_ev = pQ.n, pQ2.n, pMM.n, pEV2.n
            tgt_ld = [LD2[hc % 2].n + 16 * (hc // 2 + 1) for hc in range(NHC)]

            @block.sync
            def _(sy):
                for hc in range(NHC):
                    sy.wait_ge(p_q, max(0, b_q + hc - 1))
                    LD2[hc % 2].inc(sy.dma_start(out=gl[hc % 2][:],
                                                 in_=s_g[hc * 128:(hc + 1) * 128, tt * TTN:(tt + 1) * TTN]))

            @block.scalar
            def _(s):
                for hc in range(NHC):
                    s.wait_ge(ld2[hc % 2], tgt_ld[hc])
                    s.wait_ge(p_q2, max(0, b_q2 + hc - 1))
                    pQ.inc(s.activation(zg[hc % 2][:], gl[hc % 2][:], AF.Identity,
                                        bias=rbt[:, 0:1], scale=scal[:, 4:5]))

            @block.vector
            def _(v):
                for hc in range(NHC):
                    v.wait_ge(p_q, b_q + hc + 1)
                    v.wait_ge(p_mm, max(0, b_mm - (NHC + 4) + hc + 1))
                    pQ2.inc(v.tensor_scalar(gq[:, hc * TTN:(hc + 1) * TTN], zg[hc % 2][:],
                                            -RB, None, AL.add))

            @block.tensor
            def _(t):
                for ft in range(2):
                    for hc in range(NHC):
                        t.wait_ge(p_q2, b_q2 + hc + 1)
                        if hc == 0:
                            t.wait_ge(p_ev2, max(0, b_ev + 4 * ft))
                        for tc4 in range(4):
                            mm = t.matmul(pB[tc4][:],
                                          gq[:, hc * TTN + tc4 * 128: hc * TTN + (tc4 + 1) * 128],
                                          w2qT[:, hc * D + ft * TTN: hc * D + (ft + 1) * TTN],
                                          start=(hc == 0), stop=False)
                        pMM.inc(mm)
                    for tc4 in range(4):
                        pMM.inc(t.matmul(pB[tc4][:], ones2b[:],
                                         b2s[0:2, ft * TTN:(ft + 1) * TTN], start=False, stop=True))

            @block.vector
            def _(v):
                for ft in range(2):
                    for tc4 in range(4):
                        v.wait_ge(p_mm, b_mm + (NHC + 4) * (ft + 1))
                        tcg = tt * 4 + tc4
                        xs = xbuf[:, tcg * D + ft * TTN: tcg * D + (ft + 1) * TTN]
                        pEV2.inc(v.scalar_tensor_tensor(xs, pB[tc4][:], scal[:, 5:6],
                                                        xs, AL.mult, AL.add))

            if tt_emit and tt >= 1:
                tt_emit(tt - 1, ev_hist[tt - 1])
            if tt_emit:
                ev_hist[tt] = b_ev

        if tt_emit:
            tt_emit(NTT - 1, ev_hist[NTT - 1])
        barrier()
        if tt_close:
            tt_close()
        esB.close()
        es_w2.close()

    def finalize(xbuf):
        @block.gpsimd
        def _(g):
            for tc in range(NTC):
                qGP.inc(g.dma_start(out=d_y[tc * 128:(tc + 1) * 128, :],
                                    in_=xbuf[:, tc * D:(tc + 1) * D]))
        barrier()

    def ln2_stats_tt(tt, b_ev2):
        @block.scalar
        def _(s):
            for tc4 in range(4):
                tcg = tt * 4 + tc4
                xs = sb_x[:, tcg * D:(tcg + 1) * D]
                s.wait_ge(p_ev2, b_ev2 + 4 + tc4 + 1)
                s.wait_ge(p_q, pQ.n)
                pQ.inc(s.activation(junk[:], xs, AF.Identity, bias=zcol[:, 0:1], scale=1.0,
                                    accum_out=stat[:, tcg:tcg + 1]))
                s.wait_ge(p_q, pQ.n)
                pQ.inc(s.activation(junk[:], xs, AF.Square, bias=zcol[:, 0:1], scale=1.0,
                                    accum_out=stat[:, 16 + tcg:16 + tcg + 1]))

    def mk_trans_stats():
        emit_chunk, close = mk_trans()

        def emit(tt, b_ev2):
            emit_chunk(tt, b_ev2)
            ln2_stats_tt(tt, b_ev2)
        return emit, close

    ffn("f1", sb_x, co_stats=[su_absmax], co_math=[su_scales, su_b2split],
        co_ar=[su_wquant1], post_ar=close_es0b,
        co_sg=[su_wquant2], post_sg=close_es0, mk_tt=mk_trans_stats)

    if UPTO == "ffn1":
        finalize(sb_x)
        es1.close(); es.close()
        return nc

    ln_stats(sb_x, with_amax=False, skip_stats=True)

    es_xlt = ExitStack()
    xlt = es_xlt.enter_context(nc.sbuf_tensor("xlt", [128, NDC * T], F32, side="right"))
    es2 = ExitStack()
    xl2 = [es2.enter_context(nc.sbuf_tensor(f"xl2_{i}", [128, 1024], F32)) for i in range(2)]
    ptA = es2.enter_context(nc.psum_tensor("ptA", [128, 128], F32))
    ptB = es2.enter_context(nc.psum_tensor("ptB", [128, 128], F32))

    b_q, b_mm, b_ev = pQ.n, pMM.n, pEV.n
    tgt_st = [ST4[tc % 2].n + 16 * (tc // 2 + 1) for tc in range(NTC)]

    @block.vector
    def _(v):
        for tc in range(NTC):
            if tc >= 2:
                v.wait_ge(p_mm, b_mm + 8 * (tc - 1))
                v.wait_ge(st4[tc % 2], tgt_st[tc - 2])
            pQ.inc(v.tensor_scalar(xl2[tc % 2][:], sb_x[:, tc * D:(tc + 1) * D],
                                   stat[:, 32 + tc:32 + tc + 1], stat[:, 48 + tc:48 + tc + 1],
                                   AL.subtract, AL.mult))

    @block.gpsimd
    def _(g):
        for tc in range(NTC):
            g.wait_ge(p_q, b_q + tc + 1)
            ST4[tc % 2].inc(g.dma_start(out=s_xln2[tc * 128:(tc + 1) * 128, :], in_=xl2[tc % 2][:]))

    @block.tensor
    def _(t):
        for tc in range(NTC):
            t.wait_ge(p_q, b_q + tc + 1)
            for dc in range(NDC):
                i = tc * 8 + dc
                if i >= 2:
                    t.wait_ge(p_ev, b_ev + i - 1)
                pp = ptA if i % 2 == 0 else ptB
                pMM.inc(t.transpose(pp[:, 0:128], xl2[tc % 2][:, dc * 128:(dc + 1) * 128], ident[:]))

    @block.scalar
    def _(s):
        for tc in range(NTC):
            for dc in range(NDC):
                i = tc * 8 + dc
                s.wait_ge(p_mm, b_mm + i + 1)
                pp = ptA if i % 2 == 0 else ptB
                pEV.inc(s.copy(xlt[:, dc * T + tc * 128: dc * T + (tc + 1) * 128], pp[:, 0:128]))

    wrow2 = [es2.enter_context(nc.sbuf_tensor(f"wrw2_{i}", [128, 1024], F32)) for i in range(2)]
    tev2 = [es2.enter_context(nc.sbuf_tensor(f"tv2_{i}", [128, 1024], F32)) for i in range(2)]
    pt3 = [es2.enter_context(nc.psum_tensor(f"pt3{i}", [128, 128], F32)) for i in range(2)]
    jobsQ = [(w, ec) for w in (0, 1) for ec in range(8)]
    EV0q = pEV.n
    TD0q = tD.n
    tgt_ldq = [LDT2[j % 2].n + 16 * (j // 2 + 1) for j in range(16)]
    accq = [ST4[2].n, ST4[3].n]
    st_afterq = {}
    for j in range(16):
        accq[j % 2] += 16
        st_afterq[j] = accq[j % 2]

    @block.sync
    def _(sy):
        for j, (w, ec) in enumerate(jobsQ):
            if j >= 2:
                sy.wait_ge(p_ev, EV0q + 8 * (j - 1))
            LDT2[j % 2].inc(sy.dma_start(
                out=wrow2[j % 2][:],
                in_=(d_wq if w == 0 else d_wo)[ec * 128:(ec + 1) * 128, :]))

    @block.tensor
    def _(t):
        for j, (w, ec) in enumerate(jobsQ):
            t.wait_ge(ldt2[j % 2], tgt_ldq[j])
            for dc in range(8):
                gi = j * 8 + dc
                if gi >= 2:
                    t.wait_ge(p_ev, EV0q + gi - 1)
                tD.inc(t.transpose(pt3[gi % 2][:, 0:128],
                                   wrow2[j % 2][:, dc * 128:(dc + 1) * 128], ident[:]))

    @block.scalar
    def _(s):
        for j, (w, ec) in enumerate(jobsQ):
            if j >= 2:
                s.wait_ge(st4[2 + (j % 2)], st_afterq[j - 2])
            for dc in range(8):
                gi = j * 8 + dc
                s.wait_ge(t_d, TD0q + gi + 1)
                pEV.inc(s.copy(tev2[j % 2][:, dc * 128:(dc + 1) * 128], pt3[gi % 2][:, 0:128]))

    @block.gpsimd
    def _(g):
        for j, (w, ec) in enumerate(jobsQ):
            g.wait_ge(p_ev, EV0q + 8 * (j + 1))
            src3 = tev2[j % 2].rearrange("p (dc c) -> p dc c", dc=8)
            dst = (s_wqT if w == 0 else s_woT)[:, ec * 128:(ec + 1) * 128]
            ST4[2 + (j % 2)].inc(g.dma_start(
                out=dst.rearrange("(dc p) c -> p dc c", p=128), in_=src3))

    barrier()
    es2.close()

    @block.gpsimd
    def _(g):
        for tc in range(NTC):
            qGP.inc(g.dma_start(out=s_xpark[tc * 128:(tc + 1) * 128, :],
                                in_=sb_x[:, tc * D:(tc + 1) * D]))

    barrier()
    es1.close()

    es_q = ExitStack()
    qT = es_q.enter_context(nc.sbuf_tensor("qTt", [128, NDC * T], F32))
    es6 = ExitStack()
    wqe = [es6.enter_context(nc.sbuf_tensor(f"wqe{i}", [128, 1024], F32)) for i in range(2)]
    pQP = [es6.enter_context(nc.psum_tensor(f"pQP{i}", [128, TTN], F32)) for i in range(2)]
    qk = [0]
    qtd = {}
    q_ld = {}
    BQ = pQ.n

    def qproj_emit(n):
        ks = list(range(qk[0], min(8 * NTT, qk[0] + n)))
        qk[0] += len(ks)
        if not ks:
            return
        ecs = [ec for ec in sorted({k // NTT for k in ks}) if ec not in q_ld]

        @block.sync
        def _(sy):
            for ec in ecs:
                if ec >= 2:
                    sy.wait_ge(p_q, BQ + 4 * (ec - 1))
                LDT2[ec % 2].inc(sy.dma_start(
                    out=wqe[ec % 2].rearrange("p (dc c) -> p dc c", dc=8),
                    in_=s_wqT[:, ec * 128:(ec + 1) * 128].rearrange("(dc p) c -> p dc c", p=128)))
                q_ld[ec] = LDT2[ec % 2].n

        @block.tensor
        def _(t):
            for k in ks:
                ec, tt = divmod(k, NTT)
                if tt == 0 or k == ks[0]:
                    t.wait_ge(ldt2[ec % 2], q_ld[ec])
                if k >= 2:
                    t.wait_ge(p_q, BQ + k - 1)
                for dc in range(NDC):
                    mm = t.matmul(pQP[k % 2][:],
                                  wqe[ec % 2][:, dc * 128:(dc + 1) * 128],
                                  xlt[:, dc * T + tt * TTN: dc * T + (tt + 1) * TTN],
                                  start=(dc == 0), stop=(dc == NDC - 1))
                tD.inc(mm)
                qtd[k] = tD.n

        @block.scalar
        def _(s):
            for k in ks:
                ec, tt = divmod(k, NTT)
                s.wait_ge(t_d, qtd[k])
                pQ.inc(s.copy(qT[:, ec * T + tt * TTN: ec * T + (tt + 1) * TTN], pQP[k % 2][:]))

    es3 = ExitStack()
    mval = es3.enter_context(nc.sbuf_tensor("mval", [16, 64], F32, side="right"))
    midx = es3.enter_context(nc.sbuf_tensor("midx", [16, 64], U32, side="right"))
    midf = es3.enter_context(nc.sbuf_tensor("midf", [16, 64], F32, side="right"))
    idxT = es3.enter_context(nc.sbuf_tensor("idxT", [64, 16], F32, side="right"))
    idxTi = es3.enter_context(nc.sbuf_tensor("idxTi", [64, 16], I32, side="right"))
    psc = es3.enter_context(nc.psum_tensor("psc", [16, TTN], F32))
    pstx = es3.enter_context(nc.psum_tensor("pstx", [64, 16], F32))
    es3a = ExitStack()
    sc0 = es3a.enter_context(nc.sbuf_tensor("sc0", [16, T], F32, side="right"))
    sc1 = es3a.enter_context(nc.sbuf_tensor("sc1", [16, T], F32, side="right"))

    b_mm, b_ev = pMM.n, pEV2.n

    @block.tensor
    def _(t):
        for tt in range(NTT):
            if tt >= 1:
                t.wait_ge(p_ev2, b_ev + tt)
            for dc in range(NDC):
                mm = t.matmul(psc[:], Ut[:, dc * 16:(dc + 1) * 16],
                              xlt[:, dc * T + tt * TTN: dc * T + (tt + 1) * TTN],
                              start=(dc == 0), stop=(dc == NDC - 1))
            pMM.inc(mm)

    @block.vector
    def _(v):
        for tt in range(NTT):
            v.wait_ge(p_mm, b_mm + tt + 1)
            pEV2.inc(v.tensor_copy(sc0[:, tt * TTN:(tt + 1) * TTN], psc[:]))

    barrier()

    @block.vector
    def _(v):
        cur, nxt = sc0, sc1
        for r in range(8):
            v.wait_ge(v_d, vD.n)
            vD.inc(v.max(out=mval[:, r * 8:(r + 1) * 8], in_=cur[:]))
            v.wait_ge(v_d, vD.n)
            vD.inc(v.max_index(midx[:, r * 8:(r + 1) * 8], mval[:, r * 8:(r + 1) * 8], cur[:]))
            v.wait_ge(v_d, vD.n)
            vD.inc(v.match_replace(nxt[:], in_to_replace=mval[:, r * 8:(r + 1) * 8],
                                   in_values=cur[:], imm_value=NEG))
            cur, nxt = nxt, cur
        v.wait_ge(v_d, vD.n)
        vD.inc(v.tensor_copy(midf[:], midx[:]))

    qproj_emit(13)
    barrier()
    es3a.close()

    @block.tensor
    def _(t):
        tD.inc(t.transpose(pstx[:], midf[:], ident[0:16, 0:16]))

    @block.vector
    def _(v):
        v.wait_ge(t_d, tD.n)
        vD.inc(v.tensor_copy(idxT[:], pstx[:]))
        v.wait_ge(v_d, vD.n)
        vD.inc(v.tensor_copy(idxTi[:], idxT[:]))

    barrier()

    es4 = ExitStack()
    xcd = [es4.enter_context(nc.sbuf_tensor(f"xcd{i}", [64, 1024], F32, side="right")) for i in range(2)]
    xct = es4.enter_context(nc.sbuf_tensor("xct", [128, NDC * 1024], F32, side="right"))
    es4p = ExitStack()
    ptA2 = es4p.enter_context(nc.psum_tensor("ptA2", [128, 128], F32))
    ptB2 = es4p.enter_context(nc.psum_tensor("ptB2", [128, 128], F32))

    qproj_emit(6)
    b_mm, b_ev = pMM.n, pEV.n
    tgt_ld = [IG2[h % 2].n + 16 * (h // 2 + 1) for h in range(NH)]

    @block.gpsimd
    def _(g):
        for h in range(NH):
            if h >= 2:
                g.wait_ge(p_mm, b_mm + 8 * (h - 1))
            IG2[h % 2].inc(g.indirect_dma_start(
                out=xcd[h % 2][:], out_offset=None,
                in_=s_xln2[:, :],
                in_offset=IndirectOffsetOnAxis(idxTi[0:64, h:h + 1], axis=0)))

    @block.tensor
    def _(t):
        for h in range(NH):
            t.wait_ge(ig2[h % 2], tgt_ld[h])
            for dc in range(NDC):
                i = h * 8 + dc
                if i >= 2:
                    t.wait_ge(p_ev, b_ev + i - 1)
                pp = ptA2 if i % 2 == 0 else ptB2
                pMM.inc(t.transpose(pp[:, 0:64], xcd[h % 2][:, dc * 128:(dc + 1) * 128],
                                    ident[0:64, 0:64]))

    @block.scalar
    def _(s):
        for h in range(NH):
            for dc in range(NDC):
                i = h * 8 + dc
                s.wait_ge(p_mm, b_mm + i + 1)
                pp = ptA2 if i % 2 == 0 else ptB2
                pEV.inc(s.copy(xct[:, dc * 1024 + h * 64: dc * 1024 + (h + 1) * 64], pp[:, 0:64]))

    barrier()
    es4p.close()

    es5 = ExitStack()
    wkve = [es5.enter_context(nc.sbuf_tensor(f"wkve{i}", [128, 1024], F32, side="right")) for i in range(2)]
    pay = es5.enter_context(nc.sbuf_tensor("pays", [64, 128], F32, side="right"))
    pKV = [es5.enter_context(nc.psum_tensor(f"pKV{i}", [64, 128], F32)) for i in range(2)]

    b_mm, b_ev = pMM.n, pEV.n
    b_st0 = ST4[0].n
    tgt_kv = [LDU2[h % 2].n + 16 * (h // 2 + 1) for h in range(NH)]

    @block.sync
    def _(sy):
        for h in range(NH):
            if h >= 2:
                sy.wait_ge(p_mm, b_mm + h - 1)
            LDU2[h % 2].inc(sy.dma_start(
                out=wkve[h % 2].rearrange("p (dc c) -> p dc c", dc=8),
                in_=s_wkvT[:, h * 128:(h + 1) * 128].rearrange("(dc p) c -> p dc c", p=128)))

    @block.tensor
    def _(t):
        for h in range(NH):
            t.wait_ge(ldu2[h % 2], tgt_kv[h])
            if h >= 2:
                t.wait_ge(p_ev, b_ev + h - 1)
            for dc in range(NDC):
                mm = t.matmul(pKV[h % 2][:],
                              xct[:, dc * 1024 + h * 64: dc * 1024 + (h + 1) * 64],
                              wkve[h % 2][:, dc * 128:(dc + 1) * 128],
                              start=(dc == 0), stop=(dc == NDC - 1))
            pMM.inc(mm)

    @block.scalar
    def _(s):
        for h in range(NH):
            s.wait_ge(p_mm, b_mm + h + 1)
            if h >= 1:
                s.wait_ge(st4[0], b_st0 + 16 * h)
            pEV.inc(s.copy(pay[:], pKV[h % 2][:]))

    qproj_emit(6)

    @block.gpsimd
    def _(g):
        for h in range(NH):
            g.wait_ge(p_ev, b_ev + h + 1)
            ST4[0].inc(g.dma_start(out=ag_in[h * 64:(h + 1) * 64, :], in_=pay[:]))
        g.wait_ge(v_d, vD.n)
        qGP.inc(g.dma_start(out=ag_in[1024:1032, :].rearrange("a (b c) -> (a b) c", b=2),
                            in_=mval[:]))

    barrier()
    es5.close()
    es4.close()
    es3.close()

    es7 = ExitStack()
    hsel = es7.enter_context(nc.sbuf_tensor("hsel", [16, 2048], F32, side="right"))
    va0 = es7.enter_context(nc.sbuf_tensor("va0", [16, 512], F32, side="right"))
    va1 = es7.enter_context(nc.sbuf_tensor("va1", [16, 512], F32, side="right"))
    gval = es7.enter_context(nc.sbuf_tensor("gval", [16, 64], F32, side="right"))
    gposu = es7.enter_context(nc.sbuf_tensor("gposu", [16, 64], U32, side="right"))
    gposf = es7.enter_context(nc.sbuf_tensor("gposf", [16, 64], F32, side="right"))

    @block.gpsimd
    def _(g):
        gD.inc(g.collective_compute("AllGather", AL.bypass, replica_groups=RG,
                                    ins=[ag_in.opt()], outs=[ag_out.opt()]))

    qproj_emit(32)

    @block.sync
    def _(sy):
        sy.wait_ge(g_d, gD.n)
        qSY.inc(sy.dma_start(out=hsel[:], in_=c_hsel))
        for c in range(8):
            qSY.inc(sy.dma_start(
                out=va0[:, c * 64:(c + 1) * 64],
                in_=ag_out[c * 1032 + 1024: c * 1032 + 1032, :].rearrange("a (b c) -> (a b) c", b=2)))

    @block.vector
    def _(v):
        v.wait_ge(q_sy, qSY.n)
        cur, nxt = va0, va1
        for r in range(8):
            v.wait_ge(v_d, vD.n)
            vD.inc(v.max(out=gval[:, r * 8:(r + 1) * 8], in_=cur[:]))
            v.wait_ge(v_d, vD.n)
            vD.inc(v.max_index(gposu[:, r * 8:(r + 1) * 8], gval[:, r * 8:(r + 1) * 8], cur[:]))
            v.wait_ge(v_d, vD.n)
            vD.inc(v.match_replace(nxt[:], in_to_replace=gval[:, r * 8:(r + 1) * 8],
                                   in_values=cur[:], imm_value=NEG))
            cur, nxt = nxt, cur
        v.wait_ge(v_d, vD.n)
        vD.inc(v.tensor_copy(gposf[:], gposu[:]))

    barrier()
    es6.close()

    es7p = ExitStack()
    kva = xlt[:, 0:8192]
    e2 = xlt[:, 8192:12288]
    ppos = es7p.enter_context(nc.psum_tensor("ppos", [128, 1024], F32))
    pSELb = es7p.enter_context(nc.psum_tensor("pSELb", [128, 128], F32))
    pKTb = es7p.enter_context(nc.psum_tensor("pKTb", [128, 128], F32))
    pSEL = [pSELb[:, 0:64], pSELb[:, 64:128]]
    pKT = [pKTb[:, 0:64], pKTb[:, 64:128]]

    b_mm, b_e2 = pMM.n, pEV2.n

    @block.tensor
    def _(t):
        t.wait_ge(v_d, vD.n)
        for h in range(NH):
            mm = t.matmul(ppos[:, h * 64:(h + 1) * 64], hsel[:, h * 128:(h + 1) * 128],
                          gposf[:], start=True, stop=True, skip_group_check=True)
        pMM.inc(mm)

    @block.vector
    def _(v):
        v.wait_ge(p_mm, b_mm + 1)
        for h in range(NH):
            for q4 in range(4):
                vD.inc(v.tensor_scalar(e2[:, (h * 4 + q4) * 64:(h * 4 + q4 + 1) * 64],
                                       ppos[:, h * 64:(h + 1) * 64],
                                       iota4[:, q4:q4 + 1], None, AL.is_equal))

    @block.sync
    def _(sy):
        for c in range(8):
            dst = kva[(c % 2) * 64:(c % 2) * 64 + 64, :].rearrange(
                "p (h q k) -> p h q k", h=16, q=4)[:, :, c // 2, :]
            qSY.inc(sy.dma_start(
                out=dst,
                in_=ag_out[c * 1032: c * 1032 + 1024, :].rearrange("(h r) k -> r h k", h=16)))

    b_mm2 = pMM.n

    @block.tensor
    def _(t):
        t.wait_ge(v_d, vD.n)
        t.wait_ge(q_sy, qSY.n)
        for m in range(NH):
            hp, sub = divmod(m, 2)
            h = m
            pr = slice(64 * sub, 64 * sub + 64)
            if m >= 2:
                t.wait_ge(p_ev2, b_e2 + 2 * (m - 1))
            for q4 in range(4):
                t.matmul(pSEL[m % 2][pr, :], e2[:, (h * 4 + q4) * 64:(h * 4 + q4 + 1) * 64],
                         kva[:, (h * 4 + q4) * 128 + 64:(h * 4 + q4) * 128 + 128],
                         start=(q4 == 0), stop=(q4 == 3), skip_group_check=True)
                mm = t.matmul(pKT[m % 2][pr, :], kva[:, (h * 4 + q4) * 128:(h * 4 + q4) * 128 + 64],
                              e2[:, (h * 4 + q4) * 64:(h * 4 + q4 + 1) * 64],
                              start=(q4 == 0), stop=(q4 == 3), skip_group_check=True)
            pMM.inc(mm)

    @block.vector
    def _(v):
        for m in range(NH):
            hp, sub = divmod(m, 2)
            pr = slice(64 * sub, 64 * sub + 64)
            v.wait_ge(p_mm, b_mm2 + m + 1)
            pEV2.inc(v.tensor_copy(vselt[pr, hp * 64:(hp + 1) * 64], pSEL[m % 2][pr, :]))
            pEV2.inc(v.tensor_copy(kselT[pr, hp * 64:(hp + 1) * 64], pKT[m % 2][pr, :]))

    barrier()
    es7p.close()
    es7.close()
    es3.close()
    es_xlt.close()

    es_x2 = ExitStack()
    sb_x2 = es_x2.enter_context(nc.sbuf_tensor("sb_x2", [128, NTC * D], F32, side="right"))
    es8 = ExitStack()
    woTs = es8.enter_context(nc.sbuf_tensor("woTs", [128, NDC * 1024], F32))
    selrz = es8.enter_context(nc.sbuf_tensor("selrz", [16, 1024], F32))
    oT = [es8.enter_context(nc.sbuf_tensor(f"oTt{i}", [128, TTN], F32)) for i in range(8)]
    pb_ = [es8.enter_context(nc.sbuf_tensor(f"pbt{i}", [128, TTN], F32)) for i in range(2)]
    rzt = es8.enter_context(nc.sbuf_tensor("rztt", [16, TTN], F32))
    pS = [es8.enter_context(nc.psum_tensor(f"pSs{i}", [128, TTN], F32)) for i in range(2)]
    pO = [es8.enter_context(nc.psum_tensor(f"pOo{i}", [128, TTN], F32)) for i in range(2)]
    pZ = es8.enter_context(nc.psum_tensor("pZz", [16, TTN], F32))
    pY = [es8.enter_context(nc.psum_tensor(f"pYy{i}", [128, TTN], F32)) for i in range(2)]

    @block.sync
    def _(sy):
        qSY.inc(sy.dma_start(out=selrz[:], in_=c_selrz))
        for dc in range(NDC):
            qSY.inc(sy.dma_start(out=woTs[:, dc * 1024:(dc + 1) * 1024],
                                 in_=s_woT[dc * 128:(dc + 1) * 128, :]))

    @block.gpsimd
    def _(g):
        for tc in range(NTC):
            qGP.inc(g.dma_start(out=sb_x2[:, tc * D:(tc + 1) * D],
                                in_=s_xpark[tc * 128:(tc + 1) * 128, :]))

    barrier()

    for tt in range(NTT):
        b_mm, b_ev, b_q, b_q2, b_rd, b_td = pMM.n, pEV.n, pQ.n, pQ2.n, pRD.n, tD.n

        @block.tensor
        def _(t):
            for hp in range(8):
                t.wait_ge(p_q, max(0, b_q + hp - 1))
                t.matmul(pS[hp % 2][0:64, :], kselT[0:64, hp * 64:(hp + 1) * 64],
                         qT[0:64, hp * T + tt * TTN: hp * T + (tt + 1) * TTN],
                         start=True, stop=True, skip_group_check=True)
                pMM.inc(t.matmul(pS[hp % 2][64:128, :], kselT[64:128, hp * 64:(hp + 1) * 64],
                                 qT[64:128, hp * T + tt * TTN: hp * T + (tt + 1) * TTN],
                                 start=True, stop=True, skip_group_check=True))
                t.wait_ge(p_q, b_q + hp + 1)
                t.wait_ge(p_ev, max(0, b_ev + hp - 1))
                if hp == 0:
                    t.wait_ge(p_q2, b_q2)
                t.matmul(pO[hp % 2][0:64, :], vselt[0:64, hp * 64:(hp + 1) * 64],
                         pb_[hp % 2][0:64, :], start=True, stop=True, skip_group_check=True)
                t.matmul(pO[hp % 2][64:128, :], vselt[64:128, hp * 64:(hp + 1) * 64],
                         pb_[hp % 2][64:128, :], start=True, stop=True, skip_group_check=True)
                pMM.inc(t.matmul(pZ[:], selz[:, hp * 16:(hp + 1) * 16], pb_[hp % 2][:],
                                 start=(hp == 0), stop=(hp == 7), skip_group_check=True))

        @block.scalar
        def _(s):
            for hp in range(8):
                s.wait_ge(p_mm, b_mm + 2 * hp + 1)
                s.wait_ge(p_mm, max(0, b_mm + 2 * hp - 2))
                pQ.inc(s.activation(pb_[hp % 2][:], pS[hp % 2][:], AF.Exp,
                                    bias=zcol[:, 0:1], scale=attsc[:, 0:1]))
                if hp >= 1:
                    s.wait_ge(p_mm, b_mm + 2 * (hp - 1) + 2)
                    pEV.inc(s.copy(oT[hp - 1][:], pO[(hp - 1) % 2][:]))
            s.wait_ge(p_mm, b_mm + 16)
            pEV.inc(s.copy(oT[7][:], pO[7 % 2][:]))

        @block.vector
        def _(v):
            v.wait_ge(p_mm, b_mm + 16)
            pQ2.inc(v.reciprocal(rzt[:], pZ[:]))

        rot = [pS[0], pS[1], pO[0], pO[1]]

        @block.tensor
        def _(t):
            for hp in range(8):
                if hp == 0:
                    t.wait_ge(p_q2, b_q2 + 1)
                elif hp == 2:
                    t.wait_ge(p_ev, b_ev + 7)
                elif hp == 3:
                    t.wait_ge(p_ev, b_ev + 8)
                elif hp >= 4:
                    t.wait_ge(p_rd, b_rd + hp - 3)
                tD.inc(t.matmul(rot[hp % 4][:], selrz[:, hp * 128:(hp + 1) * 128], rzt[:],
                                start=True, stop=True))

        tD_base = tD.n - 8

        @block.vector
        def _(v):
            for hp in range(8):
                v.wait_ge(t_d, tD_base + hp + 1)
                v.wait_ge(p_ev, b_ev + hp + 1)
                pRD.inc(v.tensor_tensor(oT[hp][:], oT[hp][:], rot[hp % 4][:], AL.mult))

        b_mm2, b_evv = pMM.n, pEV2.n

        @block.tensor
        def _(t):
            t.wait_ge(p_rd, b_rd + 8)
            for gidx in range(8):
                tc4, ft = divmod(gidx, 2)
                t.wait_ge(p_ev2, max(0, b_evv + gidx - 1))
                for hp in range(8):
                    mm = t.matmul(pY[gidx % 2][:], oT[hp][:, tc4 * 128:(tc4 + 1) * 128],
                                  woTs[:, hp * 1024 + ft * TTN: hp * 1024 + (ft + 1) * TTN],
                                  start=(hp == 0), stop=(hp == 7))
                pMM.inc(mm)

        @block.vector
        def _(v):
            for gidx in range(8):
                tc4, ft = divmod(gidx, 2)
                v.wait_ge(p_mm, b_mm2 + gidx + 1)
                tcg = tt * 4 + tc4
                xs = sb_x2[:, tcg * D + ft * TTN: tcg * D + (ft + 1) * TTN]
                pEV2.inc(v.scalar_tensor_tensor(xs, pY[gidx % 2][:], 1.0, xs, AL.mult, AL.add))

        @block.scalar
        def _(s):
            for tc4 in range(4):
                tcg = tt * 4 + tc4
                xs2 = sb_x2[:, tcg * D:(tcg + 1) * D]
                s.wait_ge(p_ev2, b_evv + 2 * tc4 + 2)
                s.wait_ge(p_q, pQ.n)
                pQ.inc(s.activation(junk[:], xs2, AF.Identity, bias=zcol[:, 0:1], scale=1.0,
                                    accum_out=stat[:, tcg:tcg + 1]))
                s.wait_ge(p_q, pQ.n)
                pQ.inc(s.activation(junk[:], xs2, AF.Square, bias=zcol[:, 0:1], scale=1.0,
                                    accum_out=stat[:, 16 + tcg:16 + tcg + 1]))

        @block.vector
        def _(v):
            for tc4 in range(4):
                tcg = tt * 4 + tc4
                xs2 = sb_x2[:, tcg * D:(tcg + 1) * D]
                v.wait_ge(p_ev2, b_evv + 2 * tc4 + 2)
                vD.inc(v.reduce_max(stat[:, 96 + tcg:96 + tcg + 1], xs2, AX.X))
                vD.inc(v.tensor_reduce(stat[:, 112 + tcg:112 + tcg + 1], xs2, AX.X, AL.min))

    barrier()
    es8.close()
    es_q.close()

    if UPTO == "attn":
        finalize(sb_x2)
        es_x2.close(); es.close()
        return nc

    ffn("f2", sb_x2, skip_stats=True, early_w1=True)

    finalize(sb_x2)
    es_x2.close()
    es.close()
    return nc


def _consts():
    import ml_dtypes
    ident = np.eye(128, dtype=np.float32)
    ones1 = np.ones((1, 128), np.float32)
    ones2b = np.ones((2, 128), ml_dtypes.bfloat16)
    iota4 = (np.arange(128, dtype=np.float32)[:, None] +
             128.0 * np.arange(4, dtype=np.float32)[None, :])
    selz = np.zeros((128, 128), np.float32)
    for hp in range(8):
        selz[0:64, hp * 16 + 2 * hp] = 1.0
        selz[64:128, hp * 16 + 2 * hp + 1] = 1.0
    selrz = np.zeros((16, 1024), np.float32)
    for hp in range(8):
        for p in range(128):
            selrz[2 * hp + p // 64, hp * 128 + p] = 1.0
    hselc = np.zeros((16, 2048), np.float32)
    for h in range(16):
        hselc[h, h * 128:(h + 1) * 128] = 1.0
    return ident, ones1, ones2b, iota4, selz, selrz, hselc


def kernel(**inputs):
    x = np.asarray(inputs["L0_attn_x"], np.float32).reshape(NCORES * T, D)
    ws = np.asarray(inputs["L0_attn_w_score"], np.float32)
    wsbc = np.zeros((128, 2), np.float32)
    wsbc[0:64, 0] = ws
    wsbc[64:128, 1] = ws
    ident, ones1, ones2b, iota4, selz, selrz, hselc = _consts()

    nc = _build()
    shared = {}
    for k in ("L0_attn_Wq", "L0_attn_Wk", "L0_attn_Wv", "L0_attn_Wo",
              "L0_ffn_w1", "L0_ffn_b1", "L0_ffn_w2", "L0_ffn_b2"):
        shared[k] = np.ascontiguousarray(np.asarray(inputs[k], np.float32))
    shared.update(c_ident=ident, c_ones1=ones1, c_ones2b=ones2b, c_iota4=iota4,
                  c_selz=selz, c_selrz=selrz, c_hsel=hselc, c_wsb=wsbc)
    in_maps = []
    for c in range(NCORES):
        m = dict(shared)
        m["L0_attn_x"] = np.ascontiguousarray(x[c * T:(c + 1) * T, :])
        in_maps.append(m)
    res = run_bass_kernel_spmd(nc, in_maps, core_ids=list(range(NCORES)))
    y = np.concatenate([np.asarray(r["y"], np.float32) for r in res.results], axis=0)
    return y.reshape(1, NCORES * T, D)


# version 22
# speedup vs baseline: 1.0297x; 1.0019x over previous
import numpy as np
import concourse.bass as bass
import concourse.mybir as mybir
from concourse.bass_utils import run_bass_kernel_spmd
from concourse.bass_core_rust import IndirectOffsetOnAxis
from contextlib import ExitStack

F32 = mybir.dt.float32
F16 = mybir.dt.float16
BF16 = mybir.dt.bfloat16
I32 = mybir.dt.int32
U32 = mybir.dt.uint32
AL = mybir.AluOpType
AF = mybir.ActivationFunctionType
AX = mybir.AxisListType

NCORES = 8
T = 2048
D = 1024
HID = 4096
NH = 16
HD = 64
NTC = T // 128
NDC = D // 128
NHC = HID // 128
TTN = 512
NTT = T // TTN
EPS = 1e-5
RB = 1536.0
NEG = -3.0e38

UPTO = None


class Ctr:
    def __init__(self, sem, step=1):
        self.sem, self.n, self.step = sem, 0, step

    def inc(self, instr, k=1):
        instr.then_inc(self.sem, k * self.step)
        self.n += k * self.step
        return self.n


def _build():
    nc = bass.Bass("TRN2", target_bir_lowering=False)

    d_x = nc.dram_tensor("L0_attn_x", [T, D], F32, kind="ExternalInput").ap()
    d_wq = nc.dram_tensor("L0_attn_Wq", [D, D], F32, kind="ExternalInput").ap()
    d_wk = nc.dram_tensor("L0_attn_Wk", [D, D], F32, kind="ExternalInput").ap()
    d_wv = nc.dram_tensor("L0_attn_Wv", [D, D], F32, kind="ExternalInput").ap()
    d_wo = nc.dram_tensor("L0_attn_Wo", [D, D], F32, kind="ExternalInput").ap()
    d_w1 = nc.dram_tensor("L0_ffn_w1", [HID, D], F32, kind="ExternalInput").ap()
    d_b1 = nc.dram_tensor("L0_ffn_b1", [HID], F32, kind="ExternalInput").ap()
    d_w2 = nc.dram_tensor("L0_ffn_w2", [D, HID], F32, kind="ExternalInput").ap()
    d_b2 = nc.dram_tensor("L0_ffn_b2", [D], F32, kind="ExternalInput").ap()
    c_ident = nc.dram_tensor("c_ident", [128, 128], F32, kind="ExternalInput").ap()
    c_ones1 = nc.dram_tensor("c_ones1", [1, 128], F32, kind="ExternalInput").ap()
    c_ones2b = nc.dram_tensor("c_ones2b", [2, 128], BF16, kind="ExternalInput").ap()
    c_iota4 = nc.dram_tensor("c_iota4", [128, 4], F32, kind="ExternalInput").ap()
    c_selz = nc.dram_tensor("c_selz", [128, 128], F32, kind="ExternalInput").ap()
    c_selrz = nc.dram_tensor("c_selrz", [16, 1024], F32, kind="ExternalInput").ap()
    c_hsel = nc.dram_tensor("c_hsel", [16, 2048], F32, kind="ExternalInput").ap()
    c_wsb = nc.dram_tensor("c_wsb", [128, 2], F32, kind="ExternalInput").ap()

    d_y = nc.dram_tensor("y", [T, D], F32, kind="ExternalOutput").ap()

    s_xq = nc.dram_tensor("s_xq", [T, D], BF16).ap()
    s_g = nc.dram_tensor("s_g", [HID, T], F32).ap()
    s_xln2 = nc.dram_tensor("s_xln2", [T, D], F32).ap()
    s_xpark = nc.dram_tensor("s_xpark", [T, D], F32).ap()
    s_w1q = nc.dram_tensor("s_w1q", [HID, D], BF16).ap()
    s_w2q = nc.dram_tensor("s_w2q", [D, HID], BF16).ap()
    s_wqT = nc.dram_tensor("s_wqT", [D, D], F32).ap()
    s_woT = nc.dram_tensor("s_woT", [D, D], F32).ap()
    s_wkvT = nc.dram_tensor("s_wkvT", [D, 2 * D], F32).ap()
    cc_in = nc.dram_tensor("cc_in", [1, 8], F32).ap()
    cc_out = nc.dram_tensor("cc_out", [1, 8], F32).ap()
    ag_in = nc.dram_tensor("ag_in", [1032, 128], F32).ap()
    ag_out = nc.dram_tensor("ag_out", [8 * 1032, 128], F32).ap()
    RG = [list(range(NCORES))]

    es = ExitStack()
    sbuf = lambda n, s, d=F32: es.enter_context(nc.sbuf_tensor(n, s, d))
    sem = lambda n: es.enter_context(nc.semaphore(n))

    ident = sbuf("ident", [128, 128])
    ones1 = sbuf("ones1", [1, 128])
    ones2b = sbuf("ones2b", [2, 128], BF16)
    iota4 = sbuf("iota4", [128, 4])
    selz = sbuf("selz", [128, 128])
    wsb = sbuf("wsb", [128, 2])
    Ut = sbuf("Ut", [128, 128])
    b1t = sbuf("b1t", [128, NHC])
    b1sc = sbuf("b1sc", [128, NHC])
    b2s = sbuf("b2s", [2, D], BF16)
    scal = sbuf("scal", [128, 16])
    one1 = sbuf("one1", [1, 16])
    cct = sbuf("cct", [1, 8])
    ccr = sbuf("ccr", [1, 8])
    stat = sbuf("stat", [128, 16 * 8])
    junk = es.enter_context(nc.sbuf_tensor("junkt", [128, 1024], F32, side="right"))
    zcol = sbuf("zcol", [128, 1])
    epst = sbuf("epst", [128, 1])
    rbt = sbuf("rbt", [128, 1])
    attsc = sbuf("attsc", [128, 1])
    kselT = sbuf("kselTt", [128, 8 * 64])
    vselt = sbuf("vselt", [128, 8 * 64])
    gmax = sbuf("gmaxt", [128, 1])
    gtmp = sbuf("gtmpt", [128, 1])
    lred = sbuf("lredt", [1, 128])

    PH = sem("PH")
    t_d, v_d, s_d, g_d = sem("t_d"), sem("v_d"), sem("s_d"), sem("g_d")
    q_sy, q_gp = sem("q_sy"), sem("q_gp")
    p_mm, p_ev, p_rd = sem("p_mm"), sem("p_ev"), sem("p_rd")
    p_ev2 = sem("p_ev2")
    p_q, p_q2 = sem("p_q"), sem("p_q2")
    st4 = [sem(f"st4_{i}") for i in range(4)]
    ld2 = [sem(f"ld2_{i}") for i in range(2)]
    ig2 = [sem(f"ig2_{i}") for i in range(2)]
    ldt2 = [sem(f"ldt2_{i}") for i in range(2)]
    ldu2 = [sem(f"ldu2_{i}") for i in range(2)]
    sts2 = [sem(f"sts2_{i}") for i in range(2)]

    ps_g = es.enter_context(nc.psum_tensor("ps_g", [128, 128], F32))

    block = es.enter_context(nc.Block())

    tD, vD, sD, gD = Ctr(t_d), Ctr(v_d), Ctr(s_d), Ctr(g_d)
    qSY, qGP = Ctr(q_sy, 16), Ctr(q_gp, 16)
    pMM, pEV, pRD = Ctr(p_mm), Ctr(p_ev), Ctr(p_rd)
    pEV2 = Ctr(p_ev2)
    ST4 = [Ctr(x, 16) for x in st4]
    LD2 = [Ctr(x, 16) for x in ld2]
    IG2 = [Ctr(x, 16) for x in ig2]
    LDT2 = [Ctr(x, 16) for x in ldt2]
    LDU2 = [Ctr(x, 16) for x in ldu2]
    STS2 = [Ctr(x, 16) for x in sts2]
    pQ, pQ2 = Ctr(p_q), Ctr(p_q2)

    phase = [0]

    def barrier():
        phase[0] += 1
        tgt = 5 * phase[0]

        @block.tensor
        def _(t):
            t.wait_ge(t_d, tD.n)
            t.wait_ge(p_mm, pMM.n)
            t.sem_inc(PH, 1); t.wait_ge(PH, tgt)

        @block.vector
        def _(v):
            v.wait_ge(v_d, vD.n)
            v.wait_ge(p_q2, pQ2.n)
            v.wait_ge(p_rd, pRD.n)
            v.wait_ge(p_ev2, pEV2.n)
            v.sem_inc(PH, 1); v.wait_ge(PH, tgt)

        @block.scalar
        def _(s):
            s.wait_ge(s_d, sD.n)
            s.wait_ge(p_q, pQ.n)
            s.wait_ge(p_ev, pEV.n)
            s.sem_inc(PH, 1); s.wait_ge(PH, tgt)

        @block.sync
        def _(sy):
            sy.wait_ge(q_sy, qSY.n)
            sy.sem_inc(PH, 1); sy.wait_ge(PH, tgt)

        @block.gpsimd
        def _(g):
            g.wait_ge(q_gp, qGP.n)
            g.wait_ge(g_d, gD.n)
            for k in range(4):
                g.wait_ge(st4[k], ST4[k].n)
            for k in range(2):
                g.wait_ge(ld2[k], LD2[k].n)
                g.wait_ge(ig2[k], IG2[k].n)
                g.wait_ge(ldt2[k], LDT2[k].n)
                g.wait_ge(ldu2[k], LDU2[k].n)
                g.wait_ge(sts2[k], STS2[k].n)
            g.sem_inc(PH, 1); g.wait_ge(PH, tgt)

    def bcast_cols(src_c0, n, dst_c0):
        @block.tensor
        def _(t):
            t.wait_ge(v_d, vD.n)
            tD.inc(t.matmul(ps_g[:, 0:n], ones1[:], one1[0:1, src_c0:src_c0 + n],
                            start=True, stop=True))

        @block.vector
        def _(v):
            v.wait_ge(t_d, tD.n)
            vD.inc(v.tensor_copy(scal[:, dst_c0:dst_c0 + n], ps_g[:, 0:n]))

    def allreduce_max(src_col, dst_col):
        @block.vector
        def _(v):
            vD.inc(v.memset(cct[:], 0.0))
            v.wait_ge(v_d, vD.n)
            vD.inc(v.tensor_copy(cct[0:1, 0:1], one1[0:1, src_col:src_col + 1]))

        @block.gpsimd
        def _(g):
            g.wait_ge(v_d, vD.n)
            qGP.inc(g.dma_start(out=cc_in, in_=cct[:]))
            g.wait_ge(q_gp, qGP.n)
            gD.inc(g.collective_compute("AllReduce", AL.max, replica_groups=RG,
                                        ins=[cc_in.opt()], outs=[cc_out.opt()]))
            g.wait_ge(g_d, gD.n)
            qGP.inc(g.dma_start(out=ccr[:], in_=cc_out))

        @block.vector
        def _(v):
            v.wait_ge(q_gp, qGP.n)
            vD.inc(v.tensor_copy(one1[0:1, dst_col:dst_col + 1], ccr[0:1, 0:1]))

    def cross_part_max(dst_one1_col):
        @block.tensor
        def _(t):
            t.wait_ge(v_d, vD.n)
            tD.inc(t.transpose(ps_g[0:1, 0:128], gmax[:], ident[:]))

        @block.vector
        def _(v):
            v.wait_ge(t_d, tD.n)
            vD.inc(v.tensor_copy(lred[:], ps_g[0:1, 0:128]))
            v.wait_ge(v_d, vD.n)
            vD.inc(v.reduce_max(one1[0:1, dst_one1_col:dst_one1_col + 1], lred[:], AX.X))

    es1 = ExitStack()
    sb_x = es1.enter_context(nc.sbuf_tensor("sb_x", [128, NTC * D], F32))

    @block.sync
    def _(sy):
        for src, dst in ((c_ident, ident), (c_ones1, ones1), (c_ones2b, ones2b),
                         (c_iota4, iota4), (c_selz, selz), (c_wsb, wsb)):
            qSY.inc(sy.dma_start(out=dst[:], in_=src))
        for tc in range(NTC):
            qSY.inc(sy.dma_start(out=sb_x[:, tc * D:(tc + 1) * D],
                                 in_=d_x[tc * 128:(tc + 1) * 128, :]))
        qSY.inc(sy.dma_start(out=b1t[:], in_=d_b1.rearrange("(a b) -> b a", b=128),
                             allow_slow_non_contiguous=True))

    @block.vector
    def _(v):
        vD.inc(v.memset(zcol[:], 0.0))
        vD.inc(v.memset(epst[:], EPS))
        vD.inc(v.memset(rbt[:], RB))
        vD.inc(v.memset(attsc[:], 0.125))
        vD.inc(v.memset(junk[:], 0.0))

    barrier()

    es0a = ExitStack()
    wbuf = [es0a.enter_context(nc.sbuf_tensor(f"wbuf{i}", [128, 4096], F32)) for i in range(2)]
    wmx8 = es0a.enter_context(nc.sbuf_tensor("wmx8", [128, 16], F32))
    wmx = es0a.enter_context(nc.sbuf_tensor("wmxq", [128, 2], F32))
    pt0 = es0a.enter_context(nc.psum_tensor("pt0", [128, 128], F32))
    pt1 = es0a.enter_context(nc.psum_tensor("pt1", [128, 128], F32))

    es0b = ExitStack()
    wbufB = [es0b.enter_context(nc.sbuf_tensor(f"wbufB{i}", [128, 4096], F32)) for i in range(2)]
    wzb = [es0b.enter_context(nc.sbuf_tensor(f"wzbq{i}", [128, 4096], F16)) for i in range(2)]
    wqb = [es0b.enter_context(nc.sbuf_tensor(f"wqbq{i}", [128, 4096], BF16)) for i in range(2)]
    WB4 = wbuf + wbufB
    LD4 = LD2 + LDU2
    ld4s = ld2 + ldu2

    def unit1(u, q=False):
        w1, w2 = (s_w1q, s_w2q) if q else (d_w1, d_w2)
        if u < 8:
            src = w1[u * 512:(u + 1) * 512, :]
        else:
            j = u - 8
            src = w2[(j % 2) * 512:(j % 2) * 512 + 512, (j // 2) * 1024:(j // 2) * 1024 + 1024]
        return src.rearrange("(r p) d -> p r d", p=128)

    esP1 = ExitStack()
    b2row = esP1.enter_context(nc.sbuf_tensor("b2row", [1, D], F32, side="right"))
    b2hi32 = esP1.enter_context(nc.sbuf_tensor("b2hi32", [1, D], F32, side="right"))
    b2lob = esP1.enter_context(nc.sbuf_tensor("b2lob", [1, D], BF16, side="right"))

    @block.sync
    def _(sy):
        qSY.inc(sy.dma_start(out=b2row[:], in_=d_b2[None, :]))

    def su_absmax():
        b_rd = pRD.n
        b_ld = [LD4[i].n for i in range(4)]

        @block.sync
        def _(sy):
            for u in range(16):
                if u >= 4:
                    sy.wait_ge(p_rd, b_rd + u - 3)
                LD4[u % 4].inc(sy.dma_start(out=WB4[u % 4].rearrange("p (r d) -> p r d", r=4),
                                            in_=unit1(u)))

        @block.vector
        def _(v):
            for u in range(16):
                v.wait_ge(ld4s[u % 4], b_ld[u % 4] + 16 * (u // 4 + 1))
                pRD.inc(v.reduce_max(wmx8[:, u:u + 1], WB4[u % 4][:], AX.X,
                                     apply_absolute_value=True))

    def su_scales():
        @block.vector
        def _(v):
            vD.inc(v.reduce_max(wmx[:, 0:1], wmx8[:, 0:8], AX.X))
            vD.inc(v.reduce_max(wmx[:, 1:2], wmx8[:, 8:16], AX.X))

        @block.tensor
        def _(t):
            t.wait_ge(v_d, vD.n)
            tD.inc(t.transpose(pt0[0:1, 0:128], wmx[:, 0:1], ident[:]))
            tD.inc(t.transpose(pt1[0:1, 0:128], wmx[:, 1:2], ident[:]))

        @block.vector
        def _(v):
            v.wait_ge(t_d, tD.n)
            vD.inc(v.tensor_copy(lred[:], pt0[0:1, 0:128]))
            v.wait_ge(v_d, vD.n)
            vD.inc(v.reduce_max(one1[0:1, 0:1], lred[:], AX.X))
            v.wait_ge(v_d, vD.n)
            vD.inc(v.tensor_copy(lred[:], pt1[0:1, 0:128]))
            v.wait_ge(v_d, vD.n)
            vD.inc(v.reduce_max(one1[0:1, 1:2], lred[:], AX.X))
            v.wait_ge(v_d, vD.n)
            vD.inc(v.tensor_scalar(one1[0:1, 4:5], one1[0:1, 0:1], 1e-6, 1.0 / 127.0, AL.max, AL.mult))
            vD.inc(v.tensor_scalar(one1[0:1, 5:6], one1[0:1, 1:2], 1e-6, 1.0 / 127.0, AL.max, AL.mult))
            v.wait_ge(v_d, vD.n)
            vD.inc(v.reciprocal(one1[0:1, 6:7], one1[0:1, 4:5]))
            vD.inc(v.reciprocal(one1[0:1, 7:8], one1[0:1, 5:6]))

        bcast_cols(6, 2, 0)

    def su_wquant(units, bufs, LDs, lds, zb, qb, rr, scalar_store=False):
        nb = len(bufs)
        nu = len(units)
        b_q, b_q2 = pQ.n, pQ2.n
        b_ld = [LDs[i].n for i in range(nb)]
        shape = (lambda a: a.rearrange("p (r d) -> p r d", r=rr)) if rr else (lambda a: a[:])
        STs = STS2 if scalar_store else ST4
        sts = sts2 if scalar_store else st4
        tgt_st = [STs[k % 2].n + 16 * (k // 2 + 1) for k in range(nu)]
        b_st = [STs[0].n, STs[1].n]
        b_st4 = [ST4[0].n, ST4[1].n]

        @block.sync
        def _(sy):
            for k in range(nu):
                if k >= nb:
                    sy.wait_ge(p_q, b_q + k - nb + 1)
                LDs[k % nb].inc(sy.dma_start(out=shape(bufs[k % nb]), in_=units[k][0]))

        @block.scalar
        def _(s):
            for k in range(nu):
                s.wait_ge(lds[k % nb], b_ld[k % nb] + 16 * (k // nb + 1))
                s.wait_ge(p_q2, b_q2 + k - 1 if k >= 2 else b_q2)
                pQ.inc(s.activation(zb[k % 2][:], bufs[k % nb][:], AF.Identity,
                                    bias=rbt[:, 0:1], scale=scal[:, units[k][2]:units[k][2] + 1]))
                if scalar_store and k >= 1:
                    s.wait_ge(p_q2, b_q2 + k)
                    STs[(k - 1) % 2].inc(s.dma_start(out=units[k - 1][1], in_=shape(qb[(k - 1) % 2])))
            if scalar_store:
                s.wait_ge(p_q2, b_q2 + nu)
                STs[(nu - 1) % 2].inc(s.dma_start(out=units[nu - 1][1], in_=shape(qb[(nu - 1) % 2])))

        @block.vector
        def _(v):
            for k in range(nu):
                v.wait_ge(p_q, b_q + k + 1)
                v.wait_ge(sts[k % 2], tgt_st[k - 2] if k >= 2 else b_st[k % 2])
                if k < 2 and scalar_store:
                    v.wait_ge(st4[k % 2], b_st4[k % 2])
                pQ2.inc(v.tensor_scalar(qb[k % 2][:], zb[k % 2][:], -RB, None, AL.add))

        if not scalar_store:
            @block.gpsimd
            def _(g):
                for k in range(nu):
                    g.wait_ge(p_q2, b_q2 + k + 1)
                    STs[k % 2].inc(g.dma_start(out=units[k][1], in_=shape(qb[k % 2])))

    def su_wquant1():
        su_wquant([(unit1(u), unit1(u, True), 0) for u in range(8)],
                  WB4, LD4, ld4s, wzb, wqb, 4)

    def close_es0b():
        es0b.close()
        es0a.close()

    def close_es0b2():
        es0c.close()
        close_es0b()

    es0c = ExitStack()

    def qtr2(u, qq, q=False):
        w2 = s_w2q if q else d_w2
        j = u - 8
        rr = (j % 2) * 512 + qq * 128
        cb = (j // 2) * 1024
        return w2[rr:rr + 128, cb:cb + 1024]

    def half2(u, h, q=False):
        w2 = s_w2q if q else d_w2
        j = u - 8
        rr = (j % 2) * 512 + h * 256
        cb = (j // 2) * 1024
        return w2[rr:rr + 256, cb:cb + 1024].rearrange("(r p) d -> p r d", p=128)

    def su_wquant2():
        wbuf2 = [es0c.enter_context(nc.sbuf_tensor(f"wbc{i}", [128, 2048], F32)) for i in range(2)]
        wzb2 = [es0c.enter_context(nc.sbuf_tensor(f"wzc{i}", [128, 2048], F16)) for i in range(2)]
        wqb2 = [es0c.enter_context(nc.sbuf_tensor(f"wqc{i}", [128, 2048], BF16)) for i in range(2)]
        units = [(half2(u, h), half2(u, h, True), 1) for u in range(8, 16) for h in range(2)]
        su_wquant(units, wbuf2, LDT2[0:2], [ldt2[0], ldt2[1]], wzb2, wqb2, 2)

    def close_es0():
        es0c.close()

    def mk_trans():
        es_tr = ExitStack()
        wrow = [es_tr.enter_context(nc.sbuf_tensor(f"wrow{i}", [128, 1024], F32)) for i in range(2)]
        tev = [es_tr.enter_context(nc.sbuf_tensor(f"tev{i}", [128, 1024], F32)) for i in range(2)]
        pt2 = [es_tr.enter_context(nc.psum_tensor(f"pt2{i}", [128, 128], F32)) for i in range(2)]
        pU = es_tr.enter_context(nc.psum_tensor("pUq", [128, 4], F32))
        jobs = [(w, ec) for w in (2, 3) for ec in range(8)]
        wsrc = {0: d_wq, 1: d_wo, 2: d_wk, 3: d_wv}
        EV0, RD0, TD0 = pEV.n, pRD.n, tD.n
        tgt_ld = [LDT2[j % 2].n + 16 * (j // 2 + 1) for j in range(16)]
        acc = [ST4[2].n, ST4[3].n]
        st_after = {}
        for j, (w, ec) in enumerate(jobs):
            acc[j % 2] += 16 * (1 if w < 2 else 2)
            st_after[j] = acc[j % 2]
        td_tp, td_umm = {}, {}
        cnt = TD0
        for j, (w, ec) in enumerate(jobs):
            for dc in range(8):
                cnt += 1
                td_tp[(j, dc)] = cnt
            if w == 2:
                for dc in range(8):
                    cnt += 1
                    td_umm[(j, dc)] = cnt

        def emit_chunk(tt, b_ev2=None):
            js = list(range(tt * 4, tt * 4 + 4))

            @block.sync
            def _(sy):
                for j in js:
                    if j >= 2:
                        sy.wait_ge(p_ev, EV0 + 8 * (j - 1))
                        if jobs[j - 2][0] == 2:
                            sy.wait_ge(t_d, td_umm[(j - 2, 7)])
                    LDT2[j % 2].inc(sy.dma_start(
                        out=wrow[j % 2][:],
                        in_=wsrc[jobs[j][0]][jobs[j][1] * 128:(jobs[j][1] + 1) * 128, :]))

            @block.tensor
            def _(t):
                for j in js:
                    w, ec = jobs[j]
                    t.wait_ge(ldt2[j % 2], tgt_ld[j])
                    for dc in range(8):
                        gi = j * 8 + dc
                        if gi >= 2:
                            t.wait_ge(p_ev, EV0 + gi - 1)
                        tD.inc(t.transpose(pt2[gi % 2][:, 0:128],
                                           wrow[j % 2][:, dc * 128:(dc + 1) * 128], ident[:]))
                    if w == 2:
                        for dc in range(8):
                            ku = j * 8 + dc
                            if ku >= 2:
                                t.wait_ge(p_rd, RD0 + ku - 1)
                            tD.inc(t.matmul(pU[:, (ku % 2) * 2:(ku % 2) * 2 + 2],
                                            wrow[j % 2][:, dc * 128:(dc + 1) * 128],
                                            wsb[:], start=True, stop=True))

            @block.scalar
            def _(s):
                for j in js:
                    if j >= 2:
                        s.wait_ge(st4[2 + (j % 2)], st_after[j - 2])
                    for dc in range(8):
                        gi = j * 8 + dc
                        s.wait_ge(t_d, td_tp[(j, dc)])
                        pEV.inc(s.copy(tev[j % 2][:, dc * 128:(dc + 1) * 128], pt2[gi % 2][:, 0:128]))

            @block.vector
            def _(v):
                for j in js:
                    w, ec = jobs[j]
                    if w != 2:
                        continue
                    for dc in range(8):
                        ku = j * 8 + dc
                        v.wait_ge(t_d, td_umm[(j, dc)])
                        pRD.inc(v.tensor_copy(Ut[:, dc * 16 + 2 * ec: dc * 16 + 2 * ec + 2],
                                              pU[:, (ku % 2) * 2:(ku % 2) * 2 + 2]))

            @block.gpsimd
            def _(g):
                for j in js:
                    w, ec = jobs[j]
                    g.wait_ge(p_ev, EV0 + 8 * (j + 1))
                    src3 = tev[j % 2].rearrange("p (dc c) -> p dc c", dc=8)
                    SS = ST4[2 + (j % 2)]
                    if w == 0:
                        SS.inc(g.dma_start(
                            out=s_wqT[:, ec * 128:(ec + 1) * 128].rearrange("(dc p) c -> p dc c", p=128),
                            in_=src3))
                    elif w == 1:
                        SS.inc(g.dma_start(
                            out=s_woT[:, ec * 128:(ec + 1) * 128].rearrange("(dc p) c -> p dc c", p=128),
                            in_=src3))
                    else:
                        off = 0 if w == 2 else 64
                        srcA = tev[j % 2].rearrange("p (dc h c) -> p dc h c", dc=8, h=2)
                        SS.inc(g.dma_start(
                            out=s_wkvT[:, (2 * ec) * 128 + off:(2 * ec) * 128 + off + 64]
                                .rearrange("(dc p) c -> p dc c", p=128),
                            in_=srcA[:, :, 0, :]))
                        SS.inc(g.dma_start(
                            out=s_wkvT[:, (2 * ec + 1) * 128 + off:(2 * ec + 1) * 128 + off + 64]
                                .rearrange("(dc p) c -> p dc c", p=128),
                            in_=srcA[:, :, 1, :]))

        return emit_chunk, es_tr.close

    def su_b2split():

        @block.vector
        def _(v):
            vD.inc(v.tensor_copy(b2s[0:1, :], b2row[:]))
            v.wait_ge(v_d, vD.n)
            vD.inc(v.tensor_copy(b2hi32[:], b2s[0:1, :]))
            v.wait_ge(v_d, vD.n)
            vD.inc(v.tensor_tensor(b2lob[:], b2row[:], b2hi32[:], AL.subtract))

        @block.gpsimd
        def _(g):
            g.wait_ge(v_d, vD.n)
            qGP.inc(g.dma_start(out=b2s[1:2, :], in_=b2lob[:]))

    def ln_stats(xbuf, with_amax, co_stats=(), co_math=(), skip_stats=False):
        if not skip_stats:
            @block.scalar
            def _(s):
                for tc in range(NTC):
                    xs = xbuf[:, tc * D:(tc + 1) * D]
                    s.wait_ge(p_q, pQ.n)
                    pQ.inc(s.activation(junk[:], xs, AF.Identity, bias=zcol[:, 0:1], scale=1.0,
                                        accum_out=stat[:, 0 + tc:0 + tc + 1]))
                    s.wait_ge(p_q, pQ.n)
                    pQ.inc(s.activation(junk[:], xs, AF.Square, bias=zcol[:, 0:1], scale=1.0,
                                        accum_out=stat[:, 16 + tc:16 + tc + 1]))

        if with_amax and not skip_stats:
            @block.vector
            def _(v):
                for tc in range(NTC):
                    xs = xbuf[:, tc * D:(tc + 1) * D]
                    vD.inc(v.reduce_max(stat[:, 96 + tc:96 + tc + 1], xs, AX.X))
                    vD.inc(v.tensor_reduce(stat[:, 112 + tc:112 + tc + 1], xs, AX.X, AL.min))

        for f in co_stats:
            f()
        if (not skip_stats) or co_stats:
            barrier()

        @block.vector
        def _(v):
            MN, VAR, T4 = stat[:, 32:48], stat[:, 16:32], stat[:, 64:80]
            vD.inc(v.tensor_scalar(MN, stat[:, 0:16], 1.0 / D, None, AL.mult))
            vD.inc(v.tensor_scalar(VAR, VAR, 1.0 / D, None, AL.mult))
            v.wait_ge(v_d, vD.n)
            vD.inc(v.tensor_tensor(T4, MN, MN, AL.mult))
            v.wait_ge(v_d, vD.n)
            vD.inc(v.tensor_tensor(VAR, VAR, T4, AL.subtract))

        @block.scalar
        def _(s):
            s.wait_ge(v_d, vD.n)
            pQ.inc(s.activation(stat[:, 64:80], stat[:, 16:32], AF.Sqrt,
                                bias=epst[:, 0:1], scale=1.0))

        @block.vector
        def _(v):
            v.wait_ge(p_q, pQ.n)
            vD.inc(v.reciprocal(stat[:, 48:64], stat[:, 64:80]))

        if with_amax:
            @block.vector
            def _(v):
                MN, RS, T6, T7 = stat[:, 32:48], stat[:, 48:64], stat[:, 96:112], stat[:, 112:128]
                v.wait_ge(v_d, vD.n)
                vD.inc(v.tensor_tensor(T6, T6, MN, AL.subtract))
                vD.inc(v.tensor_tensor(T7, MN, T7, AL.subtract))
                v.wait_ge(v_d, vD.n)
                vD.inc(v.tensor_tensor(T6, T6, T7, AL.max))
                v.wait_ge(v_d, vD.n)
                vD.inc(v.tensor_tensor(T6, T6, RS, AL.mult))
                v.wait_ge(v_d, vD.n)
                vD.inc(v.reduce_max(gmax[:], T6, AX.X))

        for f in co_math:
            f()
        barrier()

    def ffn(tag, xbuf, co_stats=(), co_math=(), co_ar=(), post_ar=None,
            co_qx=(), post_qx=None, co_sg=(), post_sg=None, mk_tt=None,
            skip_stats=False, early_w1=False):
        ln_stats(xbuf, with_amax=True, co_stats=co_stats, co_math=co_math,
                 skip_stats=skip_stats)
        esA = ExitStack()
        if early_w1:
            xqT = esA.enter_context(nc.sbuf_tensor(f"{tag}xqT", [128, NDC * T], BF16))
            w1qT = esA.enter_context(nc.sbuf_tensor(f"{tag}w1qT", [128, NDC * HID], BF16))
            gb = [esA.enter_context(nc.sbuf_tensor(f"{tag}gb{i}", [128, TTN], F32)) for i in range(4)]
            pA = [esA.enter_context(nc.psum_tensor(f"{tag}pA{i}", [128, TTN], F32)) for i in range(2)]

            @block.sync
            def _(sy):
                for dc in range(NDC):
                    qSY.inc(sy.dma_start(out=w1qT[:, dc * HID:(dc + 1) * HID],
                                         in_=s_w1q[:, dc * 128:(dc + 1) * 128], transpose=True))
        cross_part_max(8)
        allreduce_max(8, 9)
        for f in co_ar:
            f()

        @block.vector
        def _(v):
            vD.inc(v.tensor_scalar(one1[0:1, 12:13], one1[0:1, 9:10], 1e-6, 1.0 / 127.0,
                                   AL.max, AL.mult))
            v.wait_ge(v_d, vD.n)
            vD.inc(v.reciprocal(one1[0:1, 13:14], one1[0:1, 12:13]))
            vD.inc(v.tensor_tensor(one1[0:1, 14:15], one1[0:1, 12:13], one1[0:1, 4:5], AL.mult))

        bcast_cols(13, 2, 2)

        @block.vector
        def _(v):
            A, B = stat[:, 64:80], stat[:, 80:96]
            vD.inc(v.tensor_scalar(A, stat[:, 48:64], scal[:, 2:3], None, AL.mult))
            v.wait_ge(v_d, vD.n)
            vD.inc(v.tensor_tensor(B, stat[:, 32:48], A, AL.mult))
            v.wait_ge(v_d, vD.n)
            vD.inc(v.tensor_scalar(B, B, -1.0, RB, AL.mult, AL.add))
            vD.inc(v.tensor_scalar(b1sc[:], b1t[:], scal[:, 3:4], None, AL.mult))

        barrier()
        if post_ar:
            post_ar()

        if not early_w1:
            xqT = esA.enter_context(nc.sbuf_tensor(f"{tag}xqT", [128, NDC * T], BF16))
            w1qT = esA.enter_context(nc.sbuf_tensor(f"{tag}w1qT", [128, NDC * HID], BF16))
            gb = [esA.enter_context(nc.sbuf_tensor(f"{tag}gb{i}", [128, TTN], F32)) for i in range(4)]
            pA = [esA.enter_context(nc.psum_tensor(f"{tag}pA{i}", [128, TTN], F32)) for i in range(2)]
        W1T = (ST4[0].n, ST4[1].n)
        esq = ExitStack()
        zq = [esq.enter_context(nc.sbuf_tensor(f"{tag}zq{i}", [128, 1024], F16, side="right")) for i in range(2)]
        xqb = [esq.enter_context(nc.sbuf_tensor(f"{tag}xqb{i}", [128, 1024], BF16, side="right")) for i in range(2)]
        b_q, b_q2 = pQ.n, pQ2.n
        tgt_st = [ST4[tc % 2].n + 16 * (tc // 2 + 1) for tc in range(NTC)]

        @block.scalar
        def _(s):
            for tc in range(NTC):
                if tc >= 2:
                    s.wait_ge(p_q2, b_q2 + tc - 1)
                pQ.inc(s.activation(zq[tc % 2][:], xbuf[:, tc * D:(tc + 1) * D], AF.Identity,
                                    bias=stat[:, 80 + tc:80 + tc + 1],
                                    scale=stat[:, 64 + tc:64 + tc + 1]))

        @block.vector
        def _(v):
            for tc in range(NTC):
                v.wait_ge(p_q, b_q + tc + 1)
                if tc >= 2:
                    v.wait_ge(st4[tc % 2], tgt_st[tc - 2])
                pQ2.inc(v.tensor_scalar(xqb[tc % 2][:], zq[tc % 2][:], -RB, None, AL.add))

        @block.gpsimd
        def _(g):
            for tc in range(NTC):
                g.wait_ge(p_q2, b_q2 + tc + 1)
                ST4[tc % 2].inc(g.dma_start(out=s_xq[tc * 128:(tc + 1) * 128, :], in_=xqb[tc % 2][:]))

        @block.sync
        def _(sy):
            if not early_w1:
                sy.wait_ge(st4[0], W1T[0])
                sy.wait_ge(st4[1], W1T[1])
                for dc in range(NDC):
                    qSY.inc(sy.dma_start(out=w1qT[:, dc * HID:(dc + 1) * HID],
                                         in_=s_w1q[:, dc * 128:(dc + 1) * 128], transpose=True))
            sy.wait_ge(st4[0], tgt_st[6])
            sy.wait_ge(st4[1], tgt_st[7])
            for dc in range(NDC):
                qSY.inc(sy.dma_start(out=xqT[:, dc * T: dc * T + 1024],
                                     in_=s_xq[0:1024, dc * 128:(dc + 1) * 128], transpose=True))
            sy.wait_ge(st4[0], ST4[0].n)
            sy.wait_ge(st4[1], ST4[1].n)
            for dc in range(NDC):
                qSY.inc(sy.dma_start(out=xqT[:, dc * T + 1024: dc * T + T],
                                     in_=s_xq[1024:2048, dc * 128:(dc + 1) * 128], transpose=True))

        for f in co_qx:
            f(zq, xqb)
        barrier()
        if post_qx:
            post_qx()
        esq.close()


        @block.vector
        def _(v):
            vD.inc(v.memset(gmax[:], 0.0))
        b_vd = vD.n

        b_mm, b_ev, b_rd = pMM.n, pEV.n, pRD.n
        ntile = NTT * NHC
        tgt_st = [ST4[i % 4].n + 16 * (i // 4 + 1) for i in range(ntile)]

        @block.tensor
        def _(t):
            for i in range(ntile):
                tt, fc = divmod(i, NHC)
                if i >= 2:
                    t.wait_ge(p_ev, b_ev + i - 1)
                for dc in range(NDC):
                    mm = t.matmul(pA[i % 2][:],
                                  w1qT[:, dc * HID + fc * 128: dc * HID + (fc + 1) * 128],
                                  xqT[:, dc * T + tt * TTN: dc * T + (tt + 1) * TTN],
                                  start=(dc == 0), stop=(dc == NDC - 1))
                pMM.inc(mm)

        @block.scalar
        def _(s):
            for i in range(ntile):
                tt, fc = divmod(i, NHC)
                s.wait_ge(p_mm, b_mm + i + 1)
                if i >= 4:
                    s.wait_ge(p_rd, b_rd + (i - 3))
                    s.wait_ge(st4[i % 4], tgt_st[i - 4])
                pEV.inc(s.activation(gb[i % 4][:], pA[i % 2][:], AF.Gelu,
                                     bias=b1sc[:, fc:fc + 1], scale=scal[:, 3:4]))

        @block.vector
        def _(v):
            for i in range(ntile):
                v.wait_ge(p_ev, b_ev + i + 1)
                if i == 0:
                    v.wait_ge(v_d, b_vd)
                else:
                    v.wait_ge(p_rd, b_rd + i)
                vD.inc(v.reduce_max(gtmp[:], gb[i % 4][:], AX.X, apply_absolute_value=True))
                v.wait_ge(v_d, vD.n)
                pRD.inc(v.tensor_tensor(gmax[:], gmax[:], gtmp[:], AL.max))

        @block.gpsimd
        def _(g):
            for i in range(ntile):
                tt, fc = divmod(i, NHC)
                g.wait_ge(p_ev, b_ev + i + 1)
                ST4[i % 4].inc(g.dma_start(out=s_g[fc * 128:(fc + 1) * 128, tt * TTN:(tt + 1) * TTN],
                                           in_=gb[i % 4][:]))

        barrier()
        esA.close()

        es_w2 = ExitStack()
        w2qT = es_w2.enter_context(nc.sbuf_tensor(f"{tag}w2qT", [128, NHC * D], BF16))
        cross_part_max(10)
        allreduce_max(10, 11)
        b2st = (ST4[0].n, ST4[1].n)
        for f in co_sg:
            f()
        w2busy = ST4[0].n != b2st[0]

        @block.sync
        def _(sy):
            for hc in range(NHC):
                if w2busy and hc % 8 == 0:
                    cb = hc // 8
                    sy.wait_ge(st4[0], b2st[0] + 16 * (cb + 1))
                    sy.wait_ge(st4[1], b2st[1] + 16 * (cb + 1))
                qSY.inc(sy.dma_start(out=w2qT[:, hc * D:(hc + 1) * D],
                                     in_=s_w2q[:, hc * 128:(hc + 1) * 128], transpose=True))

        @block.vector
        def _(v):
            vD.inc(v.tensor_scalar(one1[0:1, 12:13], one1[0:1, 11:12], 1e-6, 1.0 / 127.0,
                                   AL.max, AL.mult))
            v.wait_ge(v_d, vD.n)
            vD.inc(v.reciprocal(one1[0:1, 13:14], one1[0:1, 12:13]))
            vD.inc(v.tensor_tensor(one1[0:1, 14:15], one1[0:1, 12:13], one1[0:1, 5:6], AL.mult))
            v.wait_ge(v_d, vD.n)
            vD.inc(v.tensor_scalar(one1[0:1, 14:15], one1[0:1, 14:15], 0.5, None, AL.mult))

        bcast_cols(13, 2, 4)
        barrier()
        if post_sg:
            post_sg()

        esB = ExitStack()
        gq = esB.enter_context(nc.sbuf_tensor(f"{tag}gq", [128, NHC * TTN], BF16))
        gl = [esB.enter_context(nc.sbuf_tensor(f"{tag}gl{i}", [128, TTN], F32)) for i in range(2)]
        zg = [esB.enter_context(nc.sbuf_tensor(f"{tag}zg{i}", [128, TTN], F16)) for i in range(2)]
        pB = [esB.enter_context(nc.psum_tensor(f"{tag}pB{i}", [128, TTN], F32)) for i in range(4)]

        tt_emit, tt_close = mk_tt() if mk_tt else (None, None)
        ev_hist = {}
        for tt in range(NTT):
            b_q, b_q2, b_mm, b_ev = pQ.n, pQ2.n, pMM.n, pEV2.n
            tgt_ld = [LD2[hc % 2].n + 16 * (hc // 2 + 1) for hc in range(NHC)]

            @block.sync
            def _(sy):
                for hc in range(NHC):
                    sy.wait_ge(p_q, max(0, b_q + hc - 1))
                    LD2[hc % 2].inc(sy.dma_start(out=gl[hc % 2][:],
                                                 in_=s_g[hc * 128:(hc + 1) * 128, tt * TTN:(tt + 1) * TTN]))

            @block.scalar
            def _(s):
                for hc in range(NHC):
                    s.wait_ge(ld2[hc % 2], tgt_ld[hc])
                    s.wait_ge(p_q2, max(0, b_q2 + hc - 1))
                    pQ.inc(s.activation(zg[hc % 2][:], gl[hc % 2][:], AF.Identity,
                                        bias=rbt[:, 0:1], scale=scal[:, 4:5]))

            @block.vector
            def _(v):
                for hc in range(NHC):
                    v.wait_ge(p_q, b_q + hc + 1)
                    v.wait_ge(p_mm, max(0, b_mm - (NHC + 4) + hc + 1))
                    pQ2.inc(v.tensor_scalar(gq[:, hc * TTN:(hc + 1) * TTN], zg[hc % 2][:],
                                            -RB, None, AL.add))

            @block.tensor
            def _(t):
                for ft in range(2):
                    for hc in range(NHC):
                        t.wait_ge(p_q2, b_q2 + hc + 1)
                        if hc == 0:
                            t.wait_ge(p_ev2, max(0, b_ev + 4 * ft))
                        for tc4 in range(4):
                            mm = t.matmul(pB[tc4][:],
                                          gq[:, hc * TTN + tc4 * 128: hc * TTN + (tc4 + 1) * 128],
                                          w2qT[:, hc * D + ft * TTN: hc * D + (ft + 1) * TTN],
                                          start=(hc == 0), stop=False)
                        pMM.inc(mm)
                    for tc4 in range(4):
                        pMM.inc(t.matmul(pB[tc4][:], ones2b[:],
                                         b2s[0:2, ft * TTN:(ft + 1) * TTN], start=False, stop=True))

            @block.vector
            def _(v):
                for ft in range(2):
                    for tc4 in range(4):
                        v.wait_ge(p_mm, b_mm + (NHC + 4) * (ft + 1))
                        tcg = tt * 4 + tc4
                        xs = xbuf[:, tcg * D + ft * TTN: tcg * D + (ft + 1) * TTN]
                        pEV2.inc(v.scalar_tensor_tensor(xs, pB[tc4][:], scal[:, 5:6],
                                                        xs, AL.mult, AL.add))

            if tt_emit and tt >= 1:
                tt_emit(tt - 1, ev_hist[tt - 1])
            if tt_emit:
                ev_hist[tt] = b_ev

        if tt_emit:
            tt_emit(NTT - 1, ev_hist[NTT - 1])
        barrier()
        if tt_close:
            tt_close()
        esB.close()
        es_w2.close()

    def finalize(xbuf):
        @block.gpsimd
        def _(g):
            for tc in range(NTC):
                qGP.inc(g.dma_start(out=d_y[tc * 128:(tc + 1) * 128, :],
                                    in_=xbuf[:, tc * D:(tc + 1) * D]))
        barrier()

    def ln2_stats_tt(tt, b_ev2):
        @block.scalar
        def _(s):
            for tc4 in range(4):
                tcg = tt * 4 + tc4
                xs = sb_x[:, tcg * D:(tcg + 1) * D]
                s.wait_ge(p_ev2, b_ev2 + 4 + tc4 + 1)
                s.wait_ge(p_q, pQ.n)
                pQ.inc(s.activation(junk[:], xs, AF.Identity, bias=zcol[:, 0:1], scale=1.0,
                                    accum_out=stat[:, tcg:tcg + 1]))
                s.wait_ge(p_q, pQ.n)
                pQ.inc(s.activation(junk[:], xs, AF.Square, bias=zcol[:, 0:1], scale=1.0,
                                    accum_out=stat[:, 16 + tcg:16 + tcg + 1]))

    def mk_trans_stats():
        emit_chunk, close = mk_trans()

        def emit(tt, b_ev2):
            emit_chunk(tt, b_ev2)
            ln2_stats_tt(tt, b_ev2)
        return emit, close

    ffn("f1", sb_x, co_stats=[su_absmax], co_math=[su_scales, su_b2split],
        co_ar=[lambda: esP1.close(), su_wquant1, su_wquant2], post_ar=close_es0b2,
        mk_tt=mk_trans_stats)

    if UPTO == "ffn1":
        finalize(sb_x)
        es1.close(); es.close()
        return nc

    ln_stats(sb_x, with_amax=False, skip_stats=True)

    es_xlt = ExitStack()
    xlt = es_xlt.enter_context(nc.sbuf_tensor("xlt", [128, NDC * T], F32, side="right"))
    es2 = ExitStack()
    xl2 = [es2.enter_context(nc.sbuf_tensor(f"xl2_{i}", [128, 1024], F32)) for i in range(2)]
    ptA = es2.enter_context(nc.psum_tensor("ptA", [128, 128], F32))
    ptB = es2.enter_context(nc.psum_tensor("ptB", [128, 128], F32))

    b_q, b_mm, b_ev = pQ.n, pMM.n, pEV.n
    tgt_st = [ST4[tc % 2].n + 16 * (tc // 2 + 1) for tc in range(NTC)]

    @block.vector
    def _(v):
        for tc in range(NTC):
            if tc >= 2:
                v.wait_ge(p_mm, b_mm + 8 * (tc - 1))
                v.wait_ge(st4[tc % 2], tgt_st[tc - 2])
            pQ.inc(v.tensor_scalar(xl2[tc % 2][:], sb_x[:, tc * D:(tc + 1) * D],
                                   stat[:, 32 + tc:32 + tc + 1], stat[:, 48 + tc:48 + tc + 1],
                                   AL.subtract, AL.mult))

    @block.gpsimd
    def _(g):
        for tc in range(NTC):
            g.wait_ge(p_q, b_q + tc + 1)
            ST4[tc % 2].inc(g.dma_start(out=s_xln2[tc * 128:(tc + 1) * 128, :], in_=xl2[tc % 2][:]))

    @block.tensor
    def _(t):
        for tc in range(NTC):
            t.wait_ge(p_q, b_q + tc + 1)
            for dc in range(NDC):
                i = tc * 8 + dc
                if i >= 2:
                    t.wait_ge(p_ev, b_ev + i - 1)
                pp = ptA if i % 2 == 0 else ptB
                pMM.inc(t.transpose(pp[:, 0:128], xl2[tc % 2][:, dc * 128:(dc + 1) * 128], ident[:]))

    @block.scalar
    def _(s):
        for tc in range(NTC):
            for dc in range(NDC):
                i = tc * 8 + dc
                s.wait_ge(p_mm, b_mm + i + 1)
                pp = ptA if i % 2 == 0 else ptB
                pEV.inc(s.copy(xlt[:, dc * T + tc * 128: dc * T + (tc + 1) * 128], pp[:, 0:128]))

    wrow2 = [es2.enter_context(nc.sbuf_tensor(f"wrw2_{i}", [128, 1024], F32)) for i in range(2)]
    tev2 = [es2.enter_context(nc.sbuf_tensor(f"tv2_{i}", [128, 1024], F32)) for i in range(2)]
    pt3 = [es2.enter_context(nc.psum_tensor(f"pt3{i}", [128, 128], F32)) for i in range(2)]
    jobsQ = [(w, ec) for w in (0, 1) for ec in range(8)]
    EV0q = pEV.n
    TD0q = tD.n
    tgt_ldq = [LDT2[j % 2].n + 16 * (j // 2 + 1) for j in range(16)]
    accq = [ST4[2].n, ST4[3].n]
    st_afterq = {}
    for j in range(16):
        accq[j % 2] += 16
        st_afterq[j] = accq[j % 2]

    @block.sync
    def _(sy):
        for j, (w, ec) in enumerate(jobsQ):
            if j >= 2:
                sy.wait_ge(p_ev, EV0q + 8 * (j - 1))
            LDT2[j % 2].inc(sy.dma_start(
                out=wrow2[j % 2][:],
                in_=(d_wq if w == 0 else d_wo)[ec * 128:(ec + 1) * 128, :]))

    @block.tensor
    def _(t):
        for j, (w, ec) in enumerate(jobsQ):
            t.wait_ge(ldt2[j % 2], tgt_ldq[j])
            for dc in range(8):
                gi = j * 8 + dc
                if gi >= 2:
                    t.wait_ge(p_ev, EV0q + gi - 1)
                tD.inc(t.transpose(pt3[gi % 2][:, 0:128],
                                   wrow2[j % 2][:, dc * 128:(dc + 1) * 128], ident[:]))

    @block.scalar
    def _(s):
        for j, (w, ec) in enumerate(jobsQ):
            if j >= 2:
                s.wait_ge(st4[2 + (j % 2)], st_afterq[j - 2])
            for dc in range(8):
                gi = j * 8 + dc
                s.wait_ge(t_d, TD0q + gi + 1)
                pEV.inc(s.copy(tev2[j % 2][:, dc * 128:(dc + 1) * 128], pt3[gi % 2][:, 0:128]))

    @block.gpsimd
    def _(g):
        for j, (w, ec) in enumerate(jobsQ):
            g.wait_ge(p_ev, EV0q + 8 * (j + 1))
            src3 = tev2[j % 2].rearrange("p (dc c) -> p dc c", dc=8)
            dst = (s_wqT if w == 0 else s_woT)[:, ec * 128:(ec + 1) * 128]
            ST4[2 + (j % 2)].inc(g.dma_start(
                out=dst.rearrange("(dc p) c -> p dc c", p=128), in_=src3))

    barrier()
    es2.close()

    @block.gpsimd
    def _(g):
        for tc in range(NTC):
            qGP.inc(g.dma_start(out=s_xpark[tc * 128:(tc + 1) * 128, :],
                                in_=sb_x[:, tc * D:(tc + 1) * D]))

    barrier()
    es1.close()

    es_q = ExitStack()
    qT = es_q.enter_context(nc.sbuf_tensor("qTt", [128, NDC * T], F32))
    es6 = ExitStack()
    wqe = [es6.enter_context(nc.sbuf_tensor(f"wqe{i}", [128, 1024], F32)) for i in range(2)]
    pQP = [es6.enter_context(nc.psum_tensor(f"pQP{i}", [128, TTN], F32)) for i in range(2)]
    qk = [0]
    qtd = {}
    q_ld = {}
    BQ = pQ.n

    def qproj_emit(n):
        ks = list(range(qk[0], min(8 * NTT, qk[0] + n)))
        qk[0] += len(ks)
        if not ks:
            return
        ecs = [ec for ec in sorted({k // NTT for k in ks}) if ec not in q_ld]

        @block.sync
        def _(sy):
            for ec in ecs:
                if ec >= 2:
                    sy.wait_ge(p_q, BQ + 4 * (ec - 1))
                LDT2[ec % 2].inc(sy.dma_start(
                    out=wqe[ec % 2].rearrange("p (dc c) -> p dc c", dc=8),
                    in_=s_wqT[:, ec * 128:(ec + 1) * 128].rearrange("(dc p) c -> p dc c", p=128)))
                q_ld[ec] = LDT2[ec % 2].n

        @block.tensor
        def _(t):
            for k in ks:
                ec, tt = divmod(k, NTT)
                if tt == 0 or k == ks[0]:
                    t.wait_ge(ldt2[ec % 2], q_ld[ec])
                if k >= 2:
                    t.wait_ge(p_q, BQ + k - 1)
                for dc in range(NDC):
                    mm = t.matmul(pQP[k % 2][:],
                                  wqe[ec % 2][:, dc * 128:(dc + 1) * 128],
                                  xlt[:, dc * T + tt * TTN: dc * T + (tt + 1) * TTN],
                                  start=(dc == 0), stop=(dc == NDC - 1))
                tD.inc(mm)
                qtd[k] = tD.n

        @block.scalar
        def _(s):
            for k in ks:
                ec, tt = divmod(k, NTT)
                s.wait_ge(t_d, qtd[k])
                pQ.inc(s.copy(qT[:, ec * T + tt * TTN: ec * T + (tt + 1) * TTN], pQP[k % 2][:]))

    es3 = ExitStack()
    mval = es3.enter_context(nc.sbuf_tensor("mval", [16, 64], F32, side="right"))
    midx = es3.enter_context(nc.sbuf_tensor("midx", [16, 64], U32, side="right"))
    midf = es3.enter_context(nc.sbuf_tensor("midf", [16, 64], F32, side="right"))
    idxT = es3.enter_context(nc.sbuf_tensor("idxT", [64, 16], F32, side="right"))
    idxTi = es3.enter_context(nc.sbuf_tensor("idxTi", [64, 16], I32, side="right"))
    psc = es3.enter_context(nc.psum_tensor("psc", [16, TTN], F32))
    pstx = es3.enter_context(nc.psum_tensor("pstx", [64, 16], F32))
    es3a = ExitStack()
    sc0 = es3a.enter_context(nc.sbuf_tensor("sc0", [16, T], F32, side="right"))
    sc1 = es3a.enter_context(nc.sbuf_tensor("sc1", [16, T], F32, side="right"))

    b_mm, b_ev = pMM.n, pEV2.n

    @block.tensor
    def _(t):
        for tt in range(NTT):
            if tt >= 1:
                t.wait_ge(p_ev2, b_ev + tt)
            for dc in range(NDC):
                mm = t.matmul(psc[:], Ut[:, dc * 16:(dc + 1) * 16],
                              xlt[:, dc * T + tt * TTN: dc * T + (tt + 1) * TTN],
                              start=(dc == 0), stop=(dc == NDC - 1))
            pMM.inc(mm)

    @block.vector
    def _(v):
        for tt in range(NTT):
            v.wait_ge(p_mm, b_mm + tt + 1)
            pEV2.inc(v.tensor_copy(sc0[:, tt * TTN:(tt + 1) * TTN], psc[:]))

    barrier()

    @block.vector
    def _(v):
        cur, nxt = sc0, sc1
        for r in range(8):
            v.wait_ge(v_d, vD.n)
            vD.inc(v.max(out=mval[:, r * 8:(r + 1) * 8], in_=cur[:]))
            v.wait_ge(v_d, vD.n)
            vD.inc(v.max_index(midx[:, r * 8:(r + 1) * 8], mval[:, r * 8:(r + 1) * 8], cur[:]))
            v.wait_ge(v_d, vD.n)
            vD.inc(v.match_replace(nxt[:], in_to_replace=mval[:, r * 8:(r + 1) * 8],
                                   in_values=cur[:], imm_value=NEG))
            cur, nxt = nxt, cur
        v.wait_ge(v_d, vD.n)
        vD.inc(v.tensor_copy(midf[:], midx[:]))

    qproj_emit(13)
    barrier()
    es3a.close()

    @block.tensor
    def _(t):
        tD.inc(t.transpose(pstx[:], midf[:], ident[0:16, 0:16]))

    @block.vector
    def _(v):
        v.wait_ge(t_d, tD.n)
        vD.inc(v.tensor_copy(idxT[:], pstx[:]))
        v.wait_ge(v_d, vD.n)
        vD.inc(v.tensor_copy(idxTi[:], idxT[:]))

    barrier()

    es4 = ExitStack()
    xcd = [es4.enter_context(nc.sbuf_tensor(f"xcd{i}", [64, 1024], F32, side="right")) for i in range(2)]
    xct = es4.enter_context(nc.sbuf_tensor("xct", [128, NDC * 1024], F32, side="right"))
    es4p = ExitStack()
    ptA2 = es4p.enter_context(nc.psum_tensor("ptA2", [128, 128], F32))
    ptB2 = es4p.enter_context(nc.psum_tensor("ptB2", [128, 128], F32))

    qproj_emit(6)
    b_mm, b_ev = pMM.n, pEV.n
    tgt_ld = [IG2[h % 2].n + 16 * (h // 2 + 1) for h in range(NH)]

    @block.gpsimd
    def _(g):
        for h in range(NH):
            if h >= 2:
                g.wait_ge(p_mm, b_mm + 8 * (h - 1))
            IG2[h % 2].inc(g.indirect_dma_start(
                out=xcd[h % 2][:], out_offset=None,
                in_=s_xln2[:, :],
                in_offset=IndirectOffsetOnAxis(idxTi[0:64, h:h + 1], axis=0)))

    @block.tensor
    def _(t):
        for h in range(NH):
            t.wait_ge(ig2[h % 2], tgt_ld[h])
            for dc in range(NDC):
                i = h * 8 + dc
                if i >= 2:
                    t.wait_ge(p_ev, b_ev + i - 1)
                pp = ptA2 if i % 2 == 0 else ptB2
                pMM.inc(t.transpose(pp[:, 0:64], xcd[h % 2][:, dc * 128:(dc + 1) * 128],
                                    ident[0:64, 0:64]))

    @block.scalar
    def _(s):
        for h in range(NH):
            for dc in range(NDC):
                i = h * 8 + dc
                s.wait_ge(p_mm, b_mm + i + 1)
                pp = ptA2 if i % 2 == 0 else ptB2
                pEV.inc(s.copy(xct[:, dc * 1024 + h * 64: dc * 1024 + (h + 1) * 64], pp[:, 0:64]))

    barrier()
    es4p.close()

    es5 = ExitStack()
    wkve = [es5.enter_context(nc.sbuf_tensor(f"wkve{i}", [128, 1024], F32, side="right")) for i in range(2)]
    pay = es5.enter_context(nc.sbuf_tensor("pays", [64, 128], F32, side="right"))
    pKV = [es5.enter_context(nc.psum_tensor(f"pKV{i}", [64, 128], F32)) for i in range(2)]

    b_mm, b_ev = pMM.n, pEV.n
    b_st0 = ST4[0].n
    tgt_kv = [LDU2[h % 2].n + 16 * (h // 2 + 1) for h in range(NH)]

    @block.sync
    def _(sy):
        for h in range(NH):
            if h >= 2:
                sy.wait_ge(p_mm, b_mm + h - 1)
            LDU2[h % 2].inc(sy.dma_start(
                out=wkve[h % 2].rearrange("p (dc c) -> p dc c", dc=8),
                in_=s_wkvT[:, h * 128:(h + 1) * 128].rearrange("(dc p) c -> p dc c", p=128)))

    @block.tensor
    def _(t):
        for h in range(NH):
            t.wait_ge(ldu2[h % 2], tgt_kv[h])
            if h >= 2:
                t.wait_ge(p_ev, b_ev + h - 1)
            for dc in range(NDC):
                mm = t.matmul(pKV[h % 2][:],
                              xct[:, dc * 1024 + h * 64: dc * 1024 + (h + 1) * 64],
                              wkve[h % 2][:, dc * 128:(dc + 1) * 128],
                              start=(dc == 0), stop=(dc == NDC - 1))
            pMM.inc(mm)

    @block.scalar
    def _(s):
        for h in range(NH):
            s.wait_ge(p_mm, b_mm + h + 1)
            if h >= 1:
                s.wait_ge(st4[0], b_st0 + 16 * h)
            pEV.inc(s.copy(pay[:], pKV[h % 2][:]))

    qproj_emit(6)

    @block.gpsimd
    def _(g):
        for h in range(NH):
            g.wait_ge(p_ev, b_ev + h + 1)
            ST4[0].inc(g.dma_start(out=ag_in[h * 64:(h + 1) * 64, :], in_=pay[:]))
        g.wait_ge(v_d, vD.n)
        qGP.inc(g.dma_start(out=ag_in[1024:1032, :].rearrange("a (b c) -> (a b) c", b=2),
                            in_=mval[:]))

    barrier()
    es5.close()
    es4.close()
    es3.close()

    es7 = ExitStack()
    hsel = es7.enter_context(nc.sbuf_tensor("hsel", [16, 2048], F32, side="right"))
    va0 = es7.enter_context(nc.sbuf_tensor("va0", [16, 512], F32, side="right"))
    va1 = es7.enter_context(nc.sbuf_tensor("va1", [16, 512], F32, side="right"))
    gval = es7.enter_context(nc.sbuf_tensor("gval", [16, 64], F32, side="right"))
    gposu = es7.enter_context(nc.sbuf_tensor("gposu", [16, 64], U32, side="right"))
    gposf = es7.enter_context(nc.sbuf_tensor("gposf", [16, 64], F32, side="right"))

    @block.gpsimd
    def _(g):
        gD.inc(g.collective_compute("AllGather", AL.bypass, replica_groups=RG,
                                    ins=[ag_in.opt()], outs=[ag_out.opt()]))

    qproj_emit(32)

    @block.sync
    def _(sy):
        sy.wait_ge(g_d, gD.n)
        qSY.inc(sy.dma_start(out=hsel[:], in_=c_hsel))
        for c in range(8):
            qSY.inc(sy.dma_start(
                out=va0[:, c * 64:(c + 1) * 64],
                in_=ag_out[c * 1032 + 1024: c * 1032 + 1032, :].rearrange("a (b c) -> (a b) c", b=2)))

    @block.vector
    def _(v):
        v.wait_ge(q_sy, qSY.n)
        cur, nxt = va0, va1
        for r in range(8):
            v.wait_ge(v_d, vD.n)
            vD.inc(v.max(out=gval[:, r * 8:(r + 1) * 8], in_=cur[:]))
            v.wait_ge(v_d, vD.n)
            vD.inc(v.max_index(gposu[:, r * 8:(r + 1) * 8], gval[:, r * 8:(r + 1) * 8], cur[:]))
            v.wait_ge(v_d, vD.n)
            vD.inc(v.match_replace(nxt[:], in_to_replace=gval[:, r * 8:(r + 1) * 8],
                                   in_values=cur[:], imm_value=NEG))
            cur, nxt = nxt, cur
        v.wait_ge(v_d, vD.n)
        vD.inc(v.tensor_copy(gposf[:], gposu[:]))

    barrier()
    es6.close()

    es7p = ExitStack()
    kva = xlt[:, 0:8192]
    e2 = xlt[:, 8192:12288]
    ppos = es7p.enter_context(nc.psum_tensor("ppos", [128, 1024], F32))
    pSELb = es7p.enter_context(nc.psum_tensor("pSELb", [128, 128], F32))
    pKTb = es7p.enter_context(nc.psum_tensor("pKTb", [128, 128], F32))
    pSEL = [pSELb[:, 0:64], pSELb[:, 64:128]]
    pKT = [pKTb[:, 0:64], pKTb[:, 64:128]]

    b_mm, b_e2 = pMM.n, pEV2.n

    @block.tensor
    def _(t):
        t.wait_ge(v_d, vD.n)
        for h in range(NH):
            mm = t.matmul(ppos[:, h * 64:(h + 1) * 64], hsel[:, h * 128:(h + 1) * 128],
                          gposf[:], start=True, stop=True, skip_group_check=True)
        pMM.inc(mm)

    @block.vector
    def _(v):
        v.wait_ge(p_mm, b_mm + 1)
        for h in range(NH):
            for q4 in range(4):
                vD.inc(v.tensor_scalar(e2[:, (h * 4 + q4) * 64:(h * 4 + q4 + 1) * 64],
                                       ppos[:, h * 64:(h + 1) * 64],
                                       iota4[:, q4:q4 + 1], None, AL.is_equal))

    @block.sync
    def _(sy):
        for c in range(8):
            dst = kva[(c % 2) * 64:(c % 2) * 64 + 64, :].rearrange(
                "p (h q k) -> p h q k", h=16, q=4)[:, :, c // 2, :]
            qSY.inc(sy.dma_start(
                out=dst,
                in_=ag_out[c * 1032: c * 1032 + 1024, :].rearrange("(h r) k -> r h k", h=16)))

    b_mm2 = pMM.n

    @block.tensor
    def _(t):
        t.wait_ge(v_d, vD.n)
        t.wait_ge(q_sy, qSY.n)
        for m in range(NH):
            hp, sub = divmod(m, 2)
            h = m
            pr = slice(64 * sub, 64 * sub + 64)
            if m >= 2:
                t.wait_ge(p_ev2, b_e2 + 2 * (m - 1))
            for q4 in range(4):
                t.matmul(pSEL[m % 2][pr, :], e2[:, (h * 4 + q4) * 64:(h * 4 + q4 + 1) * 64],
                         kva[:, (h * 4 + q4) * 128 + 64:(h * 4 + q4) * 128 + 128],
                         start=(q4 == 0), stop=(q4 == 3), skip_group_check=True)
                mm = t.matmul(pKT[m % 2][pr, :], kva[:, (h * 4 + q4) * 128:(h * 4 + q4) * 128 + 64],
                              e2[:, (h * 4 + q4) * 64:(h * 4 + q4 + 1) * 64],
                              start=(q4 == 0), stop=(q4 == 3), skip_group_check=True)
            pMM.inc(mm)

    @block.vector
    def _(v):
        for m in range(NH):
            hp, sub = divmod(m, 2)
            pr = slice(64 * sub, 64 * sub + 64)
            v.wait_ge(p_mm, b_mm2 + m + 1)
            pEV2.inc(v.tensor_copy(vselt[pr, hp * 64:(hp + 1) * 64], pSEL[m % 2][pr, :]))
            pEV2.inc(v.tensor_copy(kselT[pr, hp * 64:(hp + 1) * 64], pKT[m % 2][pr, :]))

    barrier()
    es7p.close()
    es7.close()
    es3.close()
    es_xlt.close()

    es_x2 = ExitStack()
    sb_x2 = es_x2.enter_context(nc.sbuf_tensor("sb_x2", [128, NTC * D], F32, side="right"))
    es8 = ExitStack()
    woTs = es8.enter_context(nc.sbuf_tensor("woTs", [128, NDC * 1024], F32))
    selrz = es8.enter_context(nc.sbuf_tensor("selrz", [16, 1024], F32))
    oT = [es8.enter_context(nc.sbuf_tensor(f"oTt{i}", [128, TTN], F32)) for i in range(8)]
    pb_ = [es8.enter_context(nc.sbuf_tensor(f"pbt{i}", [128, TTN], F32)) for i in range(2)]
    rzt = es8.enter_context(nc.sbuf_tensor("rztt", [16, TTN], F32))
    pS = [es8.enter_context(nc.psum_tensor(f"pSs{i}", [128, TTN], F32)) for i in range(2)]
    pO = [es8.enter_context(nc.psum_tensor(f"pOo{i}", [128, TTN], F32)) for i in range(2)]
    pZ = es8.enter_context(nc.psum_tensor("pZz", [16, TTN], F32))
    pY = [es8.enter_context(nc.psum_tensor(f"pYy{i}", [128, TTN], F32)) for i in range(2)]

    @block.sync
    def _(sy):
        qSY.inc(sy.dma_start(out=selrz[:], in_=c_selrz))
        for dc in range(NDC):
            qSY.inc(sy.dma_start(out=woTs[:, dc * 1024:(dc + 1) * 1024],
                                 in_=s_woT[dc * 128:(dc + 1) * 128, :]))

    @block.gpsimd
    def _(g):
        for tc in range(NTC):
            qGP.inc(g.dma_start(out=sb_x2[:, tc * D:(tc + 1) * D],
                                in_=s_xpark[tc * 128:(tc + 1) * 128, :]))

    barrier()

    for tt in range(NTT):
        b_mm, b_ev, b_q, b_q2, b_rd, b_td = pMM.n, pEV.n, pQ.n, pQ2.n, pRD.n, tD.n

        @block.tensor
        def _(t):
            for hp in range(8):
                t.wait_ge(p_q, max(0, b_q + hp - 1))
                t.matmul(pS[hp % 2][0:64, :], kselT[0:64, hp * 64:(hp + 1) * 64],
                         qT[0:64, hp * T + tt * TTN: hp * T + (tt + 1) * TTN],
                         start=True, stop=True, skip_group_check=True)
                pMM.inc(t.matmul(pS[hp % 2][64:128, :], kselT[64:128, hp * 64:(hp + 1) * 64],
                                 qT[64:128, hp * T + tt * TTN: hp * T + (tt + 1) * TTN],
                                 start=True, stop=True, skip_group_check=True))
                t.wait_ge(p_q, b_q + hp + 1)
                t.wait_ge(p_ev, max(0, b_ev + hp - 1))
                if hp == 0:
                    t.wait_ge(p_q2, b_q2)
                t.matmul(pO[hp % 2][0:64, :], vselt[0:64, hp * 64:(hp + 1) * 64],
                         pb_[hp % 2][0:64, :], start=True, stop=True, skip_group_check=True)
                t.matmul(pO[hp % 2][64:128, :], vselt[64:128, hp * 64:(hp + 1) * 64],
                         pb_[hp % 2][64:128, :], start=True, stop=True, skip_group_check=True)
                pMM.inc(t.matmul(pZ[:], selz[:, hp * 16:(hp + 1) * 16], pb_[hp % 2][:],
                                 start=(hp == 0), stop=(hp == 7), skip_group_check=True))

        @block.scalar
        def _(s):
            for hp in range(8):
                s.wait_ge(p_mm, b_mm + 2 * hp + 1)
                s.wait_ge(p_mm, max(0, b_mm + 2 * hp - 2))
                pQ.inc(s.activation(pb_[hp % 2][:], pS[hp % 2][:], AF.Exp,
                                    bias=zcol[:, 0:1], scale=attsc[:, 0:1]))
                if hp >= 1:
                    s.wait_ge(p_mm, b_mm + 2 * (hp - 1) + 2)
                    pEV.inc(s.copy(oT[hp - 1][:], pO[(hp - 1) % 2][:]))
            s.wait_ge(p_mm, b_mm + 16)
            pEV.inc(s.copy(oT[7][:], pO[7 % 2][:]))

        @block.vector
        def _(v):
            v.wait_ge(p_mm, b_mm + 16)
            pQ2.inc(v.reciprocal(rzt[:], pZ[:]))

        rot = [pS[0], pS[1], pO[0], pO[1]]

        @block.tensor
        def _(t):
            for hp in range(8):
                if hp == 0:
                    t.wait_ge(p_q2, b_q2 + 1)
                elif hp == 2:
                    t.wait_ge(p_ev, b_ev + 7)
                elif hp == 3:
                    t.wait_ge(p_ev, b_ev + 8)
                elif hp >= 4:
                    t.wait_ge(p_rd, b_rd + hp - 3)
                tD.inc(t.matmul(rot[hp % 4][:], selrz[:, hp * 128:(hp + 1) * 128], rzt[:],
                                start=True, stop=True))

        tD_base = tD.n - 8

        @block.vector
        def _(v):
            for hp in range(8):
                v.wait_ge(t_d, tD_base + hp + 1)
                v.wait_ge(p_ev, b_ev + hp + 1)
                pRD.inc(v.tensor_tensor(oT[hp][:], oT[hp][:], rot[hp % 4][:], AL.mult))

        b_mm2, b_evv = pMM.n, pEV2.n

        @block.tensor
        def _(t):
            t.wait_ge(p_rd, b_rd + 8)
            for gidx in range(8):
                tc4, ft = divmod(gidx, 2)
                t.wait_ge(p_ev2, max(0, b_evv + gidx - 1))
                for hp in range(8):
                    mm = t.matmul(pY[gidx % 2][:], oT[hp][:, tc4 * 128:(tc4 + 1) * 128],
                                  woTs[:, hp * 1024 + ft * TTN: hp * 1024 + (ft + 1) * TTN],
                                  start=(hp == 0), stop=(hp == 7))
                pMM.inc(mm)

        @block.vector
        def _(v):
            for gidx in range(8):
                tc4, ft = divmod(gidx, 2)
                v.wait_ge(p_mm, b_mm2 + gidx + 1)
                tcg = tt * 4 + tc4
                xs = sb_x2[:, tcg * D + ft * TTN: tcg * D + (ft + 1) * TTN]
                pEV2.inc(v.scalar_tensor_tensor(xs, pY[gidx % 2][:], 1.0, xs, AL.mult, AL.add))

        @block.scalar
        def _(s):
            for tc4 in range(4):
                tcg = tt * 4 + tc4
                xs2 = sb_x2[:, tcg * D:(tcg + 1) * D]
                s.wait_ge(p_ev2, b_evv + 2 * tc4 + 2)
                s.wait_ge(p_q, pQ.n)
                pQ.inc(s.activation(junk[:], xs2, AF.Identity, bias=zcol[:, 0:1], scale=1.0,
                                    accum_out=stat[:, tcg:tcg + 1]))
                s.wait_ge(p_q, pQ.n)
                pQ.inc(s.activation(junk[:], xs2, AF.Square, bias=zcol[:, 0:1], scale=1.0,
                                    accum_out=stat[:, 16 + tcg:16 + tcg + 1]))

        @block.vector
        def _(v):
            for tc4 in range(4):
                tcg = tt * 4 + tc4
                xs2 = sb_x2[:, tcg * D:(tcg + 1) * D]
                v.wait_ge(p_ev2, b_evv + 2 * tc4 + 2)
                vD.inc(v.reduce_max(stat[:, 96 + tcg:96 + tcg + 1], xs2, AX.X))
                vD.inc(v.tensor_reduce(stat[:, 112 + tcg:112 + tcg + 1], xs2, AX.X, AL.min))

    barrier()
    es8.close()
    es_q.close()

    if UPTO == "attn":
        finalize(sb_x2)
        es_x2.close(); es.close()
        return nc

    ffn("f2", sb_x2, skip_stats=True, early_w1=True)

    finalize(sb_x2)
    es_x2.close()
    es.close()
    return nc


def _consts():
    import ml_dtypes
    ident = np.eye(128, dtype=np.float32)
    ones1 = np.ones((1, 128), np.float32)
    ones2b = np.ones((2, 128), ml_dtypes.bfloat16)
    iota4 = (np.arange(128, dtype=np.float32)[:, None] +
             128.0 * np.arange(4, dtype=np.float32)[None, :])
    selz = np.zeros((128, 128), np.float32)
    for hp in range(8):
        selz[0:64, hp * 16 + 2 * hp] = 1.0
        selz[64:128, hp * 16 + 2 * hp + 1] = 1.0
    selrz = np.zeros((16, 1024), np.float32)
    for hp in range(8):
        for p in range(128):
            selrz[2 * hp + p // 64, hp * 128 + p] = 1.0
    hselc = np.zeros((16, 2048), np.float32)
    for h in range(16):
        hselc[h, h * 128:(h + 1) * 128] = 1.0
    return ident, ones1, ones2b, iota4, selz, selrz, hselc


def kernel(**inputs):
    x = np.asarray(inputs["L0_attn_x"], np.float32).reshape(NCORES * T, D)
    ws = np.asarray(inputs["L0_attn_w_score"], np.float32)
    wsbc = np.zeros((128, 2), np.float32)
    wsbc[0:64, 0] = ws
    wsbc[64:128, 1] = ws
    ident, ones1, ones2b, iota4, selz, selrz, hselc = _consts()

    nc = _build()
    shared = {}
    for k in ("L0_attn_Wq", "L0_attn_Wk", "L0_attn_Wv", "L0_attn_Wo",
              "L0_ffn_w1", "L0_ffn_b1", "L0_ffn_w2", "L0_ffn_b2"):
        shared[k] = np.ascontiguousarray(np.asarray(inputs[k], np.float32))
    shared.update(c_ident=ident, c_ones1=ones1, c_ones2b=ones2b, c_iota4=iota4,
                  c_selz=selz, c_selrz=selrz, c_hsel=hselc, c_wsb=wsbc)
    in_maps = []
    for c in range(NCORES):
        m = dict(shared)
        m["L0_attn_x"] = np.ascontiguousarray(x[c * T:(c + 1) * T, :])
        in_maps.append(m)
    res = run_bass_kernel_spmd(nc, in_maps, core_ids=list(range(NCORES)))
    y = np.concatenate([np.asarray(r["y"], np.float32) for r in res.results], axis=0)
    return y.reshape(1, NCORES * T, D)
